# version 07
# speedup vs baseline: 1.3854x; 1.0007x over previous
import os

import numpy as np
import concourse.bass as bass
import concourse.mybir as mybir
from concourse.bass_utils import run_bass_kernel_spmd

F32 = mybir.dt.float32
BF16 = mybir.dt.float16
B16R = mybir.dt.bfloat16
I8 = mybir.dt.int8
AF = mybir.ActivationFunctionType
ALU = mybir.AluOpType
AX = mybir.AxisListType
MAGIC = 12582912.0

S = 2048
E = 1024
NCH = 8

SBUF_LO = 16512
SBUF_HI = 229344


class Tok:
    __slots__ = ("sem", "val")

    def __init__(self, sem, val):
        self.sem = sem
        self.val = val


class Res:
    __slots__ = ("name", "last_w", "readers")

    def __init__(self, name):
        self.name = name
        self.last_w = None
        self.readers = []


class KB:
    ENGINES = ("t", "v", "s", "g", "q")

    def __init__(self, nc):
        self.nc = nc
        self.ops = {e: [] for e in self.ENGINES}
        self.cnt = {e: 0 for e in self.ENGINES}
        self.sem_names = {e: f"es_{e}" for e in self.ENGINES}
        self.dma_sems = {}
        self.sem_consumers = {}
        self.res = {}
        self.free = [(SBUF_LO, SBUF_HI)]
        self.live = {}
        self.retired = []
        self.uid = 0

    def R(self, name):
        if name not in self.res:
            self.res[name] = Res(name)
        return self.res[name]

    def salloc(self, name, shape, dtype):
        nbytes = int(np.prod(shape[1:])) * mybir.dt.size(dtype)
        nbytes = (nbytes + 31) // 32 * 32
        for i, (a, b) in enumerate(self.free):
            a32 = (a + 31) // 32 * 32
            if b - a32 >= nbytes:
                self.free[i] = (a32 + nbytes, b)
                if self.free[i][0] >= self.free[i][1]:
                    self.free.pop(i)
                lo, hi = a32, a32 + nbytes
                self.live[name] = (lo, hi)
                res = self.R(name)
                keep = []
                for (ra, rb, toks) in self.retired:
                    if ra < hi and rb > lo:
                        res.readers = res.readers + toks
                    if not (lo <= ra and rb <= hi):
                        keep.append((ra, rb, toks))
                self.retired = keep
                self.uid += 1
                return self.nc.alloc_sbuf_tensor_at(
                    f"{name}_u{self.uid}", shape, dtype, offset=lo)
        raise MemoryError(f"SBUF full for {name} ({nbytes}B); live={self.live}")

    def sfree(self, name):
        a, b = self.live.pop(name)
        res = self.res.get(name)
        if res is not None:
            toks = list(res.readers)
            if res.last_w is not None:
                toks.append(res.last_w)
            if toks:
                self.retired.append((a, b, toks))
        self.free.append((a, b))
        self.free.sort()
        merged = []
        for seg in self.free:
            if merged and seg[0] <= merged[-1][1]:
                merged[-1] = (merged[-1][0], max(merged[-1][1], seg[1]))
            else:
                merged.append(seg)
        self.free = [tuple(x) for x in merged]

    def _waits(self, engine, dep_toks):
        waits = []
        own = self.sem_names[engine]
        for d in dep_toks:
            if d is None:
                continue
            if d.sem == own:
                if engine != "t":
                    waits.append((own, d.val))
                continue
            if d.sem in self.dma_sems:
                waits.append((d.sem, self.dma_sems[d.sem]))
            else:
                waits.append((d.sem, d.val))
        return waits

    def _gather(self, reads, writes, deps):
        dep_toks = list(deps)
        for r in reads:
            if r.last_w is not None:
                dep_toks.append(r.last_w)
        for w in writes:
            if w.last_w is not None:
                dep_toks.append(w.last_w)
            dep_toks.extend(w.readers)
        return dep_toks

    def op(self, engine, fn, reads=(), writes=(), deps=(), tag=""):
        dep_toks = self._gather(reads, writes, deps)
        waits = self._waits(engine, dep_toks)
        sem = self.sem_names[engine]
        self.cnt[engine] += 1
        tok = Tok(sem, self.cnt[engine])
        for (ws, wv) in waits:
            if ws in self.dma_sems:
                self.sem_consumers.setdefault(ws, []).append(tok)
        self.ops[engine].append((fn, waits, sem, 1, tag))
        for r in reads:
            r.readers.append(tok)
        for w in writes:
            w.last_w = tok
            w.readers = []
        return tok

    def dma(self, fn, sem_name, reads=(), writes=(), deps=(), tag=""):
        dep_toks = self._gather(reads, writes, deps)
        dep_toks.extend(self.sem_consumers.pop(sem_name, []))
        waits = self._waits("q", dep_toks)
        for (ws, wv) in waits:
            if ws in self.dma_sems:
                self.sem_consumers.setdefault(ws, []).append(Tok(sem_name, 0))
                self.sem_consumers[ws].pop()

        self.dma_sems.setdefault(sem_name, 0)
        self.dma_sems[sem_name] += 16
        tok = Tok(sem_name, self.dma_sems[sem_name])
        self.ops["q"].append((fn, waits, sem_name, 16, tag))
        for r in reads:
            r.readers.append(tok)
        for w in writes:
            w.last_w = tok
            w.readers = []
        return tok

    def finalize(self):
        nc = self.nc
        import contextlib
        with contextlib.ExitStack() as stack:
            sems = {}
            for e in self.ENGINES:
                n = self.sem_names[e]
                sems[n] = stack.enter_context(nc.semaphore(n))
            for n in self.dma_sems:
                sems[n] = stack.enter_context(nc.semaphore(n))
            block = stack.enter_context(nc.Block())

            def make_body(ename):
                def body(eng):
                    waited = {}
                    for fn, waits, semname, amt, tag in self.ops[ename]:
                        for (ws, wv) in waits:
                            if waited.get(ws, -1) < wv:
                                eng.wait_ge(sems[ws], wv)
                                waited[ws] = wv
                        fn(eng).then_inc(sems[semname], amt)
                    if ename == "q":
                        for sname, v in self.dma_sems.items():
                            if waited.get(sname, -1) < v:
                                eng.wait_ge(sems[sname], v)
                return body

            for ename, dec in (("t", block.tensor), ("v", block.vector),
                               ("s", block.scalar), ("g", block.gpsimd),
                               ("q", block.sync)):
                dec(make_body(ename))


def make_tables():
    t = {}
    t["ident"] = np.eye(128, dtype=np.float32)
    cb = np.zeros((128, 4), dtype=np.float32)
    cb[:, 0] = 1e-5
    cb[:, 1] = -10.0
    cb[:, 2] = 1e-8
    t["cbias"] = cb
    t["ones_col"] = np.ones((128, 1), dtype=np.float32)
    t["ones_mat"] = np.ones((128, 128), dtype=np.float32)
    blk = np.zeros((128, 128), dtype=np.float32)
    blk[:64, :64] = 1.0
    blk[64:, 64:] = 1.0
    t["blk_mat"] = blk
    e0 = np.zeros((128, 128), dtype=np.float32)
    e0[0, :] = 1.0
    t["e0_mat"] = e0
    t["ones_row"] = np.ones((1, 128), dtype=np.float32)
    top = np.zeros((128, 1), dtype=np.float32)
    top[:64] = 1.0
    t["ones_top"] = top
    bot = np.zeros((128, 1), dtype=np.float32)
    bot[64:] = 1.0
    t["ones_bot"] = bot

    D = 64
    inv_freq = 1.0 / (10000.0 ** (np.arange(0, D, 2, dtype=np.float64) / D))
    pos = np.arange(S, dtype=np.float64)
    fr = inv_freq[:, None] * pos[None, :]
    cos = np.cos(fr)
    sin = np.sin(fr)
    A = np.concatenate([cos, cos], 0)
    B = np.concatenate([-sin, sin], 0)
    t["ropeA"] = np.tile(A, (2, 1)).astype(np.float32)
    t["ropeB"] = np.tile(B, (2, 1)).astype(np.float32)

    q_loc = np.arange(512)[None, :]
    for p in range(4):
        k_glob = p * 128 + np.arange(128)[:, None]
        t[f"mask0_{p}"] = (q_loc >= k_glob).astype(np.float32)

    DH = 64
    WINDOW = 512
    half = np.arange(0, DH, 2, dtype=np.float64)
    invf = 1.0 / (10000.0 ** (half / DH))
    frq = invf[:, None] * pos[None, :]
    frq = np.concatenate([frq, frq], 0)
    base = (half + 0.4 * DH) / (1.4 * DH)
    power = (pos - S // 2) / (WINDOW / 2)
    scale = base[:, None] ** power[None, :]
    scale = np.concatenate([scale, scale], 0)
    cosq = np.cos(frq)
    sinq = np.sin(frq)
    sgn = np.where(np.arange(DH) < 32, -1.0, 1.0)[:, None]
    t["xposAq"] = np.tile(8.0 * cosq * scale, (2, 1)).astype(np.float32)
    t["xposBq"] = np.tile(8.0 * sgn * sinq * scale, (2, 1)).astype(np.float32)
    t["xposAk"] = np.tile(cosq / scale, (2, 1)).astype(np.float32)
    t["xposBk"] = np.tile(sgn * sinq / scale, (2, 1)).astype(np.float32)

    for kc in range(8):
        j_rel = kc * 128 + np.arange(128)[:, None] - 512
        valid = (q_loc >= j_rel) & (q_loc <= j_rel + 512)
        t[f"mask1_{kc}"] = valid.astype(np.float32)
    ql = np.arange(128)[:, None]
    cl = np.arange(128)[None, :]
    t["mask1U"] = np.where(ql >= cl, 0.0, -60000.0).astype(np.float32)
    t["mask1D"] = np.where(ql <= cl, 0.0, -60000.0).astype(np.float32)
    return t


def _cols(vec, n):
    nch = (n + 127) // 128
    out = np.zeros((nch * 128,), dtype=np.float32)
    out[:n] = np.asarray(vec, dtype=np.float32)
    return np.ascontiguousarray(out.reshape(nch, 128).T)


def prep_weights(inp):
    w = {}
    H, KVH, D, QL, KVL = 16, 4, 64, 64, 32
    perm = np.empty(D, dtype=np.int64)
    perm[:32] = np.arange(32) * 2
    perm[32:] = np.arange(32) * 2 + 1
    perm_sw = np.concatenate([perm[32:], perm[:32]])
    Wqb = np.asarray(inp["L0_attn_Wqb"], dtype=np.float32)
    qcols = Wqb.reshape(QL, H, D)
    w["l0a_Wqa"] = np.asarray(inp["L0_attn_Wqa"], dtype=np.float32)
    w["l0a_Wqb_n"] = np.ascontiguousarray(qcols[:, :, perm].reshape(QL, E))
    w["l0a_Wqb_s"] = np.ascontiguousarray(qcols[:, :, perm_sw].reshape(QL, E))
    Wkvb = np.asarray(inp["L0_attn_Wkvb"], dtype=np.float32)
    kv = Wkvb.reshape(KVL, KVH, D, 2)
    w["l0a_Wkva"] = np.asarray(inp["L0_attn_Wkva"], dtype=np.float32)
    w["l0a_Wkvb_k_n"] = np.ascontiguousarray(kv[:, :, perm, 0].reshape(KVL, KVH * D))
    w["l0a_Wkvb_k_s"] = np.ascontiguousarray(kv[:, :, perm_sw, 0].reshape(KVL, KVH * D))
    w["l0a_Wkvb_v"] = np.ascontiguousarray(kv[:, :, :, 1].reshape(KVL, KVH * D))
    w["l0a_Wout"] = np.asarray(inp["L0_attn_Wout"], dtype=np.float32)
    w["l0a_bout"] = _cols(inp["L0_attn_bout"], E)

    HID = 2730
    w["l0f_wqkv"] = np.asarray(inp["L0_ffn_wqkv"], dtype=np.float32)
    w["l0f_wo"] = np.asarray(inp["L0_ffn_wo"], dtype=np.float32)
    w1 = np.asarray(inp["L0_ffn_w1"], dtype=np.float32)
    w["l0f_w1a"] = np.ascontiguousarray(w1[:, :, :HID])
    w["l0f_w1g"] = np.ascontiguousarray(w1[:, :, HID:])
    w["l0f_w2"] = np.asarray(inp["L0_ffn_w2"], dtype=np.float32)
    b1 = np.asarray(inp["L0_ffn_b1"], dtype=np.float32)
    def _lstack(arrs):
        st = np.stack(arrs)
        return np.ascontiguousarray(st.transpose(1, 0, 2).reshape(128, -1))
    w["l0f_b1a"] = _lstack([_cols(b1[l, :HID], HID) for l in range(2)])
    w["l0f_b1g"] = _lstack([_cols(b1[l, HID:], HID) for l in range(2)])
    w["l0f_b2"] = _lstack([_cols(np.asarray(inp["L0_ffn_b2"])[l], E) for l in range(2)])
    w["l0f_lng"] = _lstack([_cols(np.asarray(inp["L0_ffn_ln_g"])[l], E) for l in range(2)])
    w["l0f_lnb"] = _lstack([_cols(np.asarray(inp["L0_ffn_ln_b"])[l], E) for l in range(2)])
    qs = np.asarray(inp["L0_ffn_q_scale"], dtype=np.float32)
    ks = np.asarray(inp["L0_ffn_k_scale"], dtype=np.float32)
    w["l0f_qs"] = _lstack([np.tile(qs[l], 2)[:, None].astype(np.float32) for l in range(2)])
    w["l0f_ks"] = _lstack([np.tile(ks[l], 2)[:, None].astype(np.float32) for l in range(2)])

    def _sw(vec):
        return np.concatenate([vec[32:], vec[:32]])
    w["l0f_qs_sw"] = _lstack([np.tile(_sw(qs[l]), 2)[:, None].astype(np.float32)
                              for l in range(2)])
    w["l0f_ks_sw"] = _lstack([np.tile(_sw(ks[l]), 2)[:, None].astype(np.float32)
                              for l in range(2)])

    w["l1a_Wq"] = np.asarray(inp["L1_attn_Wq"], dtype=np.float32)
    w["l1a_Wk"] = np.asarray(inp["L1_attn_Wk"], dtype=np.float32)
    w["l1a_Wv"] = np.asarray(inp["L1_attn_Wv"], dtype=np.float32)
    w["l1a_proj"] = np.asarray(inp["L1_attn_proj"], dtype=np.float32)
    w["l1a_Wo"] = np.asarray(inp["L1_attn_Wo"], dtype=np.float32)
    w["l1a_bo"] = _cols(inp["L1_attn_bo"], E)

    w["l1f_w1"] = np.asarray(inp["L1_ffn_w1"], dtype=np.float32)
    w["l1f_w2"] = np.asarray(inp["L1_ffn_w2"], dtype=np.float32)
    cw = np.asarray(inp["L1_ffn_conv_w"], dtype=np.float32).reshape(32, 128, 3)
    w["l1f_convw"] = np.ascontiguousarray(cw.transpose(1, 2, 0).reshape(128, 96))
    w["l1f_convb"] = _cols(inp["L1_ffn_conv_b"], 4096)
    return w


DRAM_SPECS = [
    ("x_in", (S, E), F32), ("ident", (128, 128), F32), ("cbias", (128, 4), F32),
    ("ones_col", (128, 1), F32), ("ones_row", (1, 128), F32),
    ("ones_mat", (128, 128), F32), ("blk_mat", (128, 128), F32),
    ("e0_mat", (128, 128), F32),
    ("ones_top", (128, 1), F32), ("ones_bot", (128, 1), F32),
    ("ropeA", (128, S), F32), ("ropeB", (128, S), F32),
    ("mask0_0", (128, 512), F32), ("mask0_1", (128, 512), F32),
    ("mask0_2", (128, 512), F32), ("mask0_3", (128, 512), F32),
    ("xposAq", (128, S), F32), ("xposBq", (128, S), F32),
    ("xposAk", (128, S), F32), ("xposBk", (128, S), F32),
] + [(f"mask1_{k}", (128, 512), F32) for k in range(8)]\
  + [("mask1U", (128, 128), F32), ("mask1D", (128, 128), F32)] + [
    ("l0a_Wqa", (E, 64), F32), ("l0a_Wqb_n", (64, E), F32), ("l0a_Wqb_s", (64, E), F32),
    ("l0a_Wkva", (E, 32), F32),
    ("l0a_Wkvb_k_n", (32, 256), F32), ("l0a_Wkvb_k_s", (32, 256), F32),
    ("l0a_Wkvb_v", (32, 256), F32),
    ("l0a_Wout", (E, E), F32), ("l0a_bout", (128, 8), F32),
    ("l0f_wqkv", (2, E, 1536), F32), ("l0f_wo", (2, 512, E), F32),
    ("l0f_w1a", (2, E, 2730), F32), ("l0f_w1g", (2, E, 2730), F32),
    ("l0f_w2", (2, 2730, E), F32),
    ("l0f_b1a", (128, 44), F32), ("l0f_b1g", (128, 44), F32),
    ("l0f_b2", (128, 16), F32),
    ("l0f_lng", (128, 16), F32), ("l0f_lnb", (128, 16), F32),
    ("l0f_qs", (128, 2), F32), ("l0f_ks", (128, 2), F32),
    ("l0f_qs_sw", (128, 2), F32), ("l0f_ks_sw", (128, 2), F32),
    ("l1a_Wq", (E, E), F32), ("l1a_Wk", (E, E), F32), ("l1a_Wv", (E, E), F32),
    ("l1a_proj", (64, 256), F32), ("l1a_Wo", (E, E), F32), ("l1a_bo", (128, 8), F32),
    ("l1f_w1", (4096, E), F32), ("l1f_w2", (E, 4096), F32),
    ("l1f_convw", (128, 96), F32), ("l1f_convb", (128, 32), F32),
]


class Prog:
    def __init__(self, n_blocks=4):
        self.n_blocks = n_blocks
        nc = bass.Bass("TRN2", target_bir_lowering=False)
        self.nc = nc
        kb = KB(nc)
        self.kb = kb
        self.dr = {}
        for name, shape, dt in DRAM_SPECS:
            self.dr[name] = nc.dram_tensor(name, list(shape), dt, kind="ExternalInput").ap()
        self.y_out = nc.dram_tensor("y_out", [S, E], F32, kind="ExternalOutput").ap()
        self.park_main = nc.dram_tensor("park_main", [NCH, 128, S], F32, kind="Internal").ap()
        self.park_int = nc.dram_tensor("park_int", [NCH, 128, S], F32, kind="Internal").ap()
        self.park_int2 = nc.dram_tensor("park_int2", [NCH, 128, S], F32, kind="Internal").ap()
        self.park_silu = nc.dram_tensor("park_silu", [32, 128, S], F32, kind="Internal").ap()

        self.dbg_names = [x for x in os.environ.get("KDBG", "").split(",") if x]
        self.pA = nc.alloc_psum_tensor("pA", [128, 2048], F32)
        self.pB = nc.alloc_psum_tensor("pB", [128, 2048], F32)
        self.PBANK = [kb.R(f"pbank{i}") for i in range(8)]

        self.build()

    def dbg(self, name, tensor, res, rows=128, cols=S):
        if name not in self.dbg_names:
            return
        ap = self.nc.dram_tensor("dbg_" + name, [rows, cols],
                                 tensor.dtype, kind="ExternalOutput").ap()
        self.kb.dma(lambda q: q.dma_start(ap, tensor[0:rows, 0:cols]), "dmdbg",
                    reads=[res], tag="dbg" + name)

    def prA(self, c0, c1):
        return [self.PBANK[i] for i in range(c0 // 512, (c1 - 1) // 512 + 1)]

    def prB(self, c0, c1):
        return [self.PBANK[4 + i] for i in range(c0 // 512, (c1 - 1) // 512 + 1)]

    def build(self):
        kb, nc, dr = self.kb, self.nc, self.dr

        self.c_ident = kb.salloc("c_ident", [128, 128], F32)
        self.c_ones = kb.salloc("c_ones", [128, 1], F32)
        self.c_onesb = kb.salloc("c_onesb", [128, 1], BF16)
        self.c_onesr = kb.salloc("c_onesr", [1, 128], F32)
        self.c_onesrb = kb.salloc("c_onesrb", [1, 128], BF16)
        self.c_onesm = kb.salloc("c_onesm", [128, 128], F32)
        self.c_onesmb = kb.salloc("c_onesmb", [128, 128], BF16)
        self.c_blkf = kb.salloc("c_blkf", [128, 128], F32)
        self.c_blkb = kb.salloc("c_blkb", [128, 128], BF16)
        self.c_e0 = kb.salloc("c_e0", [128, 128], F32)
        self.c_e0b = kb.salloc("c_e0b", [128, 128], BF16)
        self.c_bias = kb.salloc("c_bias", [128, 4], F32)
        for nm, t, src in (("c_ident", self.c_ident, "ident"),
                           ("c_ones", self.c_ones, "ones_col"),
                           ("c_onesr", self.c_onesr, "ones_row"),
                           ("c_onesm", self.c_onesm, "ones_mat"),
                           ("c_blkf", self.c_blkf, "blk_mat"),
                           ("c_e0", self.c_e0, "e0_mat"),
                           ("c_bias", self.c_bias, "cbias")):
            kb.dma(lambda q, t=t, src=src: q.dma_start(t[:], dr[src]), "dmc",
                   writes=[kb.R(nm)])
        for nm2, dst, srcn in (("c_onesb", self.c_onesb, "c_ones"),
                               ("c_onesrb", self.c_onesrb, "c_onesr"),
                               ("c_onesmb", self.c_onesmb, "c_onesm"),
                               ("c_blkb", self.c_blkb, "c_blkf"),
                               ("c_e0b", self.c_e0b, "c_e0")):
            srct = getattr(self, srcn)
            kb.op("v", lambda v, dst=dst, srct=srct: v.tensor_copy(dst[:], srct[:]),
                  reads=[kb.R(srcn)], writes=[kb.R(nm2)])
        self.Rid = kb.R("c_ident")
        self.Ron = kb.R("c_ones")
        self.Ronb = kb.R("c_onesb")
        self.Ronr = kb.R("c_onesr")
        self.Ronrb = kb.R("c_onesrb")
        self.Ronesm = kb.R("c_onesm")
        self.Ronesmb = kb.R("c_onesmb")
        self.Rblkb = kb.R("c_blkb")
        self.Re0 = kb.R("c_e0")
        self.Re0b = kb.R("c_e0b")
        self.Rcb = kb.R("c_bias")

        self.bias = {}
        for nm, shape in (("l0a_bout", (128, 8)), ("l0f_b1a", (128, 44)),
                          ("l0f_b1g", (128, 44)), ("l0f_b2", (128, 16)),
                          ("l0f_lng", (128, 16)), ("l0f_lnb", (128, 16)),
                          ("l0f_qs", (128, 2)), ("l0f_ks", (128, 2)),
                          ("l0f_qs_sw", (128, 2)), ("l0f_ks_sw", (128, 2)),
                          ("l1a_bo", (128, 8)), ("l1f_convb", (128, 32))):
            t = kb.salloc("sb_" + nm, [128, shape[1]], F32)
            kb.dma(lambda q, t=t, nm=nm: q.dma_start(t[:], dr[nm]), "dmb",
                   writes=[kb.R("b_" + nm)])
            self.bias[nm] = (t, kb.R("b_" + nm))

        self.SCa = kb.salloc("SCa", [128, S], BF16)
        self.SCb = kb.salloc("SCb", [128, S], BF16)
        self.SC4 = kb.salloc("SC4", [128, S], F32)
        self.SC1 = kb.salloc("SC1", [128, S], F32)
        self.SC2 = kb.salloc("SC2", [128, S], F32)
        self.SC3 = kb.salloc("SC3", [128, S], F32)
        self.RSC2 = kb.R("SC2")
        self.RSC3 = kb.R("SC3")
        self.RSCa = kb.R("SCa")
        self.RSC4 = kb.R("SC4")
        self.RSCb = kb.R("SCb")
        self.RSC1 = kb.R("SC1")
        self.ROWT = kb.salloc("ROWT", [128, 2304], F32)
        self.ROW = self.ROWT
        self.RROW = kb.R("ROWT")
        kb.op("g", lambda g_: g_.memset(self.ROWT[:], 0.0), writes=[self.RROW])
        self.MCOL = kb.salloc("MCOL", [128, 16], F32)
        self.RMCOL = kb.R("MCOL")

        self.X = [kb.salloc(f"X{c}", [128, S], F32) for c in range(NCH)]
        self.RX = [kb.R(f"X{c}") for c in range(NCH)]

        self.p0_load()
        if self.n_blocks >= 1:
            self.blk_l0_attn()
        if self.n_blocks >= 2:
            self.blk_l0_ffn()
        if self.n_blocks >= 3:
            self.blk_l1_attn()
        if self.n_blocks >= 4:
            self.blk_l1_ffn()
        self.p_out()
        kb.finalize()

    def p0_load(self):
        kb, dr = self.kb, self.dr
        pA = self.pA
        tin = [kb.salloc(f"tin{i}", [128, E], F32) for i in range(4)]
        Rtin = [kb.R(f"tin{i}") for i in range(4)]
        for tt in range(16):
            b = tt % 4
            kb.dma(lambda q, tt=tt, b=b: q.dma_start(
                tin[b][:], dr["x_in"][128 * tt:128 * (tt + 1), :]),
                f"dmi{b}", writes=[Rtin[b]])
            for c in range(NCH):
                col0 = 512 * (c % 4)
                kb.op("t", lambda t, b=b, c=c, col0=col0: t.transpose(
                    pA[:, col0:col0 + 128], tin[b][:, 128 * c:128 * (c + 1)],
                    self.c_ident[:]),
                    reads=[Rtin[b], self.Rid], writes=self.prA(col0, col0 + 128))
                kb.op("v", lambda v, tt=tt, col0=col0, Xc=self.X[c]: v.tensor_copy(
                    Xc[:, 128 * tt:128 * (tt + 1)], pA[:, col0:col0 + 128]),
                    reads=self.prA(col0, col0 + 128), writes=[self.RX[c]])
        for i in range(4):
            kb.sfree(f"tin{i}")

    def emit_ln(self, srcap, rsrc, outb, routb, outf_ap=None, routf=None,
               gb=None, tagp="ln"):
        kb, pA, pB = self.kb, self.pA, self.pB
        sc = [self.SCa, self.SCb]
        rs = [self.RSCa, self.RSCb]
        sq = sc
        rq = rs
        for c in range(NCH):
            j = c % 2
            kb.op("s", lambda s, j=j, ap=srcap(c): s.copy(sc[j][:], ap),
                  reads=[rsrc[c]], writes=[rs[j]], tag=f"{tagp}cp{c}")
            for nb in range(4):
                kb.op("t", lambda t, c=c, nb=nb, j=j: t.matmul(
                    pA[:, 512 * nb:512 * (nb + 1)], self.c_onesmb[:],
                    sc[j][:, 512 * nb:512 * (nb + 1)],
                    start=(c == 0), stop=(c == NCH - 1)),
                    reads=[rs[j], self.Ronesmb], writes=self.prA(512 * nb, 512 * nb + 512))
            kb.op("v", lambda v, j=j, ap=srcap(c): v.tensor_tensor(
                sq[j][:], ap, ap, ALU.mult),
                reads=[rsrc[c]], writes=[rq[j]], tag=f"{tagp}sq{c}")
            for nb in range(4):
                kb.op("t", lambda t, c=c, nb=nb, j=j: t.matmul(
                    pB[:, 512 * nb:512 * (nb + 1)], self.c_onesmb[:],
                    sq[j][:, 512 * nb:512 * (nb + 1)],
                    start=(c == 0), stop=(c == NCH - 1)),
                    reads=[rq[j], self.Ronesmb], writes=self.prB(512 * nb, 512 * nb + 512))
        kb.op("v", lambda v: v.tensor_scalar(
            self.SC2[:], pA[:, 0:S], 1.0 / E, None, ALU.mult),
            reads=self.prA(0, S), writes=[self.RSC2], tag=f"{tagp}m")
        kb.op("v", lambda v: v.tensor_tensor(
            self.SC1[:], self.SC2[:], self.SC2[:], ALU.mult),
            reads=[self.RSC2], writes=[self.RSC1], tag=f"{tagp}mm")
        kb.op("v", lambda v: v.scalar_tensor_tensor(
            self.SC1[:], pB[:, 0:S], 1.0 / E, self.SC1[:], ALU.mult, ALU.subtract),
            reads=self.prB(0, S) + [self.RSC1], writes=[self.RSC1], tag=f"{tagp}v")
        kb.op("s", lambda s: s.activation(self.SC1[:], self.SC1[:], AF.Ln,
                                          bias=self.c_bias[:, 0:1]),
              reads=[self.RSC1, self.Rcb], writes=[self.RSC1], tag=f"{tagp}ln")
        kb.op("s", lambda s: s.activation(self.SC1[:], self.SC1[:], AF.Exp, scale=-0.5),
              reads=[self.RSC1], writes=[self.RSC1], tag=f"{tagp}ex")
        kb.op("v", lambda v: v.scalar_tensor_tensor(
            self.SC3[:], self.SC2[:], -1.0, self.SC1[:], ALU.mult, ALU.mult),
            reads=[self.RSC2, self.RSC1], writes=[self.RSC3], tag=f"{tagp}b")
        for c in range(NCH):
            j = c % 2
            scr = self.SC2 if j == 0 else self.SC4
            rscr = self.RSC2 if j == 0 else self.RSC4
            aeng = "v"
            kb.op(aeng, lambda e, ap=srcap(c), scr=scr: e.tensor_tensor(
                scr[:], ap, self.SC1[:], ALU.mult),
                reads=[rsrc[c], self.RSC1], writes=[rscr], tag=f"{tagp}a1_{c}")
            if outf_ap is not None:
                kb.op(aeng, lambda e, ap=outf_ap(c), scr=scr: e.tensor_tensor(
                    ap, scr[:], self.SC3[:], ALU.add),
                    reads=[rscr, self.RSC3], writes=[routf[c]],
                    tag=f"{tagp}a2f{c}")
                if outb[c] is not None:
                    kb.op("s", lambda s, oap=outb[c][:], iap=outf_ap(c): s.copy(oap, iap),
                          reads=[routf[c]], writes=[routb[c]], tag=f"{tagp}a3_{c}")
            elif gb is not None:
                gap, bap, rgb = gb
                kb.op(aeng, lambda e, c=c, scr=scr: e.tensor_tensor(
                    scr[:], scr[:], self.SC3[:], ALU.add),
                    reads=[rscr, self.RSC3], writes=[rscr],
                    tag=f"{tagp}a2_{c}")
                kb.op("s", lambda s, oap=outb[c][:], ga=gap(c), ba=bap(c), scr=scr,
                      rscr=rscr: s.activation(
                    oap, scr[:], AF.Identity, scale=ga, bias=ba),
                    reads=[rscr, rgb], writes=[routb[c]], tag=f"{tagp}a3_{c}")
            else:
                kb.op(aeng, lambda e, oap=outb[c][:], scr=scr: e.tensor_tensor(
                    oap, scr[:], self.SC3[:], ALU.add),
                    reads=[rscr, self.RSC3], writes=[routb[c]],
                    tag=f"{tagp}a2_{c}")

    def load_wchunk(self, dram_ap, fbuf, rf, bbuf, rb, sem, tag=""):
        kb = self.kb
        kb.dma(lambda q: q.dma_start(fbuf[:], dram_ap), sem, writes=[rf], tag=tag)
        kb.op("v", lambda v: v.tensor_copy(bbuf[:], fbuf[:]), reads=[rf], writes=[rb],
              tag=tag + "cv")

    def blk_l0_attn(self):
        kb, nc, dr, pA, pB = self.kb, self.nc, self.dr, self.pA, self.pB
        LNO = [kb.salloc(f"aLN{c}", [128, S], BF16) for c in range(NCH)]
        RLNO = [kb.R(f"aLN{c}") for c in range(NCH)]
        Xs_now = list(self.X)
        self.emit_ln(lambda c: Xs_now[c][:], self.RX, LNO, RLNO, tagp="ln0")
        self.dbg("ln0", LNO[0], RLNO[0])
        for c in range(NCH):
            kb.dma(lambda q, c=c, Xc=self.X[c]: q.dma_start(self.park_main[c], Xc[:]),
                   "dmpk", reads=[self.RX[c]], writes=[kb.R("d_pm")], tag=f"parkX0{c}")
        for c in range(NCH):
            kb.sfree(f"X{c}")

        wqa = kb.salloc("wqa", [128, 8 * 64], BF16)
        wkva = kb.salloc("wkva", [128, 8 * 32], BF16)
        wqbn = kb.salloc("wqbn", [64, E], BF16)
        wqbs = kb.salloc("wqbs", [64, E], BF16)
        wkbn = kb.salloc("wkbn", [32, 256], BF16)
        wkbs = kb.salloc("wkbs", [32, 256], BF16)
        wkbv = kb.salloc("wkbv", [32, 256], BF16)
        stg = kb.salloc("stg", [128, E], F32)
        Rstg = kb.R("stg")
        names = ["wqa", "wkva", "wqbn", "wqbs", "wkbn", "wkbs", "wkbv"]
        Rw = {n: kb.R(n) for n in names}
        kb.dma(lambda q: q.dma_start(
            stg[:, 0:512].rearrange("p (c k) -> p c k", k=64),
            dr["l0a_Wqa"].rearrange("(c p) k -> p c k", p=128)),
            "dmw0", writes=[Rstg])
        kb.op("v", lambda v: v.tensor_copy(wqa[:], stg[:, 0:512]), reads=[Rstg],
              writes=[Rw["wqa"]])
        kb.dma(lambda q: q.dma_start(
            stg[:, 512:768].rearrange("p (c k) -> p c k", k=32),
            dr["l0a_Wkva"].rearrange("(c p) k -> p c k", p=128)),
            "dmw0", reads=[Rstg], writes=[Rstg])
        kb.op("v", lambda v: v.tensor_copy(wkva[:], stg[:, 512:768]), reads=[Rstg],
              writes=[Rw["wkva"]])
        for nm, dnm, tgt in (("wqbn", "l0a_Wqb_n", wqbn), ("wqbs", "l0a_Wqb_s", wqbs)):
            kb.dma(lambda q, dnm=dnm: q.dma_start(stg[0:64, :], dr[dnm]), "dmw0",
                   reads=[Rstg], writes=[Rstg])
            kb.op("v", lambda v, tgt=tgt: v.tensor_copy(tgt[:], stg[0:64, :]),
                  reads=[Rstg], writes=[Rw[nm]])
        for nm, dnm, tgt in (("wkbn", "l0a_Wkvb_k_n", wkbn),
                             ("wkbs", "l0a_Wkvb_k_s", wkbs),
                             ("wkbv", "l0a_Wkvb_v", wkbv)):
            kb.dma(lambda q, dnm=dnm: q.dma_start(stg[0:32, 0:256], dr[dnm]), "dmw0",
                   reads=[Rstg], writes=[Rstg])
            kb.op("v", lambda v, tgt=tgt: v.tensor_copy(tgt[:], stg[0:32, 0:256]),
                  reads=[Rstg], writes=[Rw[nm]])

        qlat = kb.salloc("qlat", [64, S], BF16)
        kvlat = kb.salloc("kvlat", [32, S], BF16)
        Rqlat, Rkvlat = kb.R("qlat"), kb.R("kvlat")
        for c in range(NCH):
            for nb in range(4):
                kb.op("t", lambda t, c=c, nb=nb: t.matmul(
                    pA[0:64, 512 * nb:512 * (nb + 1)], wqa[:, 64 * c:64 * (c + 1)],
                    LNO[c][:, 512 * nb:512 * (nb + 1)],
                    start=(c == 0), stop=(c == NCH - 1)),
                    reads=[RLNO[c], Rw["wqa"]], writes=self.prA(512 * nb, 512 * nb + 512))
        kb.op("v", lambda v: v.tensor_copy(qlat[:], pA[0:64, :]),
              reads=self.prA(0, S), writes=[Rqlat])
        self.dbg("qlat", qlat, Rqlat, rows=64)
        for c in range(NCH):
            for nb in range(4):
                kb.op("t", lambda t, c=c, nb=nb: t.matmul(
                    pB[0:32, 512 * nb:512 * (nb + 1)], wkva[:, 32 * c:32 * (c + 1)],
                    LNO[c][:, 512 * nb:512 * (nb + 1)],
                    start=(c == 0), stop=(c == NCH - 1)),
                    reads=[RLNO[c], Rw["wkva"]], writes=self.prB(512 * nb, 512 * nb + 512))
        kb.op("v", lambda v: v.tensor_copy(kvlat[:], pB[0:32, :]),
              reads=self.prB(0, S), writes=[Rkvlat])

        for c in range(NCH):
            kb.sfree(f"aLN{c}")
        kb.sfree("wqa"); kb.sfree("wkva"); kb.sfree("stg")

        rA = kb.salloc("rA", [128, S], F32)
        rB = kb.salloc("rB", [128, S], F32)
        RrA, RrB = kb.R("rA"), kb.R("rB")
        kb.dma(lambda q: q.dma_start(rA[:], dr["ropeA"]), "dmw0", writes=[RrA])
        kb.dma(lambda q: q.dma_start(rB[:], dr["ropeB"]), "dmw0", writes=[RrB])

        QH = [kb.salloc(f"QH{h}", [64, S], BF16) for h in range(16)]
        RQH = [kb.R(f"QH{h}") for h in range(16)]
        KG = [kb.salloc(f"KG{g}", [64, S], BF16) for g in range(4)]
        RKG = [kb.R(f"KG{g}") for g in range(4)]
        for oc in range(8):
            for nb in range(4):
                kb.op("t", lambda t, oc=oc, nb=nb: t.matmul(
                    pA[:, 512 * nb:512 * (nb + 1)], wqbn[:, 128 * oc:128 * (oc + 1)],
                    qlat[:, 512 * nb:512 * (nb + 1)], start=True, stop=True),
                    reads=[Rqlat, Rw["wqbn"]], writes=self.prA(512 * nb, 512 * nb + 512))
                kb.op("t", lambda t, oc=oc, nb=nb: t.matmul(
                    pB[:, 512 * nb:512 * (nb + 1)], wqbs[:, 128 * oc:128 * (oc + 1)],
                    qlat[:, 512 * nb:512 * (nb + 1)], start=True, stop=True),
                    reads=[Rqlat, Rw["wqbs"]], writes=self.prB(512 * nb, 512 * nb + 512))
            kb.op("v", lambda v: v.tensor_tensor(self.SC1[:], pA[:, 0:S], rA[:], ALU.mult),
                  reads=self.prA(0, S) + [RrA], writes=[self.RSC1])
            kb.op("v", lambda v: v.tensor_tensor(
                self.SCa[:], pB[:, 0:S], rB[:], ALU.mult),
                reads=self.prB(0, S) + [RrB], writes=[self.RSCa])
            for half in range(2):
                h = 2 * oc + half
                kb.op("v", lambda v, h=h, half=half: v.tensor_tensor(
                    QH[h][:], self.SC1[64 * half:64 * (half + 1), :],
                    self.SCa[64 * half:64 * (half + 1), :], ALU.add),
                    reads=[self.RSC1, self.RSCa], writes=[RQH[h]])
        for oc in range(2):
            for nb in range(4):
                kb.op("t", lambda t, oc=oc, nb=nb: t.matmul(
                    pA[:, 512 * nb:512 * (nb + 1)], wkbn[:, 128 * oc:128 * (oc + 1)],
                    kvlat[:, 512 * nb:512 * (nb + 1)], start=True, stop=True),
                    reads=[Rkvlat, Rw["wkbn"]], writes=self.prA(512 * nb, 512 * nb + 512))
                kb.op("t", lambda t, oc=oc, nb=nb: t.matmul(
                    pB[:, 512 * nb:512 * (nb + 1)], wkbs[:, 128 * oc:128 * (oc + 1)],
                    kvlat[:, 512 * nb:512 * (nb + 1)], start=True, stop=True),
                    reads=[Rkvlat, Rw["wkbs"]], writes=self.prB(512 * nb, 512 * nb + 512))
            kb.op("v", lambda v: v.tensor_tensor(self.SC1[:], pA[:, 0:S], rA[:], ALU.mult),
                  reads=self.prA(0, S) + [RrA], writes=[self.RSC1])
            kb.op("v", lambda v: v.tensor_tensor(self.SCa[:], pB[:, 0:S], rB[:], ALU.mult),
                  reads=self.prB(0, S) + [RrB], writes=[self.RSCa])
            for half in range(2):
                g = 2 * oc + half
                kb.op("v", lambda v, g=g, half=half: v.tensor_tensor(
                    KG[g][:], self.SC1[64 * half:64 * (half + 1), :],
                    self.SCa[64 * half:64 * (half + 1), :], ALU.add),
                    reads=[self.RSC1, self.RSCa], writes=[RKG[g]])

        self.dbg("qh0", QH[0], RQH[0], rows=64)
        self.dbg("kg0", KG[0], RKG[0], rows=64)
        VT = [kb.salloc(f"VT{tt}", [128, 260], BF16) for tt in range(16)]
        RVT = [kb.R(f"VT{tt}") for tt in range(16)]
        for tt in range(16):
            kb.op("g", lambda g_, tt=tt: g_.memset(
                VT[tt][:].rearrange("p (g c) -> p g c", c=65)[:, :, 64:65], 1.0),
                writes=[RVT[tt]])
        for tt in range(16):
            col0 = 512 * (tt % 4)
            kb.op("t", lambda t, tt=tt, col0=col0: t.matmul(
                pA[:, col0:col0 + 256], kvlat[:, 128 * tt:128 * (tt + 1)],
                wkbv[:], start=True, stop=True),
                reads=[Rkvlat, Rw["wkbv"]], writes=self.prA(col0, col0 + 256))
            kb.op("v", lambda v, tt=tt, col0=col0: v.tensor_copy(
                VT[tt][:].rearrange("p (g c) -> p g c", c=65)[:, :, 0:64],
                pA[:, col0:col0 + 256].rearrange("p (g c) -> p g c", c=64)),
                reads=self.prA(col0, col0 + 256), writes=[RVT[tt]])

        self.dbg("vt0", VT[0], RVT[0], cols=260)
        kb.sfree("qlat"); kb.sfree("kvlat")
        kb.sfree("rA"); kb.sfree("rB")
        for n in ("wqbn", "wqbs", "wkbn", "wkbs", "wkbv"):
            kb.sfree(n)

        mstg = kb.salloc("mstg", [128, 512], F32)
        Rmstg = kb.R("mstg")
        M0 = [kb.salloc(f"M0_{p}", [128, 512], BF16) for p in range(4)]
        RM0 = [kb.R(f"M0_{p}") for p in range(4)]
        for p in range(4):
            kb.dma(lambda q, p=p: q.dma_start(mstg[:], dr[f"mask0_{p}"]), "dmw0",
                   reads=[Rmstg], writes=[Rmstg])
            kb.op("v", lambda v, p=p: v.tensor_copy(M0[p][:], mstg[:]),
                  reads=[Rmstg], writes=[RM0[p]])
        kb.sfree("mstg")

        ATT = [kb.salloc(f"ATT{c}", [128, S], BF16) for c in range(NCH)]
        RATT = [kb.R(f"ATT{c}") for c in range(NCH)]
        PT = [kb.salloc(f"PT{i}", [128, 512], BF16) for i in range(4)]
        RPT = [kb.R(f"PT{i}") for i in range(4)]
        TMP64 = kb.salloc("TMP64", [64, 512], BF16)
        RTMP64 = kb.R("TMP64")
        BCS = kb.salloc("BCS", [64, 512], F32)
        RBCS = kb.R("BCS")
        it = 0
        for h in range(16):
            g = h // 4
            for qt in range(4):
                nkc = 4 * (qt + 1)
                par = it % 2
                acc0 = 512 * par
                bc0 = 1024 + 512 * par
                pend_pv = None
                for kc in range(nkc):
                    i = kc % 4
                    pt_cols = 512 * i
                    kb.op("t", lambda t, h=h, g=g, qt=qt, kc=kc, pt_cols=pt_cols:
                          t.matmul(pA[:, pt_cols:pt_cols + 512],
                                   KG[g][:, 128 * kc:128 * (kc + 1)],
                                   QH[h][:, 512 * qt:512 * (qt + 1)],
                                   start=True, stop=True),
                          reads=[RKG[g], RQH[h]], writes=self.prA(pt_cols, pt_cols + 512))
                    kb.op("s", lambda s, i=i, pt_cols=pt_cols: s.activation(
                        PT[i][:], pA[:, pt_cols:pt_cols + 512], AF.Exp, scale=0.125),
                        reads=self.prA(pt_cols, pt_cols + 512),
                        writes=[RPT[i]])
                    dk = kc - 4 * qt
                    if dk >= 0:
                        meng = "v" if i % 2 == 0 else "g"
                        kb.op(meng, lambda e, i=i, dk=dk: e.tensor_tensor(
                            PT[i][:], PT[i][:], M0[dk][:], ALU.mult),
                            reads=[RPT[i], RM0[dk]], writes=[RPT[i]])
                    if pend_pv is not None:
                        pend_pv()
                    pend_pv = (lambda i=i, kc=kc, g=g, nkc=nkc, acc0=acc0, par=par:
                               kb.op("t", lambda t, i=i, kc=kc, g=g, nkc=nkc,
                                     acc0=acc0, par=par: t.matmul(
                                   pB[0:65, acc0:acc0 + 512],
                                   VT[kc][:, 65 * g:65 * (g + 1)],
                                   PT[i][:], start=(kc == 0), stop=(kc == nkc - 1)),
                                   reads=[RVT[kc], RPT[i]],
                                   writes=[self.PBANK[4 + par]]))
                pend_pv()
                zr = self.ROW[0:1, 0:512]
                kb.op("v", lambda v, acc0=acc0, par=par: v.tensor_copy(
                    zr, pB[64:65, acc0:acc0 + 512]),
                    reads=[self.PBANK[4 + par]], writes=[self.RROW])
                kb.op("s", lambda s: s.activation(zr, zr, AF.Ln),
                      reads=[self.RROW], writes=[self.RROW])
                kb.op("s", lambda s: s.activation(zr, zr, AF.Exp, scale=-1.0),
                      reads=[self.RROW], writes=[self.RROW])
                kb.op("t", lambda t, bc0=bc0, par=par: t.matmul(
                    pB[0:64, bc0:bc0 + 512], self.c_e0[:, 0:64], self.ROWT[:, 0:512],
                    start=True, stop=True),
                    reads=[self.RROW, self.Re0], writes=[self.PBANK[6 + par]])
                kb.op("s", lambda s, bc0=bc0: s.copy(BCS[:], pB[0:64, bc0:bc0 + 512]),
                      reads=[self.PBANK[6 + par]], writes=[RBCS])
                kb.op("v", lambda v, acc0=acc0, par=par: v.tensor_tensor(
                    TMP64[:], pB[0:64, acc0:acc0 + 512], BCS[:], ALU.mult),
                    reads=[self.PBANK[4 + par], RBCS], writes=[RTMP64])
                c = h // 2
                half = h % 2
                kb.op("v", lambda v, c=c, half=half, qt=qt: v.tensor_copy(
                    ATT[c][64 * half:64 * (half + 1), 512 * qt:512 * (qt + 1)],
                    TMP64[:]),
                    reads=[RTMP64], writes=[RATT[c]])
                it += 1

        self.dbg("att0", ATT[0], RATT[0])
        for h in range(16):
            kb.sfree(f"QH{h}")
        for g in range(4):
            kb.sfree(f"KG{g}")
        for tt in range(16):
            kb.sfree(f"VT{tt}")
        for i in range(4):
            kb.sfree(f"PT{i}")
        for p in range(4):
            kb.sfree(f"M0_{p}")

        wof = [kb.salloc(f"wof{i}", [128, E], F32) for i in range(2)]
        Rwof = [kb.R(f"wof{i}") for i in range(2)]
        wob = [kb.salloc(f"wob{c}", [128, E], BF16) for c in range(NCH)]
        Rwob = [kb.R(f"wob{c}") for c in range(NCH)]
        for c in range(NCH):
            i = c % 2
            kb.dma(lambda q, c=c, i=i: q.dma_start(
                wof[i][:], dr["l0a_Wout"][128 * c:128 * (c + 1), :]), f"dmw0{i}",
                reads=[Rwof[i]], writes=[Rwof[i]])
            kb.op("s", lambda s_, c=c, i=i: s_.copy(wob[c][:], wof[i][:]),
                  reads=[Rwof[i]], writes=[Rwob[c]])
        bout, Rbout = self.bias["l0a_bout"]
        self.X = [kb.salloc(f"X{c}", [128, S], F32) for c in range(NCH)]
        self.RX = [kb.R(f"X{c}") for c in range(NCH)]
        for oc in range(8):
            pt = pA if oc % 2 == 0 else pB
            prf = self.prA if oc % 2 == 0 else self.prB
            kb.dma(lambda q, oc=oc, Xc=self.X[oc]: q.dma_start(Xc[:], self.park_main[oc]),
                   "dmpk", reads=[kb.R("d_pm")], writes=[self.RX[oc]], tag=f"unparkX0{oc}")
            for c in range(NCH):
                for nb in range(4):
                    kb.op("t", lambda t, oc=oc, c=c, nb=nb, pt=pt, prf=prf: t.matmul(
                        pt[:, 512 * nb:512 * (nb + 1)],
                        wob[c][:, 128 * oc:128 * (oc + 1)],
                        ATT[c][:, 512 * nb:512 * (nb + 1)],
                        start=(c == 0), stop=(c == NCH - 1)),
                        reads=[Rwob[c], RATT[c]],
                        writes=prf(512 * nb, 512 * nb + 512))
            kb.op("v", lambda v, oc=oc, Xc=self.X[oc], pt=pt, prf=prf:
                  v.scalar_tensor_tensor(
                      Xc[:], pt[:, 0:S], bout[:, oc:oc + 1], Xc[:],
                      ALU.add, ALU.add),
                  reads=prf(0, S) + [Rbout, self.RX[oc]], writes=[self.RX[oc]])
        for i in range(2):
            kb.sfree(f"wof{i}")
        for c in range(NCH):
            kb.sfree(f"wob{c}")
        for c in range(NCH):
            kb.sfree(f"ATT{c}")
        kb.sfree("TMP64"); kb.sfree("BCS")

    def blk_l0_ffn(self):
        kb, nc, dr, pA, pB = self.kb, self.nc, self.dr, self.pA, self.pB
        INT = [kb.salloc(f"INT{c}", [128, 2052], F32) for c in range(NCH)]
        RINT = [kb.R(f"INT{c}") for c in range(NCH)]
        for c in range(NCH):
            kb.op("g", lambda g_, Ic=INT[c]: g_.memset(Ic[:, 0:1], 0.0), writes=[RINT[c]])

        def intf(c):
            return INT[c][:, 1:1 + S]
        Xs_now = list(self.X)
        self.emit_ln(lambda c: Xs_now[c][:], self.RX, [None] * NCH, [None] * NCH,
                     outf_ap=intf, routf=RINT, tagp="ln1")
        for c in range(NCH):
            kb.dma(lambda q, c=c, Xc=self.X[c]: q.dma_start(self.park_main[c], Xc[:]),
                   "dmpk", reads=[self.RX[c]], writes=[kb.R("d_pm")], tag=f"parkX{c}")
            kb.dma(lambda q, c=c, Ic=INT[c]: q.dma_start(
                self.park_int[c], Ic[:, 1:1 + S]),
                "dmpi", reads=[RINT[c]], writes=[kb.R("d_pi0")], tag=f"parkI{c}")
        for c in range(NCH):
            kb.sfree(f"X{c}")

        self.l0f_layer(0, self.park_int, kb.R("d_pi0"), self.park_int2, kb.R("d_pi1"),
                       src_sb=(INT, RINT))
        self.l0f_layer(1, self.park_int2, kb.R("d_pi1"), self.park_int, kb.R("d_pi0"))

        self.X = [kb.salloc(f"X{c}", [128, S], F32) for c in range(NCH)]
        self.RX = [kb.R(f"X{c}") for c in range(NCH)]
        FIN = kb.salloc("l0fFIN", [128, S], F32)
        RFIN = kb.R("l0fFIN")
        for c in range(NCH):
            kb.dma(lambda q, c=c, Xc=self.X[c]: q.dma_start(Xc[:], self.park_main[c]),
                   "dmpk", reads=[kb.R("d_pm")], writes=[self.RX[c]], tag=f"unparkX{c}")
            kb.dma(lambda q, c=c: q.dma_start(FIN[:], self.park_int[c]),
                   "dmpi", reads=[kb.R("d_pi0")], writes=[RFIN], tag=f"unparkF{c}")
            kb.op("v", lambda v, Xc=self.X[c]: v.tensor_tensor(
                Xc[:], Xc[:], FIN[:], ALU.add),
                reads=[self.RX[c], RFIN], writes=[self.RX[c]])
        kb.sfree("l0fFIN")

    def l0f_layer(self, layer, park_in, Rpin, park_out, Rpout, src_sb=None):
        kb, nc, dr, pA, pB = self.kb, self.nc, self.dr, self.pA, self.pB
        qsb, Rqsb = self.bias["l0f_qs"]
        ksb, Rksb = self.bias["l0f_ks"]
        qsbw, Rqsbw = self.bias["l0f_qs_sw"]
        ksbw, Rksbw = self.bias["l0f_ks_sw"]

        X0B = [kb.salloc(f"X0B{c}", [128, 2052], BF16) for c in range(NCH)]
        RX0B = [kb.R(f"X0B{c}") for c in range(NCH)]
        if src_sb is not None:
            INTs, RINTs = src_sb
            for c in range(NCH):
                kb.op("g", lambda g_, Bc=X0B[c]: g_.memset(Bc[:, 0:1], 0.0),
                      writes=[RX0B[c]])
                kb.op("s", lambda s, Bc=X0B[c], Ic=INTs[c]: s.copy(
                    Bc[:, 1:1 + S], Ic[:, 1:1 + S]),
                    reads=[RINTs[c]], writes=[RX0B[c]])
            for c in range(NCH):
                kb.sfree(f"INT{c}")
        else:
            XLD = kb.salloc("l0fXLD", [128, S], F32)
            RXLD = kb.R("l0fXLD")
            for c in range(NCH):
                kb.op("g", lambda g_, Bc=X0B[c]: g_.memset(Bc[:, 0:1], 0.0),
                      writes=[RX0B[c]])
                kb.dma(lambda q, c=c: q.dma_start(XLD[:], park_in[c]), "dmxb",
                       reads=[Rpin, RXLD], writes=[RXLD], tag=f"x0bld{c}")
                kb.op("s", lambda s, Bc=X0B[c]: s.copy(Bc[:, 1:1 + S], XLD[:]),
                      reads=[RXLD], writes=[RX0B[c]])
            kb.sfree("l0fXLD")

        wst = [kb.salloc(f"wst{i}", [128, 8 * 256], F32) for i in range(2)]
        Rwst = [kb.R(f"wst{i}") for i in range(2)]
        wsb = [kb.salloc(f"wsb{i}", [128, 8 * 256], BF16) for i in range(2)]
        Rwsb = [kb.R(f"wsb{i}") for i in range(2)]
        QP = [kb.salloc(f"l0fQ{c}", [128, S], BF16) for c in range(4)]
        RQP = [kb.R(f"l0fQ{c}") for c in range(4)]
        KP = [kb.salloc(f"l0fK{c}", [128, S], BF16) for c in range(4)]
        RKP = [kb.R(f"l0fK{c}") for c in range(4)]
        VT = [kb.salloc(f"l0fV{tt}", [128, 520], BF16) for tt in range(16)]
        RVT = [kb.R(f"l0fV{tt}") for tt in range(16)]
        for tt in range(16):
            kb.op("g", lambda g_, tt=tt: g_.memset(
                VT[tt][:].rearrange("p (g c) -> p g c", c=65)[:, :, 64:65], 1.0),
                writes=[RVT[tt]])

        def load_wqkv_block(colbase, ncols, i):
            kb.dma(lambda q, colbase=colbase, ncols=ncols, i=i: q.dma_start(
                wst[i][:, 0:8 * ncols].rearrange("p (c n) -> p c n", c=8),
                dr["l0f_wqkv"][layer].rearrange("(c p) n -> p c n", p=128)
                [:, :, colbase:colbase + ncols]),
                f"dmw1q{i}", reads=[Rwst[i]], writes=[Rwst[i]])
            if i == 0:
                kb.op("s", lambda s_, ncols=ncols, i=i: s_.copy(
                    wsb[i][:, 0:8 * ncols], wst[i][:, 0:8 * ncols]),
                    reads=[Rwst[i]], writes=[Rwsb[i]])
            else:
                kb.op("v", lambda v, ncols=ncols, i=i: v.tensor_copy(
                    wsb[i][:, 0:8 * ncols], wst[i][:, 0:8 * ncols]),
                    reads=[Rwst[i]], writes=[Rwsb[i]])

        def qkv_mm(oc_in_block, ncols, pT, prT, i):
            for c in range(NCH):
                off = 1 if c < 4 else 0
                for nb in range(4):
                    kb.op("t", lambda t, c=c, nb=nb, off=off, oc_in_block=oc_in_block,
                          ncols=ncols, pT=pT, prT=prT, i=i: t.matmul(
                        pT[:, 512 * nb:512 * (nb + 1)],
                        wsb[i][:, c * ncols + 128 * oc_in_block:
                               c * ncols + 128 * (oc_in_block + 1)],
                        X0B[c][:, off + 512 * nb: off + 512 * nb + 512],
                        start=(c == 0), stop=(c == NCH - 1)),
                        reads=[Rwsb[i], RX0B[c]],
                        writes=prT(512 * nb, 512 * nb + 512))

        QRAW, RQRAW = self.SC2, self.RSC2
        QSW, RQSW = self.SC3, self.RSC3
        tabA = kb.salloc("tabA", [128, S], F32)
        tabB = kb.salloc("tabB", [128, S], F32)
        RtabA, RtabB = kb.R("tabA"), kb.R("tabB")

        def rope_side(side):
            Anm = "xposAq" if side == 0 else "xposAk"
            Bnm = "xposBq" if side == 0 else "xposBk"
            kb.dma(lambda q, Anm=Anm: q.dma_start(tabA[:], dr[Anm]), "dmtab",
                   reads=[RtabA], writes=[RtabA])
            kb.dma(lambda q, Bnm=Bnm: q.dma_start(tabB[:], dr[Bnm]), "dmtab",
                   reads=[RtabB], writes=[RtabB])
            scol = qsb if side == 0 else ksb
            Rscol = Rqsb if side == 0 else Rksb
            scolw = qsbw if side == 0 else ksbw
            Rscolw = Rqsbw if side == 0 else Rksbw
            OUT = QP if side == 0 else KP
            ROUT = RQP if side == 0 else RKP
            for blk in range(2):
                colbase = side * 512 + blk * 256
                wi = (2 * side + blk) % 2
                load_wqkv_block(colbase, 256, wi)
                for oc2 in range(2):
                    ch = blk * 2 + oc2
                    par = (2 * blk + oc2 + 2 * side) % 2
                    pQ = pA if par == 0 else pB
                    prQ = self.prA if par == 0 else self.prB
                    pN = pB if par == 0 else pA
                    prN = self.prB if par == 0 else self.prA
                    qkv_mm(oc2, 256, pQ, prQ, wi)
                    kb.op("s", lambda s, pQ=pQ, prQ=prQ: s.copy(QRAW[:], pQ[:, 0:S]),
                          reads=prQ(0, S), writes=[RQRAW])
                    kb.op("s", lambda s, pQ=pQ, prQ=prQ: s.activation(
                        self.SCa[:], pQ[:, 0:S], AF.Square),
                        reads=prQ(0, S), writes=[self.RSCa])
                    for nb in range(4):
                        kb.op("t", lambda t, nb=nb, pN=pN, prN=prN: t.matmul(
                            pN[:, 512 * nb:512 * (nb + 1)], self.c_blkb[:],
                            self.SCa[:, 512 * nb:512 * (nb + 1)], start=True, stop=True),
                            reads=[self.RSCa, self.Rblkb],
                            writes=prN(512 * nb, 512 * nb + 512))
                    kb.op("v", lambda v, pN=pN, prN=prN: v.tensor_scalar(
                        self.SC1[:], pN[:, 0:S], 5.4e-20, None, ALU.max),
                        reads=prN(0, S), writes=[self.RSC1])
                    kb.op("s", lambda s: s.activation(
                        self.SC1[:], self.SC1[:], AF.Ln),
                        reads=[self.RSC1], writes=[self.RSC1])
                    kb.op("s", lambda s: s.activation(
                        self.SC1[:], self.SC1[:], AF.Exp, scale=-0.5),
                        reads=[self.RSC1], writes=[self.RSC1])
                    for b0, b1 in ((0, 32), (32, 0), (64, 96), (96, 64)):
                        kb.dma(lambda q, b0=b0, b1=b1: q.dma_start(
                            QSW[b0:b0 + 32, :], QRAW[b1:b1 + 32, :]), "dmsw",
                            reads=[RQRAW, RQSW], writes=[RQSW])
                    kb.op("v", lambda v, scol=scol: v.scalar_tensor_tensor(
                        QRAW[:], QRAW[:], scol[:, layer:layer + 1], tabA[:],
                        ALU.mult, ALU.mult),
                        reads=[RQRAW, Rscol, RtabA], writes=[RQRAW])
                    kb.op("v", lambda v, scolw=scolw: v.scalar_tensor_tensor(
                        QSW[:], QSW[:], scolw[:, layer:layer + 1], tabB[:],
                        ALU.mult, ALU.mult),
                        reads=[RQSW, Rscolw, RtabB], writes=[RQSW])
                    kb.op("g", lambda g_: g_.tensor_tensor(
                        self.SCb[:], QRAW[:], QSW[:], ALU.add),
                        reads=[RQRAW, RQSW], writes=[self.RSCb])
                    kb.op("v", lambda v, OUT=OUT, ch=ch: v.tensor_tensor(
                        OUT[ch][:], self.SCb[:], self.SC1[:], ALU.mult),
                        reads=[self.RSCb, self.RSC1], writes=[ROUT[ch]])

        rope_side(0)
        rope_side(1)
        if layer == 0:
            self.dbg("l0fq0", QP[0], RQP[0])
            self.dbg("l0fk0", KP[0], RKP[0])

        for half in range(2):
            load_wqkv_block(1024 + 256 * half, 256, half)
            for tt in range(16):
                col0 = 512 * (tt % 4)
                for c in range(NCH):
                    off = 1 if c < 4 else 0
                    kb.op("t", lambda t, tt=tt, c=c, off=off, col0=col0, half=half:
                          t.matmul(
                        pA[:, col0:col0 + 256],
                        X0B[c][:, off + 128 * tt: off + 128 * tt + 128],
                        wsb[half][:, 256 * c:256 * (c + 1)],
                        start=(c == 0), stop=(c == NCH - 1)),
                        reads=[RX0B[c], Rwsb[half]], writes=self.prA(col0, col0 + 256))
                kb.op("v", lambda v, tt=tt, col0=col0, half=half: v.tensor_copy(
                    VT[tt][:].rearrange("p (g c) -> p g c", c=65)
                    [:, 4 * half:4 * (half + 1), 0:64],
                    pA[:, col0:col0 + 256].rearrange("p (g c) -> p g c", c=64)),
                    reads=self.prA(col0, col0 + 256), writes=[RVT[tt]])
        for i in range(2):
            kb.sfree(f"wst{i}")
            kb.sfree(f"wsb{i}")
        kb.sfree("tabA"); kb.sfree("tabB")
        for c in range(NCH):
            kb.sfree(f"X0B{c}")

        mstg = kb.salloc("l0fmstg", [128, 1024], F32)
        Rmstg = kb.R("l0fmstg")
        M1 = [kb.salloc(f"M1_{p}", [128, 512], BF16) for p in range(8)]
        RM1 = [kb.R(f"M1_{p}") for p in range(8)]
        for p in range(8):
            kb.dma(lambda q, p=p: q.dma_start(mstg[:, 0:512], dr[f"mask1_{p}"]), "dmmask",
                   reads=[Rmstg], writes=[Rmstg])
            kb.op("v", lambda v, p=p: v.tensor_copy(M1[p][:], mstg[:, 0:512]),
                  reads=[Rmstg], writes=[RM1[p]])
        MU = kb.salloc("MU", [128, 128], F32)
        RMU = kb.R("MU")
        MD = kb.salloc("MD", [128, 128], F32)
        RMD = kb.R("MD")
        kb.dma(lambda q: q.dma_start(MU[:], dr["mask1U"]), "dmmask", writes=[RMU])
        kb.dma(lambda q: q.dma_start(MD[:], dr["mask1D"]), "dmmask", writes=[RMD])
        kb.sfree("l0fmstg")

        ATT = [kb.salloc(f"l0fA{c}", [128, S], BF16) for c in range(4)]
        RATT = [kb.R(f"l0fA{c}") for c in range(4)]
        PT = [kb.salloc(f"l0fPT{i}", [128, 512], B16R) for i in range(2)]
        RPT = [kb.R(f"l0fPT{i}") for i in range(2)]
        TMP64 = kb.salloc("l0fT64", [64, 512], BF16)
        RTMP64 = kb.R("l0fT64")
        BCS = kb.salloc("l0fBCS", [64, 512], F32)
        RBCS = kb.R("l0fBCS")
        SCbb = kb.salloc("SCbb", [128, 256], BF16)
        RSCbb = kb.R("SCbb")
        MRB = kb.salloc("MRB", [128, 512], BF16)
        RMRB = kb.R("MRB")
        kb.op("g", lambda g_: g_.memset(MRB[:], 0.0), writes=[RMRB])

        MCOLB = kb.salloc("MCOLB", [128, 4], F32)
        RMCOLB = kb.R("MCOLB")
        iters = [(h, w) for h in range(8) for w in range(4)]

        def emit_spass(it2):
            h, w = iters[it2]
            ch = h // 2
            base = 64 * (h % 2)
            MC = self.MCOL if it2 % 2 == 0 else MCOLB
            RMC = self.RMCOL if it2 % 2 == 0 else RMCOLB
            for qs4 in range(4):
                q0 = 512 * w + 128 * qs4
                soff = 1024 * (qs4 % 2)
                sbanks = [self.PBANK[2 * (qs4 % 2)], self.PBANK[2 * (qs4 % 2) + 1]]
                if w == 0:
                    blocks = [(128 * kc, 128 * kc) for kc in range(qs4 + 1)]
                    has_lower = False
                else:
                    blocks = [(512 * (w - 1) + 128 * (qs4 + j), 128 * j)
                              for j in range(5)]
                    has_lower = True
                for (ctx0, dcol) in blocks:
                    kb.op("t", lambda t, ch=ch, base=base, q0=q0, ctx0=ctx0,
                          dcol=dcol, soff=soff: t.matmul(
                        pA[:, soff + dcol:soff + dcol + 128],
                        QP[ch][base:base + 64, q0:q0 + 128],
                        KP[ch][base:base + 64, ctx0:ctx0 + 128],
                        start=True, stop=True),
                        reads=[RQP[ch], RKP[ch]], writes=sbanks)
                upcol = soff + blocks[-1][1]
                mid_lo = soff + (128 if has_lower else 0)
                mid_hi = soff + blocks[-1][1]
                have_mid = mid_hi > mid_lo
                if have_mid:
                    kb.op("v", lambda v, qs4=qs4, mid_lo=mid_lo, mid_hi=mid_hi,
                          MC=MC: v.tensor_scalar(
                              self.SCb[:, 0:mid_hi - mid_lo], pA[:, mid_lo:mid_hi],
                              0.0, -3.0e38, ALU.add, ALU.max,
                              accum_out=MC[:, qs4:qs4 + 1]),
                          reads=sbanks, writes=[self.RSCb, RMC])
                kb.op("v", lambda v, upcol=upcol: v.tensor_tensor(
                    SCbb[:, 0:128], pA[:, upcol:upcol + 128], MU[:], ALU.add),
                    reads=sbanks + [RMU], writes=[RSCbb])
                nbd = 128
                if has_lower:
                    kb.op("v", lambda v, soff=soff: v.tensor_tensor(
                        SCbb[:, 128:256], pA[:, soff:soff + 128], MD[:], ALU.add),
                        reads=sbanks + [RMD], writes=[RSCbb])
                    nbd = 256
                if have_mid:
                    kb.op("v", lambda v, qs4=qs4, nbd=nbd, MC=MC: v.tensor_scalar(
                        self.SCb[:, 0:nbd], SCbb[:, 0:nbd], 0.0,
                        MC[:, qs4:qs4 + 1], ALU.add, ALU.max,
                        accum_out=MC[:, qs4:qs4 + 1]),
                        reads=[RSCbb, RMC], writes=[self.RSCb, RMC])
                else:
                    kb.op("v", lambda v, qs4=qs4, nbd=nbd, MC=MC: v.tensor_scalar(
                        self.SCb[:, 0:nbd], SCbb[:, 0:nbd], 0.0, -3.0e38,
                        ALU.add, ALU.max,
                        accum_out=MC[:, qs4:qs4 + 1]),
                        reads=[RSCbb], writes=[self.RSCb, RMC])

        emit_spass(0)
        for it in range(len(iters)):
            h, w = iters[it]
            ch = h // 2
            base = 64 * (h % 2)
            par = it % 2
            acc0 = 512 * par
            bc0 = 512 * (1 - par)
            MC = self.MCOL if it % 2 == 0 else MCOLB
            RMC = self.RMCOL if it % 2 == 0 else RMCOLB
            if it + 1 < len(iters):
                emit_spass(it + 1)
            for qs4 in range(4):
                kb.op("t", lambda t, qs4=qs4, MC=MC: t.transpose(
                    pB[0:1, 1024 + 128 * qs4:1024 + 128 * (qs4 + 1)],
                    MC[:, qs4:qs4 + 1], self.c_ident[:]),
                    reads=[RMC, self.Rid], writes=[self.PBANK[6]])
            kb.op("s", lambda s: s.activation(MRB[0:1, :], pB[0:1, 1024:1536],
                                              AF.Copy, scale=-1.0),
                  reads=[self.PBANK[6]], writes=[RMRB])
            kcs = list(range(8)) if w > 0 else list(range(4, 8))
            pend_pv = None
            for kc in kcs:
                i = kc % 2
                ctx0 = 512 * (w - 1) + 128 * kc
                pt_cols = 1024 + 512 * i
                pt_bank = [self.PBANK[6 + i]]
                kb.op("t", lambda t, ch=ch, base=base, ctx0=ctx0, w=w,
                      pt_cols=pt_cols, pt_bank=pt_bank: t.matmul(
                    pB[:, pt_cols:pt_cols + 512],
                    KP[ch][base:base + 64, ctx0:ctx0 + 128],
                    QP[ch][base:base + 64, 512 * w:512 * (w + 1)],
                    start=True, stop=False),
                    reads=[RKP[ch], RQP[ch]], writes=pt_bank)
                kb.op("t", lambda t, pt_cols=pt_cols, pt_bank=pt_bank: t.matmul(
                    pB[:, pt_cols:pt_cols + 512], self.c_e0b[:, 0:128], MRB[:],
                    start=False, stop=True),
                    reads=[self.Re0b, RMRB], writes=pt_bank)
                kb.op("s", lambda s, i=i, pt_cols=pt_cols, pt_bank=pt_bank:
                      s.activation(PT[i][:], pB[:, pt_cols:pt_cols + 512], AF.Exp),
                      reads=pt_bank, writes=[RPT[i]])
                meng = "v" if i == 0 else "g"
                kb.op(meng, lambda e, i=i, kc=kc: e.tensor_tensor(
                    PT[i][:], PT[i][:], M1[kc][:], ALU.mult),
                    reads=[RPT[i], RM1[kc]], writes=[RPT[i]])
                tt = ((512 * (w - 1) + 128 * kc) // 128)
                if pend_pv is not None:
                    pend_pv()
                pend_pv = (lambda i=i, tt=tt, h=h, kc=kc, kcs=kcs, acc0=acc0,
                           par=par: kb.op("t", lambda t, i=i, tt=tt, h=h, kc=kc,
                                          kcs=kcs, acc0=acc0, par=par: t.matmul(
                               pB[0:65, acc0:acc0 + 512],
                               VT[tt][:, 65 * h:65 * (h + 1)],
                               PT[i][:], start=(kc == kcs[0]), stop=(kc == kcs[-1])),
                               reads=[RVT[tt], RPT[i]],
                               writes=[self.PBANK[4 + par]]))
            pend_pv()
            zr = self.ROW[0:1, 0:512]
            kb.op("v", lambda v, acc0=acc0, par=par: v.tensor_copy(
                zr, pB[64:65, acc0:acc0 + 512]),
                reads=[self.PBANK[4 + par]], writes=[self.RROW])
            kb.op("s", lambda s: s.activation(zr, zr, AF.Ln),
                  reads=[self.RROW], writes=[self.RROW])
            kb.op("s", lambda s: s.activation(zr, zr, AF.Exp, scale=-1.0),
                  reads=[self.RROW], writes=[self.RROW])
            kb.op("t", lambda t, bc0=bc0, par=par: t.matmul(
                pB[0:64, bc0:bc0 + 512], self.c_e0[:, 0:64], self.ROWT[:, 0:512],
                start=True, stop=True),
                reads=[self.RROW, self.Re0], writes=[self.PBANK[5 - par]])
            kb.op("s", lambda s, bc0=bc0: s.copy(BCS[:], pB[0:64, bc0:bc0 + 512]),
                  reads=[self.PBANK[5 - par]], writes=[RBCS])
            kb.op("v", lambda v, acc0=acc0, par=par: v.tensor_tensor(
                TMP64[:], pB[0:64, acc0:acc0 + 512], BCS[:], ALU.mult),
                reads=[self.PBANK[4 + par], RBCS], writes=[RTMP64])
            kb.op("v", lambda v, ch=ch, base=base, w=w: v.tensor_copy(
                ATT[ch][base:base + 64, 512 * w:512 * (w + 1)], TMP64[:]),
                reads=[RTMP64], writes=[RATT[ch]])

        if layer == 0:
            self.dbg("l0fatt0", ATT[0], RATT[0])
            self.dbg("l0fvt0", VT[0], RVT[0], cols=520)
        for c in range(4):
            kb.sfree(f"l0fQ{c}")
            kb.sfree(f"l0fK{c}")
        for tt in range(16):
            kb.sfree(f"l0fV{tt}")
        for i in range(2):
            kb.sfree(f"l0fPT{i}")
        for p in range(8):
            kb.sfree(f"M1_{p}")
        kb.sfree("MU"); kb.sfree("MD"); kb.sfree("SCbb"); kb.sfree("MCOLB")
        kb.sfree("l0fT64"); kb.sfree("l0fBCS"); kb.sfree("MRB")

        INT2 = [kb.salloc(f"INT2_{c}", [128, 2052], F32) for c in range(NCH)]
        RINT2 = [kb.R(f"INT2_{c}") for c in range(NCH)]
        wof = kb.salloc("l0fwof", [128, E], F32)
        Rwof = kb.R("l0fwof")
        wob = [kb.salloc(f"l0fwob{c}", [128, E], BF16) for c in range(4)]
        Rwob = [kb.R(f"l0fwob{c}") for c in range(4)]
        for c in range(4):
            kb.dma(lambda q, c=c: q.dma_start(
                wof[:], dr["l0f_wo"][layer][128 * c:128 * (c + 1), :]), "dmwo",
                reads=[Rwof], writes=[Rwof])
            kb.op("v", lambda v, c=c: v.tensor_copy(wob[c][:], wof[:]),
                  reads=[Rwof], writes=[Rwob[c]])
        for oc in range(8):
            kb.op("g", lambda g_, Ic=INT2[oc]: g_.memset(Ic[:, 0:1], 0.0),
                  writes=[RINT2[oc]])
            kb.dma(lambda q, oc=oc, Ic=INT2[oc]: q.dma_start(
                Ic[:, 1:1 + S], park_in[oc]), "dmxl",
                reads=[Rpin], writes=[RINT2[oc]], tag=f"unpkwo{oc}")
            for c in range(4):
                for nb in range(4):
                    kb.op("t", lambda t, oc=oc, c=c, nb=nb: t.matmul(
                        pA[:, 512 * nb:512 * (nb + 1)],
                        wob[c][:, 128 * oc:128 * (oc + 1)],
                        ATT[c][:, 512 * nb:512 * (nb + 1)],
                        start=(c == 0), stop=(c == 3)),
                        reads=[Rwob[c], RATT[c]],
                        writes=self.prA(512 * nb, 512 * nb + 512))
            kb.op("v", lambda v, Ic=INT2[oc]: v.tensor_tensor(
                Ic[:, 1:1 + S], Ic[:, 1:1 + S], pA[:, 0:S], ALU.add),
                reads=[RINT2[oc]] + self.prA(0, S), writes=[RINT2[oc]])
        kb.sfree("l0fwof")
        for c in range(4):
            kb.sfree(f"l0fwob{c}")
        for c in range(4):
            kb.sfree(f"l0fA{c}")

        LN2 = [kb.salloc(f"LN2_{c}", [128, S], BF16) for c in range(NCH)]
        RLN2 = [kb.R(f"LN2_{c}") for c in range(NCH)]
        lng, Rlng = self.bias["l0f_lng"]
        lnb, Rlnb = self.bias["l0f_lnb"]

        if layer == 0:
            self.dbg("l0fint0", INT2[0], RINT2[0], cols=2052)

        def srcap(c):
            off = 0 if c >= 4 else 1
            return INT2[c][:, off:off + S]
        self.emit_ln(srcap, RINT2, LN2, RLN2,
                     gb=(lambda c: lng[:, 8 * layer + c:8 * layer + c + 1],
                         lambda c: lnb[:, 8 * layer + c:8 * layer + c + 1],
                         Rlng), tagp=f"ln2_{layer}")
        for c in range(NCH):
            kb.dma(lambda q, c=c, Ic=INT2[c]: q.dma_start(
                park_out[c], Ic[:, 1:1 + S]),
                "dmpi", reads=[RINT2[c]], writes=[Rpout], tag=f"parkO{c}")
        for c in range(NCH):
            kb.sfree(f"INT2_{c}")

        GCOLS = [1024, 1024, 682]
        GOFF = [0, 1024, 2048]
        w1f = [kb.salloc(f"w1f{i}", [128, 1024], F32) for i in range(2)]
        Rw1f = [kb.R(f"w1f{i}") for i in range(2)]
        W1A = [[kb.salloc(f"W1A{g}_{c}", [128, GCOLS[g]], BF16) for c in range(NCH)]
               for g in range(3)]
        RW1A = [[kb.R(f"W1A{g}_{c}") for c in range(NCH)] for g in range(3)]
        W1G = [[kb.salloc(f"W1G{g}_{c}", [128, GCOLS[g]], BF16) for c in range(NCH)]
               for g in range(3)]
        RW1G = [[kb.R(f"W1G{g}_{c}") for c in range(NCH)] for g in range(3)]
        ldi = 0
        for g in range(3):
            for nmh, DSTg, RDSTg in (("l0f_w1a", W1A, RW1A),
                                     ("l0f_w1g", W1G, RW1G)):
                for c in range(NCH):
                    i = ldi % 2
                    ldi += 1
                    kb.dma(lambda q, c=c, i=i, g=g, nmh=nmh: q.dma_start(
                        w1f[i][:, 0:GCOLS[g]],
                        dr[nmh][layer][128 * c:128 * (c + 1),
                                       GOFF[g]:GOFF[g] + GCOLS[g]]),
                        f"dmw1a{i}", reads=[Rw1f[i]], writes=[Rw1f[i]])
                    if i == 0:
                        kb.op("s", lambda s_, i=i, g=g, D=DSTg[g][c]: s_.copy(
                            D[:], w1f[i][:, 0:GCOLS[g]]),
                            reads=[Rw1f[i]], writes=[RDSTg[g][c]])
                    else:
                        kb.op("v", lambda v, i=i, g=g, D=DSTg[g][c]: v.tensor_copy(
                            D[:], w1f[i][:, 0:GCOLS[g]]),
                            reads=[Rw1f[i]], writes=[RDSTg[g][c]])
        for i in range(2):
            kb.sfree(f"w1f{i}")

        GT = [kb.salloc(f"GT{j}", [128, 512], BF16) for j in range(22)]
        RGT = [kb.R(f"GT{j}") for j in range(22)]
        kb.op("g", lambda g_: g_.memset(GT[21][:], 0.0), writes=[RGT[21]])
        w2f = [kb.salloc(f"w2f{i}", [128, 512], F32) for i in range(2)]
        Rw2f = [kb.R(f"w2f{i}") for i in range(2)]
        w2b = [kb.salloc(f"w2b{i}", [128, E], BF16) for i in range(2)]
        Rw2b = [kb.R(f"w2b{i}") for i in range(2)]
        PINB = self.SC1[:, 0:512]
        RPINB = self.RSC1
        OUTB = self.SC2[:, 0:512]
        ROUTB = self.RSC2
        b1a, Rb1a = self.bias["l0f_b1a"]
        b1g, Rb1g = self.bias["l0f_b1g"]
        b2c, Rb2 = self.bias["l0f_b2"]

        for Tt in range(4):
            t0 = 512 * Tt
            for j in range(22):
                m = 128 if j < 21 else 42
                g = j // 8
                jo = 128 * (j - 8 * g)
                pcol = 1024 * (j % 2)
                bkA, bkG = self.PBANK[2 * (j % 2)], self.PBANK[2 * (j % 2) + 1]
                for c in range(NCH):
                    kb.op("t", lambda t, j=j, c=c, m=m, t0=t0, pcol=pcol, g=g, jo=jo:
                          t.matmul(
                        pA[0:m, pcol:pcol + 512], W1A[g][c][:, jo:jo + m],
                        LN2[c][:, t0:t0 + 512], start=(c == 0), stop=(c == NCH - 1)),
                        reads=[RW1A[g][c], RLN2[c]], writes=[bkA])
                for c in range(NCH):
                    kb.op("t", lambda t, j=j, c=c, m=m, t0=t0, pcol=pcol, g=g, jo=jo:
                          t.matmul(
                        pA[0:m, pcol + 512:pcol + 1024], W1G[g][c][:, jo:jo + m],
                        LN2[c][:, t0:t0 + 512], start=(c == 0), stop=(c == NCH - 1)),
                        reads=[RW1G[g][c], RLN2[c]], writes=[bkG])
                kb.op("s", lambda s, j=j, m=m, pcol=pcol: s.activation(
                    self.SCa[0:m, 0:512], pA[0:m, pcol + 512:pcol + 1024], AF.Gelu,
                    bias=b1g[0:m, 22 * layer + j:22 * layer + j + 1]),
                    reads=[bkG, Rb1g], writes=[self.RSCa])
                kb.op("v", lambda v, j=j, m=m, pcol=pcol: v.scalar_tensor_tensor(
                    GT[j][0:m, :], pA[0:m, pcol:pcol + 512],
                    b1a[0:m, 22 * layer + j:22 * layer + j + 1],
                    self.SCa[0:m, 0:512], ALU.add, ALU.mult),
                    reads=[bkA, Rb1a, self.RSCa], writes=[RGT[j]])

            for c22 in range(22):
                m = 128 if c22 < 21 else 42
                i = c22 % 2
                for hh in range(2):
                    kb.dma(lambda q, c22=c22, m=m, hh=hh: q.dma_start(
                        w2f[hh][0:m, :],
                        dr["l0f_w2"][layer][128 * c22:128 * c22 + m,
                                            512 * hh:512 * (hh + 1)]),
                        f"dmw2{hh}", reads=[Rw2f[hh]], writes=[Rw2f[hh]])
                    kb.op("s", lambda s_, m=m, i=i, hh=hh: s_.copy(
                        w2b[i][0:m, 512 * hh:512 * (hh + 1)], w2f[hh][0:m, :]),
                        reads=[Rw2f[hh]], writes=[Rw2b[i]])
                for oc in range(8):
                    pt = pA if oc < 4 else pB
                    prf = self.prA if oc < 4 else self.prB
                    col0 = 512 * (oc % 4)
                    kb.op("t", lambda t, c22=c22, oc=oc, m=m, pt=pt, col0=col0, i=i:
                          t.matmul(pt[:, col0:col0 + 512],
                                   w2b[i][0:m, 128 * oc:128 * (oc + 1)],
                                   GT[c22][0:m, :], start=(c22 == 0), stop=(c22 == 21)),
                          reads=[Rw2b[i], RGT[c22]], writes=prf(col0, col0 + 512))
            for oc in range(8):
                kb.dma(lambda q, oc=oc, t0=t0: q.dma_start(
                    PINB[:], park_out[oc][:, t0:t0 + 512]), "dmfi",
                    reads=[Rpout, RPINB], writes=[RPINB], tag=f"ffnin{oc}_{Tt}")
                pt = pA if oc < 4 else pB
                prf = self.prA if oc < 4 else self.prB
                col0 = 512 * (oc % 4)
                kb.op("v", lambda v, oc=oc, pt=pt, col0=col0: v.scalar_tensor_tensor(
                    OUTB[:], pt[:, col0:col0 + 512],
                    b2c[:, 8 * layer + oc:8 * layer + oc + 1],
                    PINB[:], ALU.add, ALU.add),
                    reads=prf(col0, col0 + 512) + [Rb2, RPINB], writes=[ROUTB])
                kb.dma(lambda q, oc=oc, t0=t0: q.dma_start(
                    park_out[oc][:, t0:t0 + 512], OUTB[:]), "dmpi",
                    reads=[ROUTB], writes=[Rpout], tag=f"ffnout{oc}_{Tt}")
        for j in range(22):
            kb.sfree(f"GT{j}")
        for g in range(3):
            for c in range(NCH):
                kb.sfree(f"W1A{g}_{c}")
                kb.sfree(f"W1G{g}_{c}")
        for i in range(2):
            kb.sfree(f"w2f{i}")
            kb.sfree(f"w2b{i}")

        for c in range(NCH):
            kb.sfree(f"LN2_{c}")

    def blk_l1_attn(self):
        kb, nc, dr, pA, pB = self.kb, self.nc, self.dr, self.pA, self.pB
        LNO = [kb.salloc(f"cLN{c}", [128, S], BF16) for c in range(NCH)]
        RLNO = [kb.R(f"cLN{c}") for c in range(NCH)]
        Xs_now = list(self.X)
        self.emit_ln(lambda c: Xs_now[c][:], self.RX, LNO, RLNO, tagp="ln2m")
        for c in range(NCH):
            kb.dma(lambda q, c=c, Xc=self.X[c]: q.dma_start(self.park_main[c], Xc[:]),
                   "dmpk", reads=[self.RX[c]], writes=[kb.R("d_pm")], tag=f"parkX2{c}")
        for c in range(NCH):
            kb.sfree(f"X{c}")

        wf = [kb.salloc(f"l1wf{i}", [128, E], F32) for i in range(2)]
        Rwf = [kb.R(f"l1wf{i}") for i in range(2)]
        wb = [kb.salloc(f"l1wb{c}", [128, E], BF16) for c in range(NCH)]
        Rwb = [kb.R(f"l1wb{c}") for c in range(NCH)]

        def loadW(nm):
            for c in range(NCH):
                i = c % 2
                kb.dma(lambda q, c=c, nm=nm, wfx=wf[i], i=i: q.dma_start(
                    wfx[:], dr[nm][128 * c:128 * (c + 1), :]), f"dmw3{i}",
                    reads=[Rwf[i]], writes=[Rwf[i]])
                kb.op("g", lambda g_, wfx=wf[i], wbx=wb[c], i=i: g_.tensor_copy(
                    wbx[:], wfx[:]),
                    reads=[Rwf[i]], writes=[Rwb[c]])

        Q = [kb.salloc(f"l1Q{c}", [128, S], BF16) for c in range(NCH)]
        RQ = [kb.R(f"l1Q{c}") for c in range(NCH)]
        K = [kb.salloc(f"l1K{c}", [128, S], BF16) for c in range(NCH)]
        RK = [kb.R(f"l1K{c}") for c in range(NCH)]
        VT = [kb.salloc(f"l1V{tt}", [128, 1040], BF16) for tt in range(16)]
        RVT = [kb.R(f"l1V{tt}") for tt in range(16)]
        loadW("l1a_Wq")
        for oc in range(8):
            for c in range(NCH):
                for nb in range(4):
                    kb.op("t", lambda t, oc=oc, c=c, nb=nb, wbx=wb[c]: t.matmul(
                        pA[:, 512 * nb:512 * (nb + 1)], wbx[:, 128 * oc:128 * (oc + 1)],
                        LNO[c][:, 512 * nb:512 * (nb + 1)],
                        start=(c == 0), stop=(c == NCH - 1)),
                        reads=[Rwb[c], RLNO[c]], writes=self.prA(512 * nb, 512 * nb + 512))
            if oc % 2 == 0:
                kb.op("v", lambda v, oc=oc: v.tensor_copy(Q[oc][:], pA[:, 0:S]),
                      reads=self.prA(0, S), writes=[RQ[oc]])
            else:
                kb.op("s", lambda s, oc=oc: s.copy(Q[oc][:], pA[:, 0:S]),
                      reads=self.prA(0, S), writes=[RQ[oc]])
        loadW("l1a_Wk")
        for oc in range(8):
            for c in range(NCH):
                for nb in range(4):
                    kb.op("t", lambda t, oc=oc, c=c, nb=nb, wbx=wb[c]: t.matmul(
                        pB[:, 512 * nb:512 * (nb + 1)], wbx[:, 128 * oc:128 * (oc + 1)],
                        LNO[c][:, 512 * nb:512 * (nb + 1)],
                        start=(c == 0), stop=(c == NCH - 1)),
                        reads=[Rwb[c], RLNO[c]], writes=self.prB(512 * nb, 512 * nb + 512))
            if oc % 2 == 0:
                kb.op("s", lambda s, oc=oc: s.copy(K[oc][:], pB[:, 0:S]),
                      reads=self.prB(0, S), writes=[RK[oc]])
            else:
                kb.op("v", lambda v, oc=oc: v.tensor_copy(K[oc][:], pB[:, 0:S]),
                      reads=self.prB(0, S), writes=[RK[oc]])
        loadW("l1a_Wv")
        for tt in range(16):
            kb.op("g", lambda g_, tt=tt: g_.memset(
                VT[tt][:].rearrange("p (g c) -> p g c", c=65)[:, :, 64:65], 1.0),
                writes=[RVT[tt]])
            for half in range(2):
                col0 = 512 * ((2 * tt + half) % 4)
                for c in range(NCH):
                    kb.op("t", lambda t, tt=tt, c=c, half=half, col0=col0, wbx=wb[c]:
                          t.matmul(
                        pA[:, col0:col0 + 512],
                        LNO[c][:, 128 * tt:128 * (tt + 1)],
                        wbx[:, 512 * half:512 * (half + 1)],
                        start=(c == 0), stop=(c == NCH - 1)),
                        reads=[RLNO[c], Rwb[c]], writes=self.prA(col0, col0 + 512))
                kb.op("v", lambda v, tt=tt, half=half, col0=col0: v.tensor_copy(
                    VT[tt][:].rearrange("p (g c) -> p g c", c=65)
                    [:, 8 * half:8 * (half + 1), 0:64],
                    pA[:, col0:col0 + 512].rearrange("p (g c) -> p g c", c=64)),
                    reads=self.prA(col0, col0 + 512), writes=[RVT[tt]])
        for c in range(NCH):
            kb.sfree(f"cLN{c}")

        prj = kb.salloc("prj", [128, 256], BF16)
        Rprj = kb.R("prj")
        kb.dma(lambda q, wfx=wf[0]: q.dma_start(wfx[0:64, 0:256], dr["l1a_proj"]),
               "dmw30", reads=[Rwf[0]], writes=[Rwf[0]])
        kb.op("v", lambda v, wfx=wf[0]: v.tensor_copy(prj[0:64, :], wfx[0:64, 0:256]),
              reads=[Rwf[0]], writes=[Rprj])
        kb.op("v", lambda v: v.tensor_copy(prj[64:128, :], prj[0:64, :]),
              reads=[Rprj], writes=[Rprj])
        for i in range(2):
            kb.sfree(f"l1wf{i}")
        for c in range(NCH):
            kb.sfree(f"l1wb{c}")

        ATT = [kb.salloc(f"l1A{c}", [128, S], BF16) for c in range(NCH)]
        RATT = [kb.R(f"l1A{c}") for c in range(NCH)]
        QP = [kb.salloc(f"l1QP{f}", [128, S], BF16) for f in range(2)]
        RQP = [kb.R(f"l1QP{f}") for f in range(2)]
        KPT = [kb.salloc(f"l1KPT{tt}", [128, 256], BF16) for tt in range(16)]
        RKPT = [kb.R(f"l1KPT{tt}") for tt in range(16)]
        KVB = kb.salloc("l1KVB", [128, 130], BF16)
        RKVB = kb.R("l1KVB")
        BCS = self.SC1[0:64, :]
        RBCS = self.RSC1
        TMP64 = self.SCa[0:64, :]
        RTMP64 = self.RSCa

        for h in range(16):
            ch = h // 2
            base = 64 * (h % 2)
            for f2 in range(2):
                for nb in range(4):
                    kb.op("t", lambda t, ch=ch, base=base, f2=f2, nb=nb: t.matmul(
                        pA[:, 512 * nb:512 * (nb + 1)],
                        prj[base:base + 64, 128 * f2:128 * (f2 + 1)],
                        Q[ch][base:base + 64, 512 * nb:512 * (nb + 1)],
                        start=True, stop=True),
                        reads=[Rprj, RQ[ch]], writes=self.prA(512 * nb, 512 * nb + 512))
                kb.op("s", lambda s, f2=f2: s.activation(QP[f2][:], pA[:, 0:S], AF.Relu),
                      reads=self.prA(0, S), writes=[RQP[f2]])
            for tt in range(16):
                col0 = 512 * (tt % 2) + 256 * 0
                kb.op("t", lambda t, ch=ch, base=base, tt=tt, col0=col0: t.matmul(
                    pB[:, col0:col0 + 256], K[ch][base:base + 64, 128 * tt:128 * (tt + 1)],
                    prj[base:base + 64, :], start=True, stop=True),
                    reads=[RK[ch], Rprj], writes=self.prB(col0, col0 + 256))
                if tt % 2 == 0:
                    kb.op("s", lambda s, tt=tt, col0=col0: s.activation(
                        KPT[tt][:], pB[:, col0:col0 + 256], AF.Relu),
                        reads=self.prB(col0, col0 + 256), writes=[RKPT[tt]])
                else:
                    kb.op("v", lambda v, tt=tt, col0=col0: v.tensor_scalar(
                        KPT[tt][:], pB[:, col0:col0 + 256], 0.0, None, ALU.max),
                        reads=self.prB(col0, col0 + 256), writes=[RKPT[tt]])
            for f2 in range(2):
                for tt in range(16):
                    kb.op("t", lambda t, f2=f2, tt=tt, h=h: t.matmul(
                        pB[:, 1024:1024 + 65], KPT[tt][:, 128 * f2:128 * (f2 + 1)],
                        VT[tt][:, 65 * h:65 * (h + 1)],
                        start=(tt == 0), stop=(tt == 15)),
                        reads=[RKPT[tt], RVT[tt]], writes=[self.PBANK[6]])
                kb.op("v", lambda v, f2=f2: v.tensor_copy(
                    KVB[:, 65 * f2:65 * (f2 + 1)], pB[:, 1024:1024 + 65]),
                    reads=[self.PBANK[6]], writes=[RKVB])
            for f2 in range(2):
                for nb in range(4):
                    kb.op("t", lambda t, f2=f2, nb=nb: t.matmul(
                        pA[0:65, 512 * nb:512 * (nb + 1)], KVB[:, 65 * f2:65 * (f2 + 1)],
                        QP[f2][:, 512 * nb:512 * (nb + 1)],
                        start=(f2 == 0), stop=(f2 == 1)),
                        reads=[RKVB, RQP[f2]], writes=self.prA(512 * nb, 512 * nb + 512))
            nr = self.ROW[0:1, 0:2048]
            kb.op("v", lambda v: v.tensor_copy(nr, pA[64:65, 0:S]),
                  reads=self.prA(0, S), writes=[self.RROW])
            kb.op("s", lambda s: s.activation(nr, nr, AF.Ln, bias=self.c_bias[0:1, 2:3]),
                  reads=[self.RROW, self.Rcb], writes=[self.RROW])
            kb.op("s", lambda s: s.activation(nr, nr, AF.Exp, scale=-1.0),
                  reads=[self.RROW], writes=[self.RROW])
            for nb in range(4):
                kb.op("t", lambda t, nb=nb: t.matmul(
                    pB[0:64, 512 * nb:512 * (nb + 1)], self.c_e0[:, 0:64],
                    self.ROWT[:, 512 * nb:512 * (nb + 1)], start=True, stop=True),
                    reads=[self.RROW, self.Re0], writes=self.prB(512 * nb, 512 * nb + 512))
            kb.op("s", lambda s: s.copy(BCS, pB[0:64, 0:S]),
                  reads=self.prB(0, S), writes=[RBCS])
            kb.op("v", lambda v: v.tensor_tensor(TMP64, pA[0:64, 0:S], BCS, ALU.mult),
                  reads=self.prA(0, S) + [RBCS], writes=[RTMP64])
            kb.op("v", lambda v, ch=ch, base=base: v.tensor_copy(
                ATT[ch][base:base + 64, :], TMP64),
                reads=[RTMP64], writes=[RATT[ch]])

        for c in range(NCH):
            kb.sfree(f"l1Q{c}")
            kb.sfree(f"l1K{c}")
        for tt in range(16):
            kb.sfree(f"l1V{tt}")
            kb.sfree(f"l1KPT{tt}")
        for f in range(2):
            kb.sfree(f"l1QP{f}")
        kb.sfree("prj"); kb.sfree("l1KVB")

        wf = [kb.salloc(f"l1wf{i}", [128, E], F32) for i in range(2)]
        Rwf = [kb.R(f"l1wf{i}") for i in range(2)]
        wb = [kb.salloc(f"l1wb{c}", [128, E], BF16) for c in range(NCH)]
        Rwb = [kb.R(f"l1wb{c}") for c in range(NCH)]

        def loadW2(nm):
            for c in range(NCH):
                i = c % 2
                kb.dma(lambda q, c=c, nm=nm, wfx=wf[i], i=i: q.dma_start(
                    wfx[:], dr[nm][128 * c:128 * (c + 1), :]), f"dmw3{i}",
                    reads=[Rwf[i]], writes=[Rwf[i]])
                kb.op("g", lambda g_, wfx=wf[i], wbx=wb[c], i=i: g_.tensor_copy(
                    wbx[:], wfx[:]),
                    reads=[Rwf[i]], writes=[Rwb[c]])
        loadW2("l1a_Wo")
        bo, Rbo = self.bias["l1a_bo"]
        self.X = [kb.salloc(f"X{c}", [128, S], F32) for c in range(NCH)]
        self.RX = [kb.R(f"X{c}") for c in range(NCH)]
        for oc in range(8):
            pt = pA if oc % 2 == 0 else pB
            prf = self.prA if oc % 2 == 0 else self.prB
            kb.dma(lambda q, oc=oc, Xc=self.X[oc]: q.dma_start(Xc[:], self.park_main[oc]),
                   "dmpk", reads=[kb.R("d_pm")], writes=[self.RX[oc]], tag=f"unparkX2{oc}")
            for c in range(NCH):
                for nb in range(4):
                    kb.op("t", lambda t, oc=oc, c=c, nb=nb, wbx=wb[c], pt=pt, prf=prf:
                          t.matmul(
                        pt[:, 512 * nb:512 * (nb + 1)], wbx[:, 128 * oc:128 * (oc + 1)],
                        ATT[c][:, 512 * nb:512 * (nb + 1)],
                        start=(c == 0), stop=(c == NCH - 1)),
                        reads=[Rwb[c], RATT[c]], writes=prf(512 * nb, 512 * nb + 512))
            kb.op("v", lambda v, oc=oc, Xc=self.X[oc], pt=pt, prf=prf:
                  v.scalar_tensor_tensor(
                      Xc[:], pt[:, 0:S], bo[:, oc:oc + 1], Xc[:],
                      ALU.add, ALU.add),
                  reads=prf(0, S) + [Rbo, self.RX[oc]], writes=[self.RX[oc]])
        for i in range(2):
            kb.sfree(f"l1wf{i}")
        for c in range(NCH):
            kb.sfree(f"l1wb{c}")
        for c in range(NCH):
            kb.sfree(f"l1A{c}")

    def blk_l1_ffn(self):
        kb, nc, dr, pA, pB = self.kb, self.nc, self.dr, self.pA, self.pB
        sw1col, Rsw1col, iw1col, Riw1col = self._weight_absmean("l1f_w1", 32, E)
        LNF = [kb.salloc(f"LNF{c}", [128, S], F32) for c in range(NCH)]
        RLNF = [kb.R(f"LNF{c}") for c in range(NCH)]
        Xs_now = list(self.X)
        self.emit_ln(lambda c: Xs_now[c][:], self.RX, [None] * NCH, [None] * NCH,
                     outf_ap=lambda c: LNF[c][:], routf=RLNF, tagp="ln3")
        for c in range(NCH):
            kb.dma(lambda q, c=c, Xc=self.X[c]: q.dma_start(self.park_main[c], Xc[:]),
                   "dmpk", reads=[self.RX[c]], writes=[kb.R("d_pm")], tag=f"parkX3{c}")
        for c in range(NCH):
            kb.sfree(f"X{c}")

        CAND = kb.salloc("CAND", [128, S], F32)
        RCAND = kb.R("CAND")
        for c in range(NCH):
            kb.op("s", lambda s, c=c: s.activation(self.SC1[:], LNF[c][:], AF.Abs),
                  reads=[RLNF[c]], writes=[self.RSC1])
            if c == 0:
                kb.op("v", lambda v: v.tensor_copy(CAND[:], self.SC1[:]),
                      reads=[self.RSC1], writes=[RCAND])
            else:
                kb.op("v", lambda v: v.tensor_tensor(CAND[:], CAND[:], self.SC1[:], ALU.max),
                      reads=[RCAND, self.RSC1], writes=[RCAND])
        SCLBC, RSCLBC, INVBC, RINVBC = self._token_scale_rows(CAND, RCAND, 127.0, 1e-5)
        self.dbg("sclbc", SCLBC, RSCLBC)
        self.dbg("invbc", INVBC, RINVBC)
        XQ = [kb.salloc(f"XQ{c}", [128, S], BF16) for c in range(NCH)]
        RXQ = [kb.R(f"XQ{c}") for c in range(NCH)]
        X8 = [kb.salloc(f"X8_{i}", [128, S], I8) for i in range(2)]
        RX8 = [kb.R(f"X8_{i}") for i in range(2)]
        for c in range(NCH):
            j = c % 2
            scr = self.SC1 if j == 0 else self.SC4
            rscr = self.RSC1 if j == 0 else self.RSC4
            qeng = "v" if j == 0 else "g"
            kb.op(qeng, lambda e, c=c, scr=scr: e.tensor_tensor(
                scr[:], LNF[c][:], SCLBC[:], ALU.mult),
                reads=[RLNF[c], RSCLBC], writes=[rscr])
            kb.op("v", lambda v, scr=scr: v.tensor_scalar(
                scr[:], scr[:], MAGIC, -MAGIC, ALU.add, ALU.add),
                reads=[rscr], writes=[rscr])
            kb.op("v", lambda v, j=j, scr=scr: v.tensor_copy(X8[j][:], scr[:]),
                  reads=[rscr], writes=[RX8[j]])
            kb.op("v", lambda v, c=c, j=j: v.tensor_copy(XQ[c][:], X8[j][:]),
                  reads=[RX8[j]], writes=[RXQ[c]])
            self.dbg(f"xq{c}", XQ[c], RXQ[c])
        for c in range(NCH):
            kb.sfree(f"LNF{c}")
        kb.sfree("CAND")
        for i in range(2):
            kb.sfree(f"X8_{i}")
        kb.sfree("SCLBC_ts1")

        self.dbg("sw1", sw1col, Rsw1col, rows=128, cols=1)
        W1QT = [kb.salloc(f"W1QT{d}", [128, 4096], BF16) for d in range(NCH)]
        RW1QT = [kb.R(f"W1QT{d}") for d in range(NCH)]
        self._quant_transpose(dr["l1f_w1"], 32, E, sw1col, Rsw1col, W1QT, RW1QT, "qw1")
        self.dbg("w1qt0", W1QT[0], RW1QT[0], cols=4096)

        convw = kb.salloc("convw", [128, 3 * 32], F32)
        Rconvw = kb.R("convw")
        kb.dma(lambda q: q.dma_start(convw[:], dr["l1f_convw"]), "dmw4",
               writes=[Rconvw])
        convb, Rconvb = self.bias["l1f_convb"]
        H = kb.salloc("H", [128, 2050], BF16)
        RH = kb.R("H")
        kb.op("g", lambda g_: g_.memset(H[:, 0:1], 0.0), writes=[RH])
        kb.op("g", lambda g_: g_.memset(H[:, 2049:2050], 0.0), writes=[RH])
        SILF = kb.salloc("SILF", [128, S], F32)
        RSILF = kb.R("SILF")
        CAND2 = kb.salloc("CAND2", [128, S], F32)
        RCAND2 = kb.R("CAND2")
        for hc in range(32):
            pt = pA if hc % 2 == 0 else pB
            prf = self.prA if hc % 2 == 0 else self.prB
            for d in range(NCH):
                for nb in range(4):
                    kb.op("t", lambda t, hc=hc, d=d, nb=nb, pt=pt, prf=prf: t.matmul(
                        pt[:, 512 * nb:512 * (nb + 1)],
                        W1QT[d][:, 128 * hc:128 * (hc + 1)],
                        XQ[d][:, 512 * nb:512 * (nb + 1)],
                        start=(d == 0), stop=(d == NCH - 1)),
                        reads=[RW1QT[d], RXQ[d]], writes=prf(512 * nb, 512 * nb + 512))
            kb.op("v", lambda v, pt=pt, prf=prf: v.scalar_tensor_tensor(
                H[:, 1:1 + S], pt[:, 0:S], iw1col[:, 0:1], INVBC[:], ALU.mult, ALU.mult),
                reads=prf(0, S) + [Riw1col, RINVBC], writes=[RH])
            kb.op("g", lambda g_, hc=hc: g_.tensor_scalar(
                self.SCb[:], H[:, 0:S], convw[:, hc:hc + 1],
                convb[:, hc:hc + 1], ALU.mult, ALU.add),
                reads=[RH, Rconvw, Rconvb], writes=[self.RSCb])
            kb.op("v", lambda v, hc=hc: v.scalar_tensor_tensor(
                self.SCa[:], H[:, 1:1 + S], convw[:, 32 + hc:32 + hc + 1],
                self.SCb[:], ALU.mult, ALU.add),
                reads=[RH, Rconvw, self.RSCb], writes=[self.RSCa])
            kb.op("v", lambda v, hc=hc: v.scalar_tensor_tensor(
                self.SCa[:], H[:, 2:2 + S], convw[:, 64 + hc:64 + hc + 1],
                self.SCa[:], ALU.mult, ALU.add),
                reads=[RH, Rconvw, self.RSCa], writes=[self.RSCa])
            kb.op("s", lambda s: s.activation(SILF[:], self.SCa[:], AF.Silu),
                  reads=[self.RSCa], writes=[RSILF])
            if hc == 0:
                self.dbg("h0", H, RH, cols=2050)
                self.dbg("silf0", SILF, RSILF)
            kb.dma(lambda q, hc=hc: q.dma_start(self.park_silu[hc], SILF[:]),
                   "dmps", reads=[RSILF], writes=[kb.R("d_ps")], tag=f"parksil{hc}")
            kb.op("s", lambda s: s.activation(self.SC1[:], SILF[:], AF.Abs),
                  reads=[RSILF], writes=[self.RSC1])
            if hc == 0:
                kb.op("v", lambda v: v.tensor_copy(CAND2[:], self.SC1[:]),
                      reads=[self.RSC1], writes=[RCAND2])
            else:
                kb.op("v", lambda v: v.tensor_tensor(
                    CAND2[:], CAND2[:], self.SC1[:], ALU.max),
                    reads=[RCAND2, self.RSC1], writes=[RCAND2])
        for d in range(NCH):
            kb.sfree(f"W1QT{d}")
        for c in range(NCH):
            kb.sfree(f"XQ{c}")
        kb.sfree("H"); kb.sfree("SILF")
        kb.sfree("convw")

        kb.sfree("INVBC_ts1")
        SCL2BC, RSCL2BC, INV2BC, RINV2BC = self._token_scale_rows(CAND2, RCAND2, 127.0, 1e-5)
        self.dbg("scl2bc", SCL2BC, RSCL2BC)
        kb.sfree("CAND2")

        sw2col, Rsw2col, iw2col, Riw2col = self._weight_absmean("l1f_w2", 8, 4096)
        W2QT = [kb.salloc(f"W2QT{hc}", [128, E], BF16) for hc in range(32)]
        RW2QT = [kb.R(f"W2QT{hc}") for hc in range(32)]
        self._quant_transpose(dr["l1f_w2"], 8, 2048, sw2col, Rsw2col,
                              W2QT[0:16], RW2QT[0:16], "qw2a", kcol0=0)
        self._quant_transpose(dr["l1f_w2"], 8, 2048, sw2col, Rsw2col,
                              W2QT[16:32], RW2QT[16:32], "qw2b", kcol0=2048)

        SQB = [kb.salloc(f"SQB{i}", [128, 512], BF16) for i in range(2)]
        RSQB = [kb.R(f"SQB{i}") for i in range(2)]
        S8 = [kb.salloc(f"S8_{i}", [128, 512], I8) for i in range(2)]
        RS8 = [kb.R(f"S8_{i}") for i in range(2)]
        SIN = [kb.salloc(f"SIN{i}", [128, 512], F32) for i in range(2)]
        RSIN = [kb.R(f"SIN{i}") for i in range(2)]
        XOUT = [kb.salloc(f"XO{c}", [128, S], F32) for c in range(NCH)]
        RXOUT = [kb.R(f"XO{c}") for c in range(NCH)]
        for oc in range(8):
            kb.dma(lambda q, oc=oc: q.dma_start(XOUT[oc][:], self.park_main[oc]),
                   "dmpk", reads=[kb.R("d_pm")], writes=[RXOUT[oc]], tag=f"unparkX3{oc}")
        for qu in range(4):
            t0 = 512 * qu
            for hc in range(32):
                i = hc % 2
                kb.dma(lambda q, hc=hc, t0=t0, i=i: q.dma_start(
                    SIN[i][:], self.park_silu[hc][:, t0:t0 + 512]), f"dmsl{i}",
                    reads=[kb.R("d_ps")], writes=[RSIN[i]], tag=f"sil{qu}_{hc}")
                kb.op("v", lambda v, i=i, t0=t0: v.tensor_tensor(
                    SIN[i][:], SIN[i][:], SCL2BC[:, t0:t0 + 512], ALU.mult),
                    reads=[RSIN[i], RSCL2BC], writes=[RSIN[i]])
                kb.op("v", lambda v, i=i: v.tensor_scalar(
                    SIN[i][:], SIN[i][:], MAGIC, -MAGIC, ALU.add, ALU.add),
                    reads=[RSIN[i]], writes=[RSIN[i]])
                kb.op("v", lambda v, i=i: v.tensor_copy(S8[i][:], SIN[i][:]),
                      reads=[RSIN[i]], writes=[RS8[i]])
                kb.op("v", lambda v, i=i: v.tensor_copy(SQB[i][:], S8[i][:]),
                      reads=[RS8[i]], writes=[RSQB[i]])
                for oc in range(8):
                    pt = pA if oc < 4 else pB
                    prf = self.prA if oc < 4 else self.prB
                    col0 = 512 * (oc % 4)
                    kb.op("t", lambda t, hc=hc, oc=oc, pt=pt, col0=col0, i=i: t.matmul(
                        pt[:, col0:col0 + 512], W2QT[hc][:, 128 * oc:128 * (oc + 1)],
                        SQB[i][:], start=(hc == 0), stop=(hc == 31)),
                        reads=[RW2QT[hc], RSQB[i]], writes=prf(col0, col0 + 512))
            for oc in range(8):
                pt = pA if oc < 4 else pB
                prf = self.prA if oc < 4 else self.prB
                col0 = 512 * (oc % 4)
                kb.op("v", lambda v, oc=oc, t0=t0, pt=pt, col0=col0:
                      v.scalar_tensor_tensor(
                          self.SC1[:, 0:512], pt[:, col0:col0 + 512], iw2col[:, 0:1],
                          INV2BC[:, t0:t0 + 512], ALU.mult, ALU.mult),
                      reads=prf(col0, col0 + 512) + [Riw2col, RINV2BC],
                      writes=[self.RSC1])
                kb.op("v", lambda v, oc=oc, t0=t0: v.tensor_tensor(
                    XOUT[oc][:, t0:t0 + 512], XOUT[oc][:, t0:t0 + 512],
                    self.SC1[:, 0:512], ALU.add),
                    reads=[RXOUT[oc], self.RSC1], writes=[RXOUT[oc]])
        for hc in range(32):
            kb.sfree(f"W2QT{hc}")
        for i in range(2):
            kb.sfree(f"SQB{i}")
            kb.sfree(f"S8_{i}")
        for i in range(2):
            kb.sfree(f"SIN{i}")
        self.X = XOUT
        self.RX = RXOUT

    def _token_scale_rows(self, CAND, RCAND, num, clipmin):
        kb, pA, pB = self.kb, self.pA, self.pB
        TRE = self.kb.salloc("TRE_ts", [64, S], F32)
        RTRE = kb.R("TRE_ts")
        kb.op("v", lambda v: v.tensor_copy(TRE[0:64, :], CAND[64:128, :]),
              reads=[RCAND], writes=[RTRE])
        kb.op("v", lambda v: v.tensor_tensor(CAND[0:64, :], CAND[0:64, :],
                                             TRE[0:64, :], ALU.max),
              reads=[RCAND, RTRE], writes=[RCAND])
        kb.op("v", lambda v: v.tensor_copy(TRE[0:32, :], CAND[32:64, :]),
              reads=[RCAND], writes=[RTRE])
        kb.op("v", lambda v: v.tensor_tensor(CAND[0:32, :], CAND[0:32, :],
                                             TRE[0:32, :], ALU.max),
              reads=[RCAND, RTRE], writes=[RCAND])
        kb.sfree("TRE_ts")
        for half in range(2):
            for i in range(8):
                cidx = half * 8 + i
                kb.op("t", lambda t, cidx=cidx, i=i: t.transpose(
                    pB[:, 1536 + 32 * i:1536 + 32 * (i + 1)],
                    CAND[0:32, 128 * cidx:128 * (cidx + 1)], self.c_ident[0:32, 0:32]),
                    reads=[RCAND, self.Rid], writes=[self.PBANK[7]])
            for i in range(8):
                cidx = half * 8 + i
                kb.op("v", lambda v, cidx=cidx, i=i: v.tensor_reduce(
                    self.MCOL[:, cidx:cidx + 1], pB[:, 1536 + 32 * i:1536 + 32 * (i + 1)],
                    axis=AX.X, op=ALU.max),
                    reads=[self.PBANK[7]], writes=[self.RMCOL])
        SCLT = self.kb.salloc("SCLT_ts", [128, 32], F32)
        RSCLT = kb.R("SCLT_ts")
        kb.op("v", lambda v: v.tensor_scalar(
            SCLT[:, 0:16], self.MCOL[:, 0:16], clipmin, None, ALU.max),
            reads=[self.RMCOL], writes=[RSCLT])
        kb.op("v", lambda v: v.reciprocal(SCLT[:, 16:32], SCLT[:, 0:16]),
              reads=[RSCLT], writes=[RSCLT])
        kb.op("v", lambda v: v.tensor_scalar(
            SCLT[:, 16:32], SCLT[:, 16:32], num, None, ALU.mult),
            reads=[RSCLT], writes=[RSCLT])
        kb.op("v", lambda v: v.tensor_scalar(
            SCLT[:, 0:16], SCLT[:, 0:16], 1.0 / num, None, ALU.mult),
            reads=[RSCLT], writes=[RSCLT])
        for i in range(16):
            kb.op("t", lambda t, i=i: t.transpose(
                pA[0:1, 128 * i:128 * (i + 1)],
                SCLT[:, 16 + i:17 + i], self.c_ident[:]),
                reads=[RSCLT, self.Rid], writes=self.prA(128 * i, 128 * (i + 1)))
            kb.op("t", lambda t, i=i: t.transpose(
                pB[0:1, 128 * i:128 * (i + 1)],
                SCLT[:, i:i + 1], self.c_ident[:]),
                reads=[RSCLT, self.Rid], writes=self.prB(128 * i, 128 * (i + 1)))
        kb.op("v", lambda v: v.tensor_copy(self.ROW[0:1, 0:S], pA[0:1, 0:S]),
              reads=self.prA(0, S), writes=[self.RROW])
        self._ts_ctr = getattr(self, "_ts_ctr", 0) + 1
        SCLBC = self.kb.salloc(f"SCLBC_ts{self._ts_ctr}", [128, S], F32)
        RSCLBC = kb.R(f"SCLBC_ts{self._ts_ctr}")
        INVBC = self.kb.salloc(f"INVBC_ts{self._ts_ctr}", [128, S], F32)
        RINVBC = kb.R(f"INVBC_ts{self._ts_ctr}")
        for nb in range(4):
            kb.op("t", lambda t, nb=nb: t.matmul(
                self.pA[:, 512 * nb:512 * (nb + 1)], self.c_e0[:],
                self.ROWT[:, 512 * nb:512 * (nb + 1)], start=True, stop=True),
                reads=[self.RROW, self.Re0], writes=self.prA(512 * nb, 512 * nb + 512))
        kb.op("v", lambda v: v.tensor_copy(SCLBC[:], self.pA[:, 0:S]),
              reads=self.prA(0, S), writes=[RSCLBC])
        kb.op("s", lambda s: s.copy(self.ROW[0:1, 0:S], pB[0:1, 0:S]),
              reads=self.prB(0, S), writes=[self.RROW])
        kb.sfree("SCLT_ts")
        for nb in range(4):
            kb.op("t", lambda t, nb=nb: t.matmul(
                self.pB[:, 512 * nb:512 * (nb + 1)], self.c_e0[:],
                self.ROWT[:, 512 * nb:512 * (nb + 1)],
                start=True, stop=True),
                reads=[self.RROW, self.Re0], writes=self.prB(512 * nb, 512 * nb + 512))
        kb.op("v", lambda v: v.tensor_copy(INVBC[:], self.pB[:, 0:S]),
              reads=self.prB(0, S), writes=[RINVBC])
        return SCLBC, RSCLBC, INVBC, RINVBC

    def _weight_absmean(self, nm, nch, K):
        kb, dr = self.kb, self.dr
        stg = [kb.salloc(f"wam_stg{i}", [128, K], F32) for i in range(2)]
        Rstg = [kb.R(f"wam_stg{i}") for i in range(2)]
        acc = kb.salloc("wam_acc", [128, 3], F32)
        Racc = kb.R("wam_acc")
        for c in range(nch):
            i = c % 2
            kb.dma(lambda q, c=c, i=i: q.dma_start(
                stg[i][:], dr[nm][128 * c:128 * (c + 1), :]),
                f"dmwm{i}", reads=[Rstg[i]], writes=[Rstg[i]])
            kb.op("s", lambda s, c=c, i=i: s.activation(
                stg[i][:], stg[i][:], AF.Abs,
                accum_out=acc[:, 1 + i:2 + i]),
                reads=[Rstg[i]], writes=[Rstg[i], Racc])
            if c == 0:
                kb.op("v", lambda v, i=i: v.tensor_copy(acc[:, 0:1], acc[:, 1 + i:2 + i]),
                      reads=[Racc], writes=[Racc])
            else:
                kb.op("v", lambda v, i=i: v.tensor_tensor(
                    acc[:, 0:1], acc[:, 0:1], acc[:, 1 + i:2 + i], ALU.add),
                    reads=[Racc], writes=[Racc])
        for i in range(2):
            kb.sfree(f"wam_stg{i}")
        kb.op("t", lambda t: t.matmul(self.pB[0:1, 1536:1537], acc[:, 0:1],
                                      self.c_ones[:], start=True, stop=True),
              reads=[Racc, self.Ron], writes=[self.PBANK[7]])
        r = self.ROW[0:1, 2050:2051]
        ri = self.ROW[0:1, 2051:2052]
        kb.op("v", lambda v, nch=nch, K=K: v.tensor_scalar(
            r, self.pB[0:1, 1536:1537], 1.0 / (nch * 128 * K), None, ALU.mult),
            reads=[self.PBANK[7]], writes=[self.RROW])
        kb.op("v", lambda v: v.tensor_scalar(r, r, 1e-5, None, ALU.max),
              reads=[self.RROW], writes=[self.RROW])
        kb.op("v", lambda v: v.reciprocal(ri, r), reads=[self.RROW], writes=[self.RROW])
        scol = kb.salloc("wam_scol" + nm, [128, 1], F32)
        Rscol = kb.R("wam_scol" + nm)
        icol = kb.salloc("wam_icol" + nm, [128, 1], F32)
        Ricol = kb.R("wam_icol" + nm)
        kb.op("t", lambda t: t.matmul(self.pB[:, 1536:1537], self.c_e0[:],
                                      self.ROWT[:, 2051:2052],
                                      start=True, stop=True),
              reads=[self.RROW, self.Re0], writes=[self.PBANK[7]])
        kb.op("v", lambda v: v.tensor_copy(scol[:], self.pB[:, 1536:1537]),
              reads=[self.PBANK[7]], writes=[Rscol])
        kb.op("t", lambda t: t.matmul(self.pB[:, 1537:1538], self.c_e0[:],
                                      self.ROWT[:, 2050:2051],
                                      start=True, stop=True),
              reads=[self.RROW, self.Re0], writes=[self.PBANK[7]])
        kb.op("v", lambda v: v.tensor_copy(icol[:], self.pB[:, 1537:1538]),
              reads=[self.PBANK[7]], writes=[Ricol])
        kb.sfree("wam_acc")
        return scol, Rscol, icol, Ricol

    def _quant_transpose(self, dram, nch, K, scol, Rscol, OUT, ROUT, tag, kcol0=0):
        kb = self.kb
        stg = [kb.salloc(f"{tag}_stg{i}", [128, K], F32) for i in range(2)]
        Rstg = [kb.R(f"{tag}_stg{i}") for i in range(2)]
        q8 = [kb.salloc(f"{tag}_q8{i}", [128, K], I8) for i in range(2)]
        Rq8 = [kb.R(f"{tag}_q8{i}") for i in range(2)]
        qf = [kb.salloc(f"{tag}_qf{i}", [128, K], F32) for i in range(2)]
        Rqf = [kb.R(f"{tag}_qf{i}") for i in range(2)]
        nd = K // 128
        for c in range(nch):
            i = c % 2
            kb.dma(lambda q, c=c, i=i, kcol0=kcol0, K=K: q.dma_start(
                stg[i][:], dram[128 * c:128 * (c + 1), kcol0:kcol0 + K]),
                f"dmwq{i}", reads=[Rstg[i]], writes=[Rstg[i]])
            kb.op("g", lambda g_, i=i: g_.tensor_scalar(
                qf[i][:], stg[i][:], scol[:, 0:1], 0.0, ALU.mult, ALU.add),
                reads=[Rstg[i], Rscol], writes=[Rqf[i]])
            kb.op("v", lambda v, i=i: v.tensor_scalar(
                qf[i][:], qf[i][:], MAGIC, -MAGIC, ALU.add, ALU.add),
                reads=[Rqf[i]], writes=[Rqf[i]])
            kb.op("v", lambda v, i=i: v.tensor_copy(q8[i][:], qf[i][:]),
                  reads=[Rqf[i]], writes=[Rq8[i]])
            kb.op("v", lambda v, i=i: v.tensor_scalar(
                qf[i][:], q8[i][:], -1.0, 1.0, ALU.max, ALU.min),
                reads=[Rq8[i]], writes=[Rqf[i]])
            for d in range(nd):
                col0 = 512 * (d % 3)
                kb.op("t", lambda t, d=d, col0=col0, i=i: t.transpose(
                    self.pB[:, col0:col0 + 128], qf[i][:, 128 * d:128 * (d + 1)],
                    self.c_ident[:]),
                    reads=[Rqf[i], self.Rid], writes=self.prB(col0, col0 + 128))
                if d % 2 == 0:
                    kb.op("v", lambda v, d=d, c=c, col0=col0: v.tensor_copy(
                        OUT[d][:, 128 * c:128 * (c + 1)], self.pB[:, col0:col0 + 128]),
                        reads=self.prB(col0, col0 + 128), writes=[ROUT[d]])
                else:
                    kb.op("s", lambda s, d=d, c=c, col0=col0: s.copy(
                        OUT[d][:, 128 * c:128 * (c + 1)], self.pB[:, col0:col0 + 128]),
                        reads=self.prB(col0, col0 + 128), writes=[ROUT[d]])
        for i in range(2):
            kb.sfree(f"{tag}_stg{i}")
            kb.sfree(f"{tag}_q8{i}")
            kb.sfree(f"{tag}_qf{i}")

    def p_out(self):
        kb, pA = self.kb, self.pA
        OT = [kb.salloc(f"OT{i}", [128, E], F32) for i in range(2)]
        ROT = [kb.R(f"OT{i}") for i in range(2)]
        for tt in range(16):
            i = tt % 2
            for c in range(NCH):
                col0 = 512 * (c % 4)
                kb.op("t", lambda t, tt=tt, col0=col0, Xc=self.X[c]: t.transpose(
                    pA[:, col0:col0 + 128], Xc[:, 128 * tt:128 * (tt + 1)],
                    self.c_ident[:]),
                    reads=[self.RX[c], self.Rid], writes=self.prA(col0, col0 + 128))
                kb.op("v", lambda v, i=i, c=c, col0=col0: v.tensor_copy(
                    OT[i][:, 128 * c:128 * (c + 1)], pA[:, col0:col0 + 128]),
                    reads=self.prA(col0, col0 + 128), writes=[ROT[i]])
            kb.dma(lambda q, tt=tt, i=i: q.dma_start(
                self.y_out[128 * tt:128 * (tt + 1), :], OT[i][:]), "dmout",
                reads=[ROT[i]], tag=f"out{tt}")


class _SliceTensor:
    def __init__(self, t, a, b):
        self.t = t
        self.a = a
        self.b = b

    def __getitem__(self, idx):
        return self.t[:, self.a:self.b]


_PROG_CACHE = {}


def get_prog(n_blocks=4):
    if n_blocks not in _PROG_CACHE:
        _PROG_CACHE[n_blocks] = Prog(n_blocks)
    return _PROG_CACHE[n_blocks]


def kernel(**inputs):
    n_blocks = int(inputs.pop("_n_blocks", 4))
    prog = get_prog(n_blocks)
    tabs = make_tables()
    w = prep_weights(inputs)
    x = np.ascontiguousarray(np.asarray(inputs["L0_attn_x"], dtype=np.float32))
    in_maps = []
    for core in range(8):
        m = {}
        for name, shape, dt in DRAM_SPECS:
            if name == "x_in":
                m[name] = x[core]
            elif name in tabs:
                m[name] = tabs[name]
            else:
                m[name] = w[name]
        in_maps.append(m)
    res = run_bass_kernel_spmd(prog.nc, in_maps, core_ids=list(range(8)))
    out = np.stack([res.results[i]["y_out"] for i in range(8)], axis=0)
    return out


# version 08 (final)
# speedup vs baseline: 1.4146x; 1.0211x over previous
import os

import numpy as np
import concourse.bass as bass
import concourse.mybir as mybir
from concourse.bass_utils import run_bass_kernel_spmd

F32 = mybir.dt.float32
BF16 = mybir.dt.float16
B16R = mybir.dt.bfloat16
I8 = mybir.dt.int8
AF = mybir.ActivationFunctionType
ALU = mybir.AluOpType
AX = mybir.AxisListType
MAGIC = 12582912.0

S = 2048
E = 1024
NCH = 8

SBUF_LO = 16512
SBUF_HI = 229344


class Tok:
    __slots__ = ("sem", "val")

    def __init__(self, sem, val):
        self.sem = sem
        self.val = val


class Res:
    __slots__ = ("name", "last_w", "readers")

    def __init__(self, name):
        self.name = name
        self.last_w = None
        self.readers = []


class KB:
    ENGINES = ("t", "v", "s", "g", "q")

    def __init__(self, nc):
        self.nc = nc
        self.ops = {e: [] for e in self.ENGINES}
        self.cnt = {e: 0 for e in self.ENGINES}
        self.sem_names = {e: f"es_{e}" for e in self.ENGINES}
        self.dma_sems = {}
        self.sem_consumers = {}
        self.res = {}
        self.free = [(SBUF_LO, SBUF_HI)]
        self.live = {}
        self.retired = []
        self.uid = 0

    def R(self, name):
        if name not in self.res:
            self.res[name] = Res(name)
        return self.res[name]

    def salloc(self, name, shape, dtype):
        nbytes = int(np.prod(shape[1:])) * mybir.dt.size(dtype)
        nbytes = (nbytes + 31) // 32 * 32
        for i, (a, b) in enumerate(self.free):
            a32 = (a + 31) // 32 * 32
            if b - a32 >= nbytes:
                self.free[i] = (a32 + nbytes, b)
                if self.free[i][0] >= self.free[i][1]:
                    self.free.pop(i)
                lo, hi = a32, a32 + nbytes
                self.live[name] = (lo, hi)
                res = self.R(name)
                keep = []
                for (ra, rb, toks) in self.retired:
                    if ra < hi and rb > lo:
                        res.readers = res.readers + toks
                    if not (lo <= ra and rb <= hi):
                        keep.append((ra, rb, toks))
                self.retired = keep
                self.uid += 1
                return self.nc.alloc_sbuf_tensor_at(
                    f"{name}_u{self.uid}", shape, dtype, offset=lo)
        raise MemoryError(f"SBUF full for {name} ({nbytes}B); live={self.live}")

    def sfree(self, name):
        a, b = self.live.pop(name)
        res = self.res.get(name)
        if res is not None:
            toks = list(res.readers)
            if res.last_w is not None:
                toks.append(res.last_w)
            if toks:
                self.retired.append((a, b, toks))
        self.free.append((a, b))
        self.free.sort()
        merged = []
        for seg in self.free:
            if merged and seg[0] <= merged[-1][1]:
                merged[-1] = (merged[-1][0], max(merged[-1][1], seg[1]))
            else:
                merged.append(seg)
        self.free = [tuple(x) for x in merged]

    def _waits(self, engine, dep_toks):
        waits = []
        own = self.sem_names[engine]
        for d in dep_toks:
            if d is None:
                continue
            if d.sem == own:
                if engine != "t":
                    waits.append((own, d.val))
                continue
            if d.sem in self.dma_sems:
                waits.append((d.sem, self.dma_sems[d.sem]))
            else:
                waits.append((d.sem, d.val))
        return waits

    def _gather(self, reads, writes, deps):
        dep_toks = list(deps)
        for r in reads:
            if r.last_w is not None:
                dep_toks.append(r.last_w)
        for w in writes:
            if w.last_w is not None:
                dep_toks.append(w.last_w)
            dep_toks.extend(w.readers)
        return dep_toks

    def op(self, engine, fn, reads=(), writes=(), deps=(), tag=""):
        dep_toks = self._gather(reads, writes, deps)
        waits = self._waits(engine, dep_toks)
        sem = self.sem_names[engine]
        self.cnt[engine] += 1
        tok = Tok(sem, self.cnt[engine])
        for (ws, wv) in waits:
            if ws in self.dma_sems:
                self.sem_consumers.setdefault(ws, []).append(tok)
        self.ops[engine].append((fn, waits, sem, 1, tag))
        for r in reads:
            r.readers.append(tok)
        for w in writes:
            w.last_w = tok
            w.readers = []
        return tok

    def dma(self, fn, sem_name, reads=(), writes=(), deps=(), tag=""):
        dep_toks = self._gather(reads, writes, deps)
        dep_toks.extend(self.sem_consumers.pop(sem_name, []))
        waits = self._waits("q", dep_toks)
        for (ws, wv) in waits:
            if ws in self.dma_sems:
                self.sem_consumers.setdefault(ws, []).append(Tok(sem_name, 0))
                self.sem_consumers[ws].pop()

        self.dma_sems.setdefault(sem_name, 0)
        self.dma_sems[sem_name] += 16
        tok = Tok(sem_name, self.dma_sems[sem_name])
        self.ops["q"].append((fn, waits, sem_name, 16, tag))
        for r in reads:
            r.readers.append(tok)
        for w in writes:
            w.last_w = tok
            w.readers = []
        return tok

    def finalize(self):
        nc = self.nc
        import contextlib
        with contextlib.ExitStack() as stack:
            sems = {}
            for e in self.ENGINES:
                n = self.sem_names[e]
                sems[n] = stack.enter_context(nc.semaphore(n))
            for n in self.dma_sems:
                sems[n] = stack.enter_context(nc.semaphore(n))
            block = stack.enter_context(nc.Block())

            def make_body(ename):
                def body(eng):
                    waited = {}
                    for fn, waits, semname, amt, tag in self.ops[ename]:
                        for (ws, wv) in waits:
                            if waited.get(ws, -1) < wv:
                                eng.wait_ge(sems[ws], wv)
                                waited[ws] = wv
                        fn(eng).then_inc(sems[semname], amt)
                    if ename == "q":
                        for sname, v in self.dma_sems.items():
                            if waited.get(sname, -1) < v:
                                eng.wait_ge(sems[sname], v)
                return body

            for ename, dec in (("t", block.tensor), ("v", block.vector),
                               ("s", block.scalar), ("g", block.gpsimd),
                               ("q", block.sync)):
                dec(make_body(ename))


def make_tables():
    t = {}
    t["ident"] = np.eye(128, dtype=np.float32)
    cb = np.zeros((128, 4), dtype=np.float32)
    cb[:, 0] = 1e-5
    cb[:, 1] = -10.0
    cb[:, 2] = 1e-8
    t["cbias"] = cb
    t["ones_col"] = np.ones((128, 1), dtype=np.float32)
    t["ones_mat"] = np.ones((128, 128), dtype=np.float32)
    blk = np.zeros((128, 128), dtype=np.float32)
    blk[:64, :64] = 1.0
    blk[64:, 64:] = 1.0
    t["blk_mat"] = blk
    e0 = np.zeros((128, 128), dtype=np.float32)
    e0[0, :] = 1.0
    t["e0_mat"] = e0
    t["ones_row"] = np.ones((1, 128), dtype=np.float32)
    top = np.zeros((128, 1), dtype=np.float32)
    top[:64] = 1.0
    t["ones_top"] = top
    bot = np.zeros((128, 1), dtype=np.float32)
    bot[64:] = 1.0
    t["ones_bot"] = bot

    D = 64
    inv_freq = 1.0 / (10000.0 ** (np.arange(0, D, 2, dtype=np.float64) / D))
    pos = np.arange(S, dtype=np.float64)
    fr = inv_freq[:, None] * pos[None, :]
    cos = np.cos(fr)
    sin = np.sin(fr)
    A = np.concatenate([cos, cos], 0)
    B = np.concatenate([-sin, sin], 0)
    t["ropeA"] = np.tile(A, (2, 1)).astype(np.float32)
    t["ropeB"] = np.tile(B, (2, 1)).astype(np.float32)

    q_loc = np.arange(512)[None, :]
    for p in range(4):
        k_glob = p * 128 + np.arange(128)[:, None]
        t[f"mask0_{p}"] = (q_loc >= k_glob).astype(np.float32)

    DH = 64
    WINDOW = 512
    half = np.arange(0, DH, 2, dtype=np.float64)
    invf = 1.0 / (10000.0 ** (half / DH))
    frq = invf[:, None] * pos[None, :]
    frq = np.concatenate([frq, frq], 0)
    base = (half + 0.4 * DH) / (1.4 * DH)
    power = (pos - S // 2) / (WINDOW / 2)
    scale = base[:, None] ** power[None, :]
    scale = np.concatenate([scale, scale], 0)
    cosq = np.cos(frq)
    sinq = np.sin(frq)
    sgn = np.where(np.arange(DH) < 32, -1.0, 1.0)[:, None]
    t["xposAq"] = np.tile(8.0 * cosq * scale, (2, 1)).astype(np.float32)
    t["xposBq"] = np.tile(8.0 * sgn * sinq * scale, (2, 1)).astype(np.float32)
    t["xposAk"] = np.tile(cosq / scale, (2, 1)).astype(np.float32)
    t["xposBk"] = np.tile(sgn * sinq / scale, (2, 1)).astype(np.float32)

    for kc in range(8):
        j_rel = kc * 128 + np.arange(128)[:, None] - 512
        valid = (q_loc >= j_rel) & (q_loc <= j_rel + 512)
        t[f"mask1_{kc}"] = valid.astype(np.float32)
    ql = np.arange(128)[:, None]
    cl = np.arange(128)[None, :]
    t["mask1U"] = np.where(ql >= cl, 0.0, -60000.0).astype(np.float32)
    t["mask1D"] = np.where(ql <= cl, 0.0, -60000.0).astype(np.float32)
    return t


def _cols(vec, n):
    nch = (n + 127) // 128
    out = np.zeros((nch * 128,), dtype=np.float32)
    out[:n] = np.asarray(vec, dtype=np.float32)
    return np.ascontiguousarray(out.reshape(nch, 128).T)


def prep_weights(inp):
    w = {}
    H, KVH, D, QL, KVL = 16, 4, 64, 64, 32
    perm = np.empty(D, dtype=np.int64)
    perm[:32] = np.arange(32) * 2
    perm[32:] = np.arange(32) * 2 + 1
    perm_sw = np.concatenate([perm[32:], perm[:32]])
    Wqb = np.asarray(inp["L0_attn_Wqb"], dtype=np.float32)
    qcols = Wqb.reshape(QL, H, D)
    w["l0a_Wqa"] = np.asarray(inp["L0_attn_Wqa"], dtype=np.float32)
    w["l0a_Wqb_n"] = np.ascontiguousarray(qcols[:, :, perm].reshape(QL, E))
    w["l0a_Wqb_s"] = np.ascontiguousarray(qcols[:, :, perm_sw].reshape(QL, E))
    Wkvb = np.asarray(inp["L0_attn_Wkvb"], dtype=np.float32)
    kv = Wkvb.reshape(KVL, KVH, D, 2)
    w["l0a_Wkva"] = np.asarray(inp["L0_attn_Wkva"], dtype=np.float32)
    w["l0a_Wkvb_k_n"] = np.ascontiguousarray(kv[:, :, perm, 0].reshape(KVL, KVH * D))
    w["l0a_Wkvb_k_s"] = np.ascontiguousarray(kv[:, :, perm_sw, 0].reshape(KVL, KVH * D))
    w["l0a_Wkvb_v"] = np.ascontiguousarray(kv[:, :, :, 1].reshape(KVL, KVH * D))
    w["l0a_Wout"] = np.asarray(inp["L0_attn_Wout"], dtype=np.float32)
    w["l0a_bout"] = _cols(inp["L0_attn_bout"], E)

    HID = 2730
    w["l0f_wqkv"] = np.asarray(inp["L0_ffn_wqkv"], dtype=np.float32)
    w["l0f_wo"] = np.asarray(inp["L0_ffn_wo"], dtype=np.float32)
    w1 = np.asarray(inp["L0_ffn_w1"], dtype=np.float32)
    w["l0f_w1a"] = np.ascontiguousarray(w1[:, :, :HID])
    w["l0f_w1g"] = np.ascontiguousarray(w1[:, :, HID:])
    w["l0f_w2"] = np.asarray(inp["L0_ffn_w2"], dtype=np.float32)
    b1 = np.asarray(inp["L0_ffn_b1"], dtype=np.float32)
    def _lstack(arrs):
        st = np.stack(arrs)
        return np.ascontiguousarray(st.transpose(1, 0, 2).reshape(128, -1))
    w["l0f_b1a"] = _lstack([_cols(b1[l, :HID], HID) for l in range(2)])
    w["l0f_b1g"] = _lstack([_cols(b1[l, HID:], HID) for l in range(2)])
    w["l0f_b2"] = _lstack([_cols(np.asarray(inp["L0_ffn_b2"])[l], E) for l in range(2)])
    w["l0f_lng"] = _lstack([_cols(np.asarray(inp["L0_ffn_ln_g"])[l], E) for l in range(2)])
    w["l0f_lnb"] = _lstack([_cols(np.asarray(inp["L0_ffn_ln_b"])[l], E) for l in range(2)])
    qs = np.asarray(inp["L0_ffn_q_scale"], dtype=np.float32)
    ks = np.asarray(inp["L0_ffn_k_scale"], dtype=np.float32)
    w["l0f_qs"] = _lstack([np.tile(qs[l], 2)[:, None].astype(np.float32) for l in range(2)])
    w["l0f_ks"] = _lstack([np.tile(ks[l], 2)[:, None].astype(np.float32) for l in range(2)])

    def _sw(vec):
        return np.concatenate([vec[32:], vec[:32]])
    w["l0f_qs_sw"] = _lstack([np.tile(_sw(qs[l]), 2)[:, None].astype(np.float32)
                              for l in range(2)])
    w["l0f_ks_sw"] = _lstack([np.tile(_sw(ks[l]), 2)[:, None].astype(np.float32)
                              for l in range(2)])

    w["l1a_Wq"] = np.asarray(inp["L1_attn_Wq"], dtype=np.float32)
    w["l1a_Wk"] = np.asarray(inp["L1_attn_Wk"], dtype=np.float32)
    w["l1a_Wv"] = np.asarray(inp["L1_attn_Wv"], dtype=np.float32)
    w["l1a_proj"] = np.asarray(inp["L1_attn_proj"], dtype=np.float32)
    w["l1a_Wo"] = np.asarray(inp["L1_attn_Wo"], dtype=np.float32)
    w["l1a_bo"] = _cols(inp["L1_attn_bo"], E)

    w["l1f_w1"] = np.asarray(inp["L1_ffn_w1"], dtype=np.float32)
    w["l1f_w2"] = np.asarray(inp["L1_ffn_w2"], dtype=np.float32)
    cw = np.asarray(inp["L1_ffn_conv_w"], dtype=np.float32).reshape(32, 128, 3)
    w["l1f_convw"] = np.ascontiguousarray(cw.transpose(1, 2, 0).reshape(128, 96))
    w["l1f_convb"] = _cols(inp["L1_ffn_conv_b"], 4096)
    return w


DRAM_SPECS = [
    ("x_in", (S, E), F32), ("ident", (128, 128), F32), ("cbias", (128, 4), F32),
    ("ones_col", (128, 1), F32), ("ones_row", (1, 128), F32),
    ("ones_mat", (128, 128), F32), ("blk_mat", (128, 128), F32),
    ("e0_mat", (128, 128), F32),
    ("ones_top", (128, 1), F32), ("ones_bot", (128, 1), F32),
    ("ropeA", (128, S), F32), ("ropeB", (128, S), F32),
    ("mask0_0", (128, 512), F32), ("mask0_1", (128, 512), F32),
    ("mask0_2", (128, 512), F32), ("mask0_3", (128, 512), F32),
    ("xposAq", (128, S), F32), ("xposBq", (128, S), F32),
    ("xposAk", (128, S), F32), ("xposBk", (128, S), F32),
] + [(f"mask1_{k}", (128, 512), F32) for k in range(8)]\
  + [("mask1U", (128, 128), F32), ("mask1D", (128, 128), F32)] + [
    ("l0a_Wqa", (E, 64), F32), ("l0a_Wqb_n", (64, E), F32), ("l0a_Wqb_s", (64, E), F32),
    ("l0a_Wkva", (E, 32), F32),
    ("l0a_Wkvb_k_n", (32, 256), F32), ("l0a_Wkvb_k_s", (32, 256), F32),
    ("l0a_Wkvb_v", (32, 256), F32),
    ("l0a_Wout", (E, E), F32), ("l0a_bout", (128, 8), F32),
    ("l0f_wqkv", (2, E, 1536), F32), ("l0f_wo", (2, 512, E), F32),
    ("l0f_w1a", (2, E, 2730), F32), ("l0f_w1g", (2, E, 2730), F32),
    ("l0f_w2", (2, 2730, E), F32),
    ("l0f_b1a", (128, 44), F32), ("l0f_b1g", (128, 44), F32),
    ("l0f_b2", (128, 16), F32),
    ("l0f_lng", (128, 16), F32), ("l0f_lnb", (128, 16), F32),
    ("l0f_qs", (128, 2), F32), ("l0f_ks", (128, 2), F32),
    ("l0f_qs_sw", (128, 2), F32), ("l0f_ks_sw", (128, 2), F32),
    ("l1a_Wq", (E, E), F32), ("l1a_Wk", (E, E), F32), ("l1a_Wv", (E, E), F32),
    ("l1a_proj", (64, 256), F32), ("l1a_Wo", (E, E), F32), ("l1a_bo", (128, 8), F32),
    ("l1f_w1", (4096, E), F32), ("l1f_w2", (E, 4096), F32),
    ("l1f_convw", (128, 96), F32), ("l1f_convb", (128, 32), F32),
]


class Prog:
    def __init__(self, n_blocks=4):
        self.n_blocks = n_blocks
        nc = bass.Bass("TRN2", target_bir_lowering=False)
        self.nc = nc
        kb = KB(nc)
        self.kb = kb
        self.dr = {}
        for name, shape, dt in DRAM_SPECS:
            self.dr[name] = nc.dram_tensor(name, list(shape), dt, kind="ExternalInput").ap()
        self.y_out = nc.dram_tensor("y_out", [S, E], F32, kind="ExternalOutput").ap()
        self.park_main = nc.dram_tensor("park_main", [NCH, 128, S], F32, kind="Internal").ap()
        self.park_int = nc.dram_tensor("park_int", [NCH, 128, S], F32, kind="Internal").ap()
        self.park_int2 = nc.dram_tensor("park_int2", [NCH, 128, S], F32, kind="Internal").ap()
        self.park_silu = nc.dram_tensor("park_silu", [32, 128, S], F32, kind="Internal").ap()

        self.dbg_names = [x for x in os.environ.get("KDBG", "").split(",") if x]
        self.pA = nc.alloc_psum_tensor("pA", [128, 2048], F32)
        self.pB = nc.alloc_psum_tensor("pB", [128, 2048], F32)
        self.PBANK = [kb.R(f"pbank{i}") for i in range(8)]

        self.build()

    def dbg(self, name, tensor, res, rows=128, cols=S):
        if name not in self.dbg_names:
            return
        ap = self.nc.dram_tensor("dbg_" + name, [rows, cols],
                                 tensor.dtype, kind="ExternalOutput").ap()
        self.kb.dma(lambda q: q.dma_start(ap, tensor[0:rows, 0:cols]), "dmdbg",
                    reads=[res], tag="dbg" + name)

    def prA(self, c0, c1):
        return [self.PBANK[i] for i in range(c0 // 512, (c1 - 1) // 512 + 1)]

    def prB(self, c0, c1):
        return [self.PBANK[4 + i] for i in range(c0 // 512, (c1 - 1) // 512 + 1)]

    def build(self):
        kb, nc, dr = self.kb, self.nc, self.dr

        self.c_ident = kb.salloc("c_ident", [128, 128], F32)
        self.c_ones = kb.salloc("c_ones", [128, 1], F32)
        self.c_onesb = kb.salloc("c_onesb", [128, 1], BF16)
        self.c_onesr = kb.salloc("c_onesr", [1, 128], F32)
        self.c_onesrb = kb.salloc("c_onesrb", [1, 128], BF16)
        self.c_onesm = kb.salloc("c_onesm", [128, 128], F32)
        self.c_onesmb = kb.salloc("c_onesmb", [128, 128], BF16)
        self.c_blkf = kb.salloc("c_blkf", [128, 128], F32)
        self.c_blkb = kb.salloc("c_blkb", [128, 128], BF16)
        self.c_e0 = kb.salloc("c_e0", [128, 128], F32)
        self.c_e0b = kb.salloc("c_e0b", [128, 128], BF16)
        self.c_bias = kb.salloc("c_bias", [128, 4], F32)
        for nm, t, src in (("c_ident", self.c_ident, "ident"),
                           ("c_ones", self.c_ones, "ones_col"),
                           ("c_onesr", self.c_onesr, "ones_row"),
                           ("c_onesm", self.c_onesm, "ones_mat"),
                           ("c_blkf", self.c_blkf, "blk_mat"),
                           ("c_e0", self.c_e0, "e0_mat"),
                           ("c_bias", self.c_bias, "cbias")):
            kb.dma(lambda q, t=t, src=src: q.dma_start(t[:], dr[src]), "dmc",
                   writes=[kb.R(nm)])
        for nm2, dst, srcn in (("c_onesb", self.c_onesb, "c_ones"),
                               ("c_onesrb", self.c_onesrb, "c_onesr"),
                               ("c_onesmb", self.c_onesmb, "c_onesm"),
                               ("c_blkb", self.c_blkb, "c_blkf"),
                               ("c_e0b", self.c_e0b, "c_e0")):
            srct = getattr(self, srcn)
            kb.op("v", lambda v, dst=dst, srct=srct: v.tensor_copy(dst[:], srct[:]),
                  reads=[kb.R(srcn)], writes=[kb.R(nm2)])
        self.Rid = kb.R("c_ident")
        self.Ron = kb.R("c_ones")
        self.Ronb = kb.R("c_onesb")
        self.Ronr = kb.R("c_onesr")
        self.Ronrb = kb.R("c_onesrb")
        self.Ronesm = kb.R("c_onesm")
        self.Ronesmb = kb.R("c_onesmb")
        self.Rblkb = kb.R("c_blkb")
        self.Re0 = kb.R("c_e0")
        self.Re0b = kb.R("c_e0b")
        self.Rcb = kb.R("c_bias")

        self.bias = {}
        for nm, shape in (("l0a_bout", (128, 8)), ("l0f_b1a", (128, 44)),
                          ("l0f_b1g", (128, 44)), ("l0f_b2", (128, 16)),
                          ("l0f_lng", (128, 16)), ("l0f_lnb", (128, 16)),
                          ("l0f_qs", (128, 2)), ("l0f_ks", (128, 2)),
                          ("l0f_qs_sw", (128, 2)), ("l0f_ks_sw", (128, 2)),
                          ("l1a_bo", (128, 8)), ("l1f_convb", (128, 32))):
            t = kb.salloc("sb_" + nm, [128, shape[1]], F32)
            kb.dma(lambda q, t=t, nm=nm: q.dma_start(t[:], dr[nm]), "dmb",
                   writes=[kb.R("b_" + nm)])
            self.bias[nm] = (t, kb.R("b_" + nm))

        self.SCa = kb.salloc("SCa", [128, S], BF16)
        self.SCb = kb.salloc("SCb", [128, S], BF16)
        self.SC4 = kb.salloc("SC4", [128, S], F32)
        self.SC1 = kb.salloc("SC1", [128, S], F32)
        self.SC2 = kb.salloc("SC2", [128, S], F32)
        self.SC3 = kb.salloc("SC3", [128, S], F32)
        self.RSC2 = kb.R("SC2")
        self.RSC3 = kb.R("SC3")
        self.RSCa = kb.R("SCa")
        self.RSC4 = kb.R("SC4")
        self.RSCb = kb.R("SCb")
        self.RSC1 = kb.R("SC1")
        self.ROWT = kb.salloc("ROWT", [128, 2304], F32)
        self.ROW = self.ROWT
        self.RROW = kb.R("ROWT")
        kb.op("g", lambda g_: g_.memset(self.ROWT[:], 0.0), writes=[self.RROW])
        self.MCOL = kb.salloc("MCOL", [128, 16], F32)
        self.RMCOL = kb.R("MCOL")

        self.X = [kb.salloc(f"X{c}", [128, S], F32) for c in range(NCH)]
        self.RX = [kb.R(f"X{c}") for c in range(NCH)]

        self.p0_load()
        if self.n_blocks >= 1:
            self.blk_l0_attn()
        if self.n_blocks >= 2:
            self.blk_l0_ffn()
        if self.n_blocks >= 3:
            self.blk_l1_attn()
        if self.n_blocks >= 4:
            self.blk_l1_ffn()
        self.p_out()
        kb.finalize()

    def p0_load(self):
        kb, dr = self.kb, self.dr
        pA = self.pA
        tin = [kb.salloc(f"tin{i}", [128, E], F32) for i in range(4)]
        Rtin = [kb.R(f"tin{i}") for i in range(4)]
        for tt in range(16):
            b = tt % 4
            kb.dma(lambda q, tt=tt, b=b: q.dma_start(
                tin[b][:], dr["x_in"][128 * tt:128 * (tt + 1), :]),
                f"dmi{b}", writes=[Rtin[b]])
            for c in range(NCH):
                col0 = 512 * (c % 4)
                kb.op("t", lambda t, b=b, c=c, col0=col0: t.transpose(
                    pA[:, col0:col0 + 128], tin[b][:, 128 * c:128 * (c + 1)],
                    self.c_ident[:]),
                    reads=[Rtin[b], self.Rid], writes=self.prA(col0, col0 + 128))
                kb.op("v", lambda v, tt=tt, col0=col0, Xc=self.X[c]: v.tensor_copy(
                    Xc[:, 128 * tt:128 * (tt + 1)], pA[:, col0:col0 + 128]),
                    reads=self.prA(col0, col0 + 128), writes=[self.RX[c]])
        for i in range(4):
            kb.sfree(f"tin{i}")

    def emit_ln(self, srcap, rsrc, outb, routb, outf_ap=None, routf=None,
               gb=None, tagp="ln"):
        kb, pA, pB = self.kb, self.pA, self.pB
        sc = [self.SCa, self.SCb]
        rs = [self.RSCa, self.RSCb]
        sq = sc
        rq = rs
        for c in range(NCH):
            j = c % 2
            kb.op("s", lambda s, j=j, ap=srcap(c): s.copy(sc[j][:], ap),
                  reads=[rsrc[c]], writes=[rs[j]], tag=f"{tagp}cp{c}")
            for nb in range(4):
                kb.op("t", lambda t, c=c, nb=nb, j=j: t.matmul(
                    pA[:, 512 * nb:512 * (nb + 1)], self.c_onesmb[:],
                    sc[j][:, 512 * nb:512 * (nb + 1)],
                    start=(c == 0), stop=(c == NCH - 1)),
                    reads=[rs[j], self.Ronesmb], writes=self.prA(512 * nb, 512 * nb + 512))
            kb.op("v", lambda v, j=j, ap=srcap(c): v.tensor_tensor(
                sq[j][:], ap, ap, ALU.mult),
                reads=[rsrc[c]], writes=[rq[j]], tag=f"{tagp}sq{c}")
            for nb in range(4):
                kb.op("t", lambda t, c=c, nb=nb, j=j: t.matmul(
                    pB[:, 512 * nb:512 * (nb + 1)], self.c_onesmb[:],
                    sq[j][:, 512 * nb:512 * (nb + 1)],
                    start=(c == 0), stop=(c == NCH - 1)),
                    reads=[rq[j], self.Ronesmb], writes=self.prB(512 * nb, 512 * nb + 512))
        kb.op("v", lambda v: v.tensor_scalar(
            self.SC2[:], pA[:, 0:S], 1.0 / E, None, ALU.mult),
            reads=self.prA(0, S), writes=[self.RSC2], tag=f"{tagp}m")
        kb.op("v", lambda v: v.tensor_tensor(
            self.SC1[:], self.SC2[:], self.SC2[:], ALU.mult),
            reads=[self.RSC2], writes=[self.RSC1], tag=f"{tagp}mm")
        kb.op("v", lambda v: v.scalar_tensor_tensor(
            self.SC1[:], pB[:, 0:S], 1.0 / E, self.SC1[:], ALU.mult, ALU.subtract),
            reads=self.prB(0, S) + [self.RSC1], writes=[self.RSC1], tag=f"{tagp}v")
        kb.op("s", lambda s: s.activation(self.SC1[:], self.SC1[:], AF.Ln,
                                          bias=self.c_bias[:, 0:1]),
              reads=[self.RSC1, self.Rcb], writes=[self.RSC1], tag=f"{tagp}ln")
        kb.op("s", lambda s: s.activation(self.SC1[:], self.SC1[:], AF.Exp, scale=-0.5),
              reads=[self.RSC1], writes=[self.RSC1], tag=f"{tagp}ex")
        kb.op("v", lambda v: v.scalar_tensor_tensor(
            self.SC3[:], self.SC2[:], -1.0, self.SC1[:], ALU.mult, ALU.mult),
            reads=[self.RSC2, self.RSC1], writes=[self.RSC3], tag=f"{tagp}b")
        for c in range(NCH):
            j = c % 2
            scr = self.SC2 if j == 0 else self.SC4
            rscr = self.RSC2 if j == 0 else self.RSC4
            aeng = "v"
            kb.op(aeng, lambda e, ap=srcap(c), scr=scr: e.tensor_tensor(
                scr[:], ap, self.SC1[:], ALU.mult),
                reads=[rsrc[c], self.RSC1], writes=[rscr], tag=f"{tagp}a1_{c}")
            if outf_ap is not None:
                kb.op(aeng, lambda e, ap=outf_ap(c), scr=scr: e.tensor_tensor(
                    ap, scr[:], self.SC3[:], ALU.add),
                    reads=[rscr, self.RSC3], writes=[routf[c]],
                    tag=f"{tagp}a2f{c}")
                if outb[c] is not None:
                    kb.op("s", lambda s, oap=outb[c][:], iap=outf_ap(c): s.copy(oap, iap),
                          reads=[routf[c]], writes=[routb[c]], tag=f"{tagp}a3_{c}")
            elif gb is not None:
                gap, bap, rgb = gb
                kb.op(aeng, lambda e, c=c, scr=scr: e.tensor_tensor(
                    scr[:], scr[:], self.SC3[:], ALU.add),
                    reads=[rscr, self.RSC3], writes=[rscr],
                    tag=f"{tagp}a2_{c}")
                kb.op("s", lambda s, oap=outb[c][:], ga=gap(c), ba=bap(c), scr=scr,
                      rscr=rscr: s.activation(
                    oap, scr[:], AF.Identity, scale=ga, bias=ba),
                    reads=[rscr, rgb], writes=[routb[c]], tag=f"{tagp}a3_{c}")
            else:
                kb.op(aeng, lambda e, oap=outb[c][:], scr=scr: e.tensor_tensor(
                    oap, scr[:], self.SC3[:], ALU.add),
                    reads=[rscr, self.RSC3], writes=[routb[c]],
                    tag=f"{tagp}a2_{c}")

    def load_wchunk(self, dram_ap, fbuf, rf, bbuf, rb, sem, tag=""):
        kb = self.kb
        kb.dma(lambda q: q.dma_start(fbuf[:], dram_ap), sem, writes=[rf], tag=tag)
        kb.op("v", lambda v: v.tensor_copy(bbuf[:], fbuf[:]), reads=[rf], writes=[rb],
              tag=tag + "cv")

    def blk_l0_attn(self):
        kb, nc, dr, pA, pB = self.kb, self.nc, self.dr, self.pA, self.pB
        LNO = [kb.salloc(f"aLN{c}", [128, S], BF16) for c in range(NCH)]
        RLNO = [kb.R(f"aLN{c}") for c in range(NCH)]
        Xs_now = list(self.X)
        self.emit_ln(lambda c: Xs_now[c][:], self.RX, LNO, RLNO, tagp="ln0")
        self.dbg("ln0", LNO[0], RLNO[0])
        for c in range(NCH):
            kb.dma(lambda q, c=c, Xc=self.X[c]: q.dma_start(self.park_main[c], Xc[:]),
                   "dmpk", reads=[self.RX[c]], writes=[kb.R("d_pm")], tag=f"parkX0{c}")
        for c in range(NCH):
            kb.sfree(f"X{c}")

        wqa = kb.salloc("wqa", [128, 8 * 64], BF16)
        wkva = kb.salloc("wkva", [128, 8 * 32], BF16)
        wqbn = kb.salloc("wqbn", [64, E], BF16)
        wqbs = kb.salloc("wqbs", [64, E], BF16)
        wkbn = kb.salloc("wkbn", [32, 256], BF16)
        wkbs = kb.salloc("wkbs", [32, 256], BF16)
        wkbv = kb.salloc("wkbv", [32, 256], BF16)
        stg = kb.salloc("stg", [128, E], F32)
        Rstg = kb.R("stg")
        names = ["wqa", "wkva", "wqbn", "wqbs", "wkbn", "wkbs", "wkbv"]
        Rw = {n: kb.R(n) for n in names}
        kb.dma(lambda q: q.dma_start(
            stg[:, 0:512].rearrange("p (c k) -> p c k", k=64),
            dr["l0a_Wqa"].rearrange("(c p) k -> p c k", p=128)),
            "dmw0", writes=[Rstg])
        kb.op("v", lambda v: v.tensor_copy(wqa[:], stg[:, 0:512]), reads=[Rstg],
              writes=[Rw["wqa"]])
        kb.dma(lambda q: q.dma_start(
            stg[:, 512:768].rearrange("p (c k) -> p c k", k=32),
            dr["l0a_Wkva"].rearrange("(c p) k -> p c k", p=128)),
            "dmw0", reads=[Rstg], writes=[Rstg])
        kb.op("v", lambda v: v.tensor_copy(wkva[:], stg[:, 512:768]), reads=[Rstg],
              writes=[Rw["wkva"]])
        for nm, dnm, tgt in (("wqbn", "l0a_Wqb_n", wqbn), ("wqbs", "l0a_Wqb_s", wqbs)):
            kb.dma(lambda q, dnm=dnm: q.dma_start(stg[0:64, :], dr[dnm]), "dmw0",
                   reads=[Rstg], writes=[Rstg])
            kb.op("v", lambda v, tgt=tgt: v.tensor_copy(tgt[:], stg[0:64, :]),
                  reads=[Rstg], writes=[Rw[nm]])
        for nm, dnm, tgt in (("wkbn", "l0a_Wkvb_k_n", wkbn),
                             ("wkbs", "l0a_Wkvb_k_s", wkbs),
                             ("wkbv", "l0a_Wkvb_v", wkbv)):
            kb.dma(lambda q, dnm=dnm: q.dma_start(stg[0:32, 0:256], dr[dnm]), "dmw0",
                   reads=[Rstg], writes=[Rstg])
            kb.op("v", lambda v, tgt=tgt: v.tensor_copy(tgt[:], stg[0:32, 0:256]),
                  reads=[Rstg], writes=[Rw[nm]])

        qlat = kb.salloc("qlat", [64, S], BF16)
        kvlat = kb.salloc("kvlat", [32, S], BF16)
        Rqlat, Rkvlat = kb.R("qlat"), kb.R("kvlat")
        for c in range(NCH):
            for nb in range(4):
                kb.op("t", lambda t, c=c, nb=nb: t.matmul(
                    pA[0:64, 512 * nb:512 * (nb + 1)], wqa[:, 64 * c:64 * (c + 1)],
                    LNO[c][:, 512 * nb:512 * (nb + 1)],
                    start=(c == 0), stop=(c == NCH - 1)),
                    reads=[RLNO[c], Rw["wqa"]], writes=self.prA(512 * nb, 512 * nb + 512))
        kb.op("v", lambda v: v.tensor_copy(qlat[:], pA[0:64, :]),
              reads=self.prA(0, S), writes=[Rqlat])
        self.dbg("qlat", qlat, Rqlat, rows=64)
        for c in range(NCH):
            for nb in range(4):
                kb.op("t", lambda t, c=c, nb=nb: t.matmul(
                    pB[0:32, 512 * nb:512 * (nb + 1)], wkva[:, 32 * c:32 * (c + 1)],
                    LNO[c][:, 512 * nb:512 * (nb + 1)],
                    start=(c == 0), stop=(c == NCH - 1)),
                    reads=[RLNO[c], Rw["wkva"]], writes=self.prB(512 * nb, 512 * nb + 512))
        kb.op("v", lambda v: v.tensor_copy(kvlat[:], pB[0:32, :]),
              reads=self.prB(0, S), writes=[Rkvlat])

        for c in range(NCH):
            kb.sfree(f"aLN{c}")
        kb.sfree("wqa"); kb.sfree("wkva"); kb.sfree("stg")

        rA = kb.salloc("rA", [128, S], F32)
        rB = kb.salloc("rB", [128, S], F32)
        RrA, RrB = kb.R("rA"), kb.R("rB")
        kb.dma(lambda q: q.dma_start(rA[:], dr["ropeA"]), "dmw0", writes=[RrA])
        kb.dma(lambda q: q.dma_start(rB[:], dr["ropeB"]), "dmw0", writes=[RrB])

        QH = [kb.salloc(f"QH{h}", [64, S], BF16) for h in range(16)]
        RQH = [kb.R(f"QH{h}") for h in range(16)]
        KG = [kb.salloc(f"KG{g}", [64, S], BF16) for g in range(4)]
        RKG = [kb.R(f"KG{g}") for g in range(4)]
        for oc in range(8):
            for nb in range(4):
                kb.op("t", lambda t, oc=oc, nb=nb: t.matmul(
                    pA[:, 512 * nb:512 * (nb + 1)], wqbn[:, 128 * oc:128 * (oc + 1)],
                    qlat[:, 512 * nb:512 * (nb + 1)], start=True, stop=True),
                    reads=[Rqlat, Rw["wqbn"]], writes=self.prA(512 * nb, 512 * nb + 512))
                kb.op("t", lambda t, oc=oc, nb=nb: t.matmul(
                    pB[:, 512 * nb:512 * (nb + 1)], wqbs[:, 128 * oc:128 * (oc + 1)],
                    qlat[:, 512 * nb:512 * (nb + 1)], start=True, stop=True),
                    reads=[Rqlat, Rw["wqbs"]], writes=self.prB(512 * nb, 512 * nb + 512))
            kb.op("v", lambda v: v.tensor_tensor(self.SC1[:], pA[:, 0:S], rA[:], ALU.mult),
                  reads=self.prA(0, S) + [RrA], writes=[self.RSC1])
            kb.op("v", lambda v: v.tensor_tensor(
                self.SCa[:], pB[:, 0:S], rB[:], ALU.mult),
                reads=self.prB(0, S) + [RrB], writes=[self.RSCa])
            for half in range(2):
                h = 2 * oc + half
                kb.op("v", lambda v, h=h, half=half: v.tensor_tensor(
                    QH[h][:], self.SC1[64 * half:64 * (half + 1), :],
                    self.SCa[64 * half:64 * (half + 1), :], ALU.add),
                    reads=[self.RSC1, self.RSCa], writes=[RQH[h]])
        for oc in range(2):
            for nb in range(4):
                kb.op("t", lambda t, oc=oc, nb=nb: t.matmul(
                    pA[:, 512 * nb:512 * (nb + 1)], wkbn[:, 128 * oc:128 * (oc + 1)],
                    kvlat[:, 512 * nb:512 * (nb + 1)], start=True, stop=True),
                    reads=[Rkvlat, Rw["wkbn"]], writes=self.prA(512 * nb, 512 * nb + 512))
                kb.op("t", lambda t, oc=oc, nb=nb: t.matmul(
                    pB[:, 512 * nb:512 * (nb + 1)], wkbs[:, 128 * oc:128 * (oc + 1)],
                    kvlat[:, 512 * nb:512 * (nb + 1)], start=True, stop=True),
                    reads=[Rkvlat, Rw["wkbs"]], writes=self.prB(512 * nb, 512 * nb + 512))
            kb.op("v", lambda v: v.tensor_tensor(self.SC1[:], pA[:, 0:S], rA[:], ALU.mult),
                  reads=self.prA(0, S) + [RrA], writes=[self.RSC1])
            kb.op("v", lambda v: v.tensor_tensor(self.SCa[:], pB[:, 0:S], rB[:], ALU.mult),
                  reads=self.prB(0, S) + [RrB], writes=[self.RSCa])
            for half in range(2):
                g = 2 * oc + half
                kb.op("v", lambda v, g=g, half=half: v.tensor_tensor(
                    KG[g][:], self.SC1[64 * half:64 * (half + 1), :],
                    self.SCa[64 * half:64 * (half + 1), :], ALU.add),
                    reads=[self.RSC1, self.RSCa], writes=[RKG[g]])

        self.dbg("qh0", QH[0], RQH[0], rows=64)
        self.dbg("kg0", KG[0], RKG[0], rows=64)
        VT = [kb.salloc(f"VT{tt}", [128, 260], BF16) for tt in range(16)]
        RVT = [kb.R(f"VT{tt}") for tt in range(16)]
        for tt in range(16):
            kb.op("g", lambda g_, tt=tt: g_.memset(
                VT[tt][:].rearrange("p (g c) -> p g c", c=65)[:, :, 64:65], 1.0),
                writes=[RVT[tt]])
        for tt in range(16):
            col0 = 512 * (tt % 4)
            kb.op("t", lambda t, tt=tt, col0=col0: t.matmul(
                pA[:, col0:col0 + 256], kvlat[:, 128 * tt:128 * (tt + 1)],
                wkbv[:], start=True, stop=True),
                reads=[Rkvlat, Rw["wkbv"]], writes=self.prA(col0, col0 + 256))
            kb.op("v", lambda v, tt=tt, col0=col0: v.tensor_copy(
                VT[tt][:].rearrange("p (g c) -> p g c", c=65)[:, :, 0:64],
                pA[:, col0:col0 + 256].rearrange("p (g c) -> p g c", c=64)),
                reads=self.prA(col0, col0 + 256), writes=[RVT[tt]])

        self.dbg("vt0", VT[0], RVT[0], cols=260)
        kb.sfree("qlat"); kb.sfree("kvlat")
        kb.sfree("rA"); kb.sfree("rB")
        for n in ("wqbn", "wqbs", "wkbn", "wkbs", "wkbv"):
            kb.sfree(n)

        mstg = kb.salloc("mstg", [128, 512], F32)
        Rmstg = kb.R("mstg")
        M0 = [kb.salloc(f"M0_{p}", [128, 512], BF16) for p in range(4)]
        RM0 = [kb.R(f"M0_{p}") for p in range(4)]
        for p in range(4):
            kb.dma(lambda q, p=p: q.dma_start(mstg[:], dr[f"mask0_{p}"]), "dmw0",
                   reads=[Rmstg], writes=[Rmstg])
            kb.op("v", lambda v, p=p: v.tensor_copy(M0[p][:], mstg[:]),
                  reads=[Rmstg], writes=[RM0[p]])
        kb.sfree("mstg")

        ATT = [kb.salloc(f"ATT{c}", [128, S], BF16) for c in range(NCH)]
        RATT = [kb.R(f"ATT{c}") for c in range(NCH)]
        PT = [kb.salloc(f"PT{i}", [128, 512], BF16) for i in range(4)]
        RPT = [kb.R(f"PT{i}") for i in range(4)]
        TMP64 = kb.salloc("TMP64", [64, 512], BF16)
        RTMP64 = kb.R("TMP64")
        BCS = kb.salloc("BCS", [64, 512], F32)
        RBCS = kb.R("BCS")
        it = 0
        for h in range(16):
            g = h // 4
            for qt in range(4):
                nkc = 4 * (qt + 1)
                par = it % 2
                acc0 = 512 * par
                bc0 = 1024 + 512 * par
                pend_pv = None
                for kc in range(nkc):
                    i = kc % 4
                    pt_cols = 512 * i
                    kb.op("t", lambda t, h=h, g=g, qt=qt, kc=kc, pt_cols=pt_cols:
                          t.matmul(pA[:, pt_cols:pt_cols + 512],
                                   KG[g][:, 128 * kc:128 * (kc + 1)],
                                   QH[h][:, 512 * qt:512 * (qt + 1)],
                                   start=True, stop=True),
                          reads=[RKG[g], RQH[h]], writes=self.prA(pt_cols, pt_cols + 512))
                    kb.op("s", lambda s, i=i, pt_cols=pt_cols: s.activation(
                        PT[i][:], pA[:, pt_cols:pt_cols + 512], AF.Exp, scale=0.125),
                        reads=self.prA(pt_cols, pt_cols + 512),
                        writes=[RPT[i]])
                    dk = kc - 4 * qt
                    if dk >= 0:
                        kb.op("v", lambda e, i=i, dk=dk: e.tensor_tensor(
                            PT[i][:], PT[i][:], M0[dk][:], ALU.mult),
                            reads=[RPT[i], RM0[dk]], writes=[RPT[i]])
                    if pend_pv is not None:
                        pend_pv()
                    pend_pv = (lambda i=i, kc=kc, g=g, nkc=nkc, acc0=acc0, par=par:
                               kb.op("t", lambda t, i=i, kc=kc, g=g, nkc=nkc,
                                     acc0=acc0, par=par: t.matmul(
                                   pB[0:65, acc0:acc0 + 512],
                                   VT[kc][:, 65 * g:65 * (g + 1)],
                                   PT[i][:], start=(kc == 0), stop=(kc == nkc - 1)),
                                   reads=[RVT[kc], RPT[i]],
                                   writes=[self.PBANK[4 + par]]))
                pend_pv()
                zr = self.ROW[0:1, 0:512]
                kb.op("v", lambda v, acc0=acc0, par=par: v.tensor_copy(
                    zr, pB[64:65, acc0:acc0 + 512]),
                    reads=[self.PBANK[4 + par]], writes=[self.RROW])
                kb.op("s", lambda s: s.activation(zr, zr, AF.Ln),
                      reads=[self.RROW], writes=[self.RROW])
                kb.op("s", lambda s: s.activation(zr, zr, AF.Exp, scale=-1.0),
                      reads=[self.RROW], writes=[self.RROW])
                kb.op("t", lambda t, bc0=bc0, par=par: t.matmul(
                    pB[0:64, bc0:bc0 + 512], self.c_e0[:, 0:64], self.ROWT[:, 0:512],
                    start=True, stop=True),
                    reads=[self.RROW, self.Re0], writes=[self.PBANK[6 + par]])
                kb.op("s", lambda s, bc0=bc0: s.copy(BCS[:], pB[0:64, bc0:bc0 + 512]),
                      reads=[self.PBANK[6 + par]], writes=[RBCS])
                kb.op("v", lambda v, acc0=acc0, par=par: v.tensor_tensor(
                    TMP64[:], pB[0:64, acc0:acc0 + 512], BCS[:], ALU.mult),
                    reads=[self.PBANK[4 + par], RBCS], writes=[RTMP64])
                c = h // 2
                half = h % 2
                kb.op("v", lambda v, c=c, half=half, qt=qt: v.tensor_copy(
                    ATT[c][64 * half:64 * (half + 1), 512 * qt:512 * (qt + 1)],
                    TMP64[:]),
                    reads=[RTMP64], writes=[RATT[c]])
                it += 1

        self.dbg("att0", ATT[0], RATT[0])
        for h in range(16):
            kb.sfree(f"QH{h}")
        for g in range(4):
            kb.sfree(f"KG{g}")
        for tt in range(16):
            kb.sfree(f"VT{tt}")
        for i in range(4):
            kb.sfree(f"PT{i}")
        for p in range(4):
            kb.sfree(f"M0_{p}")

        wof = [kb.salloc(f"wof{i}", [128, E], F32) for i in range(2)]
        Rwof = [kb.R(f"wof{i}") for i in range(2)]
        wob = [kb.salloc(f"wob{c}", [128, E], BF16) for c in range(NCH)]
        Rwob = [kb.R(f"wob{c}") for c in range(NCH)]
        for c in range(NCH):
            i = c % 2
            kb.dma(lambda q, c=c, i=i: q.dma_start(
                wof[i][:], dr["l0a_Wout"][128 * c:128 * (c + 1), :]), f"dmw0{i}",
                reads=[Rwof[i]], writes=[Rwof[i]])
            kb.op("s", lambda s_, c=c, i=i: s_.copy(wob[c][:], wof[i][:]),
                  reads=[Rwof[i]], writes=[Rwob[c]])
        bout, Rbout = self.bias["l0a_bout"]
        self.X = [kb.salloc(f"X{c}", [128, S], F32) for c in range(NCH)]
        self.RX = [kb.R(f"X{c}") for c in range(NCH)]
        for oc in range(8):
            pt = pA if oc % 2 == 0 else pB
            prf = self.prA if oc % 2 == 0 else self.prB
            kb.dma(lambda q, oc=oc, Xc=self.X[oc]: q.dma_start(Xc[:], self.park_main[oc]),
                   "dmpk", reads=[kb.R("d_pm")], writes=[self.RX[oc]], tag=f"unparkX0{oc}")
            for c in range(NCH):
                for nb in range(4):
                    kb.op("t", lambda t, oc=oc, c=c, nb=nb, pt=pt, prf=prf: t.matmul(
                        pt[:, 512 * nb:512 * (nb + 1)],
                        wob[c][:, 128 * oc:128 * (oc + 1)],
                        ATT[c][:, 512 * nb:512 * (nb + 1)],
                        start=(c == 0), stop=(c == NCH - 1)),
                        reads=[Rwob[c], RATT[c]],
                        writes=prf(512 * nb, 512 * nb + 512))
            kb.op("v", lambda v, oc=oc, Xc=self.X[oc], pt=pt, prf=prf:
                  v.scalar_tensor_tensor(
                      Xc[:], pt[:, 0:S], bout[:, oc:oc + 1], Xc[:],
                      ALU.add, ALU.add),
                  reads=prf(0, S) + [Rbout, self.RX[oc]], writes=[self.RX[oc]])
        for i in range(2):
            kb.sfree(f"wof{i}")
        for c in range(NCH):
            kb.sfree(f"wob{c}")
        for c in range(NCH):
            kb.sfree(f"ATT{c}")
        kb.sfree("TMP64"); kb.sfree("BCS")

    def blk_l0_ffn(self):
        kb, nc, dr, pA, pB = self.kb, self.nc, self.dr, self.pA, self.pB
        INT = [kb.salloc(f"INT{c}", [128, 2052], F32) for c in range(NCH)]
        RINT = [kb.R(f"INT{c}") for c in range(NCH)]
        for c in range(NCH):
            kb.op("g", lambda g_, Ic=INT[c]: g_.memset(Ic[:, 0:1], 0.0), writes=[RINT[c]])

        def intf(c):
            return INT[c][:, 1:1 + S]
        Xs_now = list(self.X)
        self.emit_ln(lambda c: Xs_now[c][:], self.RX, [None] * NCH, [None] * NCH,
                     outf_ap=intf, routf=RINT, tagp="ln1")
        for c in range(NCH):
            kb.dma(lambda q, c=c, Xc=self.X[c]: q.dma_start(self.park_main[c], Xc[:]),
                   "dmpk", reads=[self.RX[c]], writes=[kb.R("d_pm")], tag=f"parkX{c}")
            kb.dma(lambda q, c=c, Ic=INT[c]: q.dma_start(
                self.park_int[c], Ic[:, 1:1 + S]),
                "dmpi", reads=[RINT[c]], writes=[kb.R("d_pi0")], tag=f"parkI{c}")
        for c in range(NCH):
            kb.sfree(f"X{c}")
            kb.sfree(f"INT{c}")

        self.l0f_layer(0, self.park_int, kb.R("d_pi0"), self.park_int2, kb.R("d_pi1"))
        self.l0f_layer(1, self.park_int2, kb.R("d_pi1"), self.park_int, kb.R("d_pi0"))

        self.X = [kb.salloc(f"X{c}", [128, S], F32) for c in range(NCH)]
        self.RX = [kb.R(f"X{c}") for c in range(NCH)]
        FIN = kb.salloc("l0fFIN", [128, S], F32)
        RFIN = kb.R("l0fFIN")
        for c in range(NCH):
            kb.dma(lambda q, c=c, Xc=self.X[c]: q.dma_start(Xc[:], self.park_main[c]),
                   "dmpk", reads=[kb.R("d_pm")], writes=[self.RX[c]], tag=f"unparkX{c}")
            kb.dma(lambda q, c=c: q.dma_start(FIN[:], self.park_int[c]),
                   "dmpi", reads=[kb.R("d_pi0")], writes=[RFIN], tag=f"unparkF{c}")
            kb.op("v", lambda v, Xc=self.X[c]: v.tensor_tensor(
                Xc[:], Xc[:], FIN[:], ALU.add),
                reads=[self.RX[c], RFIN], writes=[self.RX[c]])
        kb.sfree("l0fFIN")

    def l0f_layer(self, layer, park_in, Rpin, park_out, Rpout):
        kb, nc, dr, pA, pB = self.kb, self.nc, self.dr, self.pA, self.pB
        qsb, Rqsb = self.bias["l0f_qs"]
        ksb, Rksb = self.bias["l0f_ks"]
        qsbw, Rqsbw = self.bias["l0f_qs_sw"]
        ksbw, Rksbw = self.bias["l0f_ks_sw"]

        X0B = [kb.salloc(f"X0B{c}", [128, 2052], BF16) for c in range(NCH)]
        RX0B = [kb.R(f"X0B{c}") for c in range(NCH)]
        XLD = kb.salloc("l0fXLD", [128, S], F32)
        RXLD = kb.R("l0fXLD")
        for c in range(NCH):
            kb.op("g", lambda g_, Bc=X0B[c]: g_.memset(Bc[:, 0:1], 0.0), writes=[RX0B[c]])
            kb.dma(lambda q, c=c: q.dma_start(XLD[:], park_in[c]), "dmxb",
                   reads=[Rpin, RXLD], writes=[RXLD], tag=f"x0bld{c}")
            kb.op("s", lambda s, Bc=X0B[c]: s.copy(Bc[:, 1:1 + S], XLD[:]),
                  reads=[RXLD], writes=[RX0B[c]])
        kb.sfree("l0fXLD")

        wst = [kb.salloc(f"wst{i}", [128, 8 * 256], F32) for i in range(2)]
        Rwst = [kb.R(f"wst{i}") for i in range(2)]
        wsb = [kb.salloc(f"wsb{i}", [128, 8 * 256], BF16) for i in range(2)]
        Rwsb = [kb.R(f"wsb{i}") for i in range(2)]
        QP = [kb.salloc(f"l0fQ{c}", [128, S], BF16) for c in range(4)]
        RQP = [kb.R(f"l0fQ{c}") for c in range(4)]
        KP = [kb.salloc(f"l0fK{c}", [128, S], BF16) for c in range(4)]
        RKP = [kb.R(f"l0fK{c}") for c in range(4)]
        VT = [kb.salloc(f"l0fV{tt}", [128, 520], BF16) for tt in range(16)]
        RVT = [kb.R(f"l0fV{tt}") for tt in range(16)]
        for tt in range(16):
            kb.op("g", lambda g_, tt=tt: g_.memset(
                VT[tt][:].rearrange("p (g c) -> p g c", c=65)[:, :, 64:65], 1.0),
                writes=[RVT[tt]])

        def load_wqkv_block(colbase, ncols, i):
            kb.dma(lambda q, colbase=colbase, ncols=ncols, i=i: q.dma_start(
                wst[i][:, 0:8 * ncols].rearrange("p (c n) -> p c n", c=8),
                dr["l0f_wqkv"][layer].rearrange("(c p) n -> p c n", p=128)
                [:, :, colbase:colbase + ncols]),
                f"dmw1q{i}", reads=[Rwst[i]], writes=[Rwst[i]])
            if i == 0:
                kb.op("s", lambda s_, ncols=ncols, i=i: s_.copy(
                    wsb[i][:, 0:8 * ncols], wst[i][:, 0:8 * ncols]),
                    reads=[Rwst[i]], writes=[Rwsb[i]])
            else:
                kb.op("v", lambda v, ncols=ncols, i=i: v.tensor_copy(
                    wsb[i][:, 0:8 * ncols], wst[i][:, 0:8 * ncols]),
                    reads=[Rwst[i]], writes=[Rwsb[i]])

        def qkv_mm(oc_in_block, ncols, pT, prT, i):
            for c in range(NCH):
                off = 1 if c < 4 else 0
                for nb in range(4):
                    kb.op("t", lambda t, c=c, nb=nb, off=off, oc_in_block=oc_in_block,
                          ncols=ncols, pT=pT, prT=prT, i=i: t.matmul(
                        pT[:, 512 * nb:512 * (nb + 1)],
                        wsb[i][:, c * ncols + 128 * oc_in_block:
                               c * ncols + 128 * (oc_in_block + 1)],
                        X0B[c][:, off + 512 * nb: off + 512 * nb + 512],
                        start=(c == 0), stop=(c == NCH - 1)),
                        reads=[Rwsb[i], RX0B[c]],
                        writes=prT(512 * nb, 512 * nb + 512))

        QRAW, RQRAW = self.SC2, self.RSC2
        QSW, RQSW = self.SC3, self.RSC3
        tabA = kb.salloc("tabA", [128, S], F32)
        tabB = kb.salloc("tabB", [128, S], F32)
        RtabA, RtabB = kb.R("tabA"), kb.R("tabB")

        def rope_side(side):
            Anm = "xposAq" if side == 0 else "xposAk"
            Bnm = "xposBq" if side == 0 else "xposBk"
            kb.dma(lambda q, Anm=Anm: q.dma_start(tabA[:], dr[Anm]), "dmtab",
                   reads=[RtabA], writes=[RtabA])
            kb.dma(lambda q, Bnm=Bnm: q.dma_start(tabB[:], dr[Bnm]), "dmtab",
                   reads=[RtabB], writes=[RtabB])
            scol = qsb if side == 0 else ksb
            Rscol = Rqsb if side == 0 else Rksb
            scolw = qsbw if side == 0 else ksbw
            Rscolw = Rqsbw if side == 0 else Rksbw
            OUT = QP if side == 0 else KP
            ROUT = RQP if side == 0 else RKP
            for blk in range(2):
                colbase = side * 512 + blk * 256
                wi = (2 * side + blk) % 2
                load_wqkv_block(colbase, 256, wi)
                for oc2 in range(2):
                    ch = blk * 2 + oc2
                    par = (2 * blk + oc2 + 2 * side) % 2
                    pQ = pA if par == 0 else pB
                    prQ = self.prA if par == 0 else self.prB
                    pN = pB if par == 0 else pA
                    prN = self.prB if par == 0 else self.prA
                    qkv_mm(oc2, 256, pQ, prQ, wi)
                    kb.op("s", lambda s, pQ=pQ, prQ=prQ: s.copy(QRAW[:], pQ[:, 0:S]),
                          reads=prQ(0, S), writes=[RQRAW])
                    kb.op("s", lambda s, pQ=pQ, prQ=prQ: s.activation(
                        self.SCa[:], pQ[:, 0:S], AF.Square),
                        reads=prQ(0, S), writes=[self.RSCa])
                    for nb in range(4):
                        kb.op("t", lambda t, nb=nb, pN=pN, prN=prN: t.matmul(
                            pN[:, 512 * nb:512 * (nb + 1)], self.c_blkb[:],
                            self.SCa[:, 512 * nb:512 * (nb + 1)], start=True, stop=True),
                            reads=[self.RSCa, self.Rblkb],
                            writes=prN(512 * nb, 512 * nb + 512))
                    kb.op("v", lambda v, pN=pN, prN=prN: v.tensor_scalar(
                        self.SC1[:], pN[:, 0:S], 5.4e-20, None, ALU.max),
                        reads=prN(0, S), writes=[self.RSC1])
                    kb.op("s", lambda s: s.activation(
                        self.SC1[:], self.SC1[:], AF.Ln),
                        reads=[self.RSC1], writes=[self.RSC1])
                    kb.op("s", lambda s: s.activation(
                        self.SC1[:], self.SC1[:], AF.Exp, scale=-0.5),
                        reads=[self.RSC1], writes=[self.RSC1])
                    for b0, b1 in ((0, 32), (32, 0), (64, 96), (96, 64)):
                        kb.dma(lambda q, b0=b0, b1=b1: q.dma_start(
                            QSW[b0:b0 + 32, :], QRAW[b1:b1 + 32, :]), "dmsw",
                            reads=[RQRAW, RQSW], writes=[RQSW])
                    kb.op("v", lambda v, scol=scol: v.scalar_tensor_tensor(
                        QRAW[:], QRAW[:], scol[:, layer:layer + 1], tabA[:],
                        ALU.mult, ALU.mult),
                        reads=[RQRAW, Rscol, RtabA], writes=[RQRAW])
                    kb.op("v", lambda v, scolw=scolw: v.scalar_tensor_tensor(
                        QSW[:], QSW[:], scolw[:, layer:layer + 1], tabB[:],
                        ALU.mult, ALU.mult),
                        reads=[RQSW, Rscolw, RtabB], writes=[RQSW])
                    kb.op("g", lambda g_: g_.tensor_tensor(
                        self.SCb[:], QRAW[:], QSW[:], ALU.add),
                        reads=[RQRAW, RQSW], writes=[self.RSCb])
                    kb.op("v", lambda v, OUT=OUT, ch=ch: v.tensor_tensor(
                        OUT[ch][:], self.SCb[:], self.SC1[:], ALU.mult),
                        reads=[self.RSCb, self.RSC1], writes=[ROUT[ch]])

        rope_side(0)
        rope_side(1)
        if layer == 0:
            self.dbg("l0fq0", QP[0], RQP[0])
            self.dbg("l0fk0", KP[0], RKP[0])

        for half in range(2):
            load_wqkv_block(1024 + 256 * half, 256, half)
            for tt in range(16):
                col0 = 512 * (tt % 4)
                for c in range(NCH):
                    off = 1 if c < 4 else 0
                    kb.op("t", lambda t, tt=tt, c=c, off=off, col0=col0, half=half:
                          t.matmul(
                        pA[:, col0:col0 + 256],
                        X0B[c][:, off + 128 * tt: off + 128 * tt + 128],
                        wsb[half][:, 256 * c:256 * (c + 1)],
                        start=(c == 0), stop=(c == NCH - 1)),
                        reads=[RX0B[c], Rwsb[half]], writes=self.prA(col0, col0 + 256))
                kb.op("v", lambda v, tt=tt, col0=col0, half=half: v.tensor_copy(
                    VT[tt][:].rearrange("p (g c) -> p g c", c=65)
                    [:, 4 * half:4 * (half + 1), 0:64],
                    pA[:, col0:col0 + 256].rearrange("p (g c) -> p g c", c=64)),
                    reads=self.prA(col0, col0 + 256), writes=[RVT[tt]])
        for i in range(2):
            kb.sfree(f"wst{i}")
            kb.sfree(f"wsb{i}")
        kb.sfree("tabA"); kb.sfree("tabB")
        for c in range(NCH):
            kb.sfree(f"X0B{c}")

        mstg = kb.salloc("l0fmstg", [128, 1024], F32)
        Rmstg = kb.R("l0fmstg")
        M1 = [kb.salloc(f"M1_{p}", [128, 512], BF16) for p in range(8)]
        RM1 = [kb.R(f"M1_{p}") for p in range(8)]
        for p in range(8):
            kb.dma(lambda q, p=p: q.dma_start(mstg[:, 0:512], dr[f"mask1_{p}"]), "dmmask",
                   reads=[Rmstg], writes=[Rmstg])
            kb.op("v", lambda v, p=p: v.tensor_copy(M1[p][:], mstg[:, 0:512]),
                  reads=[Rmstg], writes=[RM1[p]])
        MU = kb.salloc("MU", [128, 128], F32)
        RMU = kb.R("MU")
        MD = kb.salloc("MD", [128, 128], F32)
        RMD = kb.R("MD")
        kb.dma(lambda q: q.dma_start(MU[:], dr["mask1U"]), "dmmask", writes=[RMU])
        kb.dma(lambda q: q.dma_start(MD[:], dr["mask1D"]), "dmmask", writes=[RMD])
        kb.sfree("l0fmstg")

        ATT = [kb.salloc(f"l0fA{c}", [128, S], BF16) for c in range(4)]
        RATT = [kb.R(f"l0fA{c}") for c in range(4)]
        PT = [kb.salloc(f"l0fPT{i}", [128, 512], B16R) for i in range(2)]
        RPT = [kb.R(f"l0fPT{i}") for i in range(2)]
        TMP64 = kb.salloc("l0fT64", [64, 512], BF16)
        RTMP64 = kb.R("l0fT64")
        BCS = kb.salloc("l0fBCS", [64, 512], F32)
        RBCS = kb.R("l0fBCS")
        SCbb = kb.salloc("SCbb", [128, 256], BF16)
        RSCbb = kb.R("SCbb")
        MRB = kb.salloc("MRB", [128, 512], BF16)
        RMRB = kb.R("MRB")
        kb.op("g", lambda g_: g_.memset(MRB[:], 0.0), writes=[RMRB])

        MCOLB = kb.salloc("MCOLB", [128, 4], F32)
        RMCOLB = kb.R("MCOLB")
        iters = [(h, w) for h in range(8) for w in range(4)]

        def emit_spass(it2):
            h, w = iters[it2]
            ch = h // 2
            base = 64 * (h % 2)
            MC = self.MCOL if it2 % 2 == 0 else MCOLB
            RMC = self.RMCOL if it2 % 2 == 0 else RMCOLB
            for qs4 in range(4):
                q0 = 512 * w + 128 * qs4
                soff = 1024 * (qs4 % 2)
                sbanks = [self.PBANK[2 * (qs4 % 2)], self.PBANK[2 * (qs4 % 2) + 1]]
                if w == 0:
                    blocks = [(128 * kc, 128 * kc) for kc in range(qs4 + 1)]
                    has_lower = False
                else:
                    blocks = [(512 * (w - 1) + 128 * (qs4 + j), 128 * j)
                              for j in range(5)]
                    has_lower = True
                for (ctx0, dcol) in blocks:
                    kb.op("t", lambda t, ch=ch, base=base, q0=q0, ctx0=ctx0,
                          dcol=dcol, soff=soff: t.matmul(
                        pA[:, soff + dcol:soff + dcol + 128],
                        QP[ch][base:base + 64, q0:q0 + 128],
                        KP[ch][base:base + 64, ctx0:ctx0 + 128],
                        start=True, stop=True),
                        reads=[RQP[ch], RKP[ch]], writes=sbanks)
                upcol = soff + blocks[-1][1]
                mid_lo = soff + (128 if has_lower else 0)
                mid_hi = soff + blocks[-1][1]
                have_mid = mid_hi > mid_lo
                if have_mid:
                    kb.op("v", lambda v, qs4=qs4, mid_lo=mid_lo, mid_hi=mid_hi,
                          MC=MC: v.tensor_scalar(
                              self.SCb[:, 0:mid_hi - mid_lo], pA[:, mid_lo:mid_hi],
                              0.0, -3.0e38, ALU.add, ALU.max,
                              accum_out=MC[:, qs4:qs4 + 1]),
                          reads=sbanks, writes=[self.RSCb, RMC])
                kb.op("v", lambda v, upcol=upcol: v.tensor_tensor(
                    SCbb[:, 0:128], pA[:, upcol:upcol + 128], MU[:], ALU.add),
                    reads=sbanks + [RMU], writes=[RSCbb])
                nbd = 128
                if has_lower:
                    kb.op("v", lambda v, soff=soff: v.tensor_tensor(
                        SCbb[:, 128:256], pA[:, soff:soff + 128], MD[:], ALU.add),
                        reads=sbanks + [RMD], writes=[RSCbb])
                    nbd = 256
                if have_mid:
                    kb.op("v", lambda v, qs4=qs4, nbd=nbd, MC=MC: v.tensor_scalar(
                        self.SCb[:, 0:nbd], SCbb[:, 0:nbd], 0.0,
                        MC[:, qs4:qs4 + 1], ALU.add, ALU.max,
                        accum_out=MC[:, qs4:qs4 + 1]),
                        reads=[RSCbb, RMC], writes=[self.RSCb, RMC])
                else:
                    kb.op("v", lambda v, qs4=qs4, nbd=nbd, MC=MC: v.tensor_scalar(
                        self.SCb[:, 0:nbd], SCbb[:, 0:nbd], 0.0, -3.0e38,
                        ALU.add, ALU.max,
                        accum_out=MC[:, qs4:qs4 + 1]),
                        reads=[RSCbb], writes=[self.RSCb, RMC])

        emit_spass(0)
        for it in range(len(iters)):
            h, w = iters[it]
            ch = h // 2
            base = 64 * (h % 2)
            par = it % 2
            acc0 = 512 * par
            bc0 = 512 * (1 - par)
            MC = self.MCOL if it % 2 == 0 else MCOLB
            RMC = self.RMCOL if it % 2 == 0 else RMCOLB
            if it + 1 < len(iters):
                emit_spass(it + 1)
            for qs4 in range(4):
                kb.op("t", lambda t, qs4=qs4, MC=MC: t.transpose(
                    pB[0:1, 1024 + 128 * qs4:1024 + 128 * (qs4 + 1)],
                    MC[:, qs4:qs4 + 1], self.c_ident[:]),
                    reads=[RMC, self.Rid], writes=[self.PBANK[6]])
            kb.op("s", lambda s: s.activation(MRB[0:1, :], pB[0:1, 1024:1536],
                                              AF.Copy, scale=-1.0),
                  reads=[self.PBANK[6]], writes=[RMRB])
            kcs = list(range(8)) if w > 0 else list(range(4, 8))
            pend_pv = None
            for kc in kcs:
                i = kc % 2
                ctx0 = 512 * (w - 1) + 128 * kc
                pt_cols = 1024 + 512 * i
                pt_bank = [self.PBANK[6 + i]]
                kb.op("t", lambda t, ch=ch, base=base, ctx0=ctx0, w=w,
                      pt_cols=pt_cols, pt_bank=pt_bank: t.matmul(
                    pB[:, pt_cols:pt_cols + 512],
                    KP[ch][base:base + 64, ctx0:ctx0 + 128],
                    QP[ch][base:base + 64, 512 * w:512 * (w + 1)],
                    start=True, stop=False),
                    reads=[RKP[ch], RQP[ch]], writes=pt_bank)
                kb.op("t", lambda t, pt_cols=pt_cols, pt_bank=pt_bank: t.matmul(
                    pB[:, pt_cols:pt_cols + 512], self.c_e0b[:, 0:128], MRB[:],
                    start=False, stop=True),
                    reads=[self.Re0b, RMRB], writes=pt_bank)
                kb.op("s", lambda s, i=i, pt_cols=pt_cols, pt_bank=pt_bank:
                      s.activation(PT[i][:], pB[:, pt_cols:pt_cols + 512], AF.Exp),
                      reads=pt_bank, writes=[RPT[i]])
                kb.op("v", lambda e, i=i, kc=kc: e.tensor_tensor(
                    PT[i][:], PT[i][:], M1[kc][:], ALU.mult),
                    reads=[RPT[i], RM1[kc]], writes=[RPT[i]])
                tt = ((512 * (w - 1) + 128 * kc) // 128)
                if pend_pv is not None:
                    pend_pv()
                pend_pv = (lambda i=i, tt=tt, h=h, kc=kc, kcs=kcs, acc0=acc0,
                           par=par: kb.op("t", lambda t, i=i, tt=tt, h=h, kc=kc,
                                          kcs=kcs, acc0=acc0, par=par: t.matmul(
                               pB[0:65, acc0:acc0 + 512],
                               VT[tt][:, 65 * h:65 * (h + 1)],
                               PT[i][:], start=(kc == kcs[0]), stop=(kc == kcs[-1])),
                               reads=[RVT[tt], RPT[i]],
                               writes=[self.PBANK[4 + par]]))
            pend_pv()
            zr = self.ROW[0:1, 0:512]
            kb.op("v", lambda v, acc0=acc0, par=par: v.tensor_copy(
                zr, pB[64:65, acc0:acc0 + 512]),
                reads=[self.PBANK[4 + par]], writes=[self.RROW])
            kb.op("s", lambda s: s.activation(zr, zr, AF.Ln),
                  reads=[self.RROW], writes=[self.RROW])
            kb.op("s", lambda s: s.activation(zr, zr, AF.Exp, scale=-1.0),
                  reads=[self.RROW], writes=[self.RROW])
            kb.op("t", lambda t, bc0=bc0, par=par: t.matmul(
                pB[0:64, bc0:bc0 + 512], self.c_e0[:, 0:64], self.ROWT[:, 0:512],
                start=True, stop=True),
                reads=[self.RROW, self.Re0], writes=[self.PBANK[5 - par]])
            kb.op("s", lambda s, bc0=bc0: s.copy(BCS[:], pB[0:64, bc0:bc0 + 512]),
                  reads=[self.PBANK[5 - par]], writes=[RBCS])
            kb.op("v", lambda v, acc0=acc0, par=par: v.tensor_tensor(
                TMP64[:], pB[0:64, acc0:acc0 + 512], BCS[:], ALU.mult),
                reads=[self.PBANK[4 + par], RBCS], writes=[RTMP64])
            kb.op("v", lambda v, ch=ch, base=base, w=w: v.tensor_copy(
                ATT[ch][base:base + 64, 512 * w:512 * (w + 1)], TMP64[:]),
                reads=[RTMP64], writes=[RATT[ch]])

        if layer == 0:
            self.dbg("l0fatt0", ATT[0], RATT[0])
            self.dbg("l0fvt0", VT[0], RVT[0], cols=520)
        for c in range(4):
            kb.sfree(f"l0fQ{c}")
            kb.sfree(f"l0fK{c}")
        for tt in range(16):
            kb.sfree(f"l0fV{tt}")
        for i in range(2):
            kb.sfree(f"l0fPT{i}")
        for p in range(8):
            kb.sfree(f"M1_{p}")
        kb.sfree("MU"); kb.sfree("MD"); kb.sfree("SCbb"); kb.sfree("MCOLB")
        kb.sfree("l0fT64"); kb.sfree("l0fBCS"); kb.sfree("MRB")

        INT2 = [kb.salloc(f"INT2_{c}", [128, 2052], F32) for c in range(NCH)]
        RINT2 = [kb.R(f"INT2_{c}") for c in range(NCH)]
        wof = kb.salloc("l0fwof", [128, E], F32)
        Rwof = kb.R("l0fwof")
        wob = [kb.salloc(f"l0fwob{c}", [128, E], BF16) for c in range(4)]
        Rwob = [kb.R(f"l0fwob{c}") for c in range(4)]
        for c in range(4):
            kb.dma(lambda q, c=c: q.dma_start(
                wof[:], dr["l0f_wo"][layer][128 * c:128 * (c + 1), :]), "dmwo",
                reads=[Rwof], writes=[Rwof])
            kb.op("v", lambda v, c=c: v.tensor_copy(wob[c][:], wof[:]),
                  reads=[Rwof], writes=[Rwob[c]])
        for oc in range(8):
            kb.op("g", lambda g_, Ic=INT2[oc]: g_.memset(Ic[:, 0:1], 0.0),
                  writes=[RINT2[oc]])
            kb.dma(lambda q, oc=oc, Ic=INT2[oc]: q.dma_start(
                Ic[:, 1:1 + S], park_in[oc]), "dmxl",
                reads=[Rpin], writes=[RINT2[oc]], tag=f"unpkwo{oc}")
            for c in range(4):
                for nb in range(4):
                    kb.op("t", lambda t, oc=oc, c=c, nb=nb: t.matmul(
                        pA[:, 512 * nb:512 * (nb + 1)],
                        wob[c][:, 128 * oc:128 * (oc + 1)],
                        ATT[c][:, 512 * nb:512 * (nb + 1)],
                        start=(c == 0), stop=(c == 3)),
                        reads=[Rwob[c], RATT[c]],
                        writes=self.prA(512 * nb, 512 * nb + 512))
            kb.op("v", lambda v, Ic=INT2[oc]: v.tensor_tensor(
                Ic[:, 1:1 + S], Ic[:, 1:1 + S], pA[:, 0:S], ALU.add),
                reads=[RINT2[oc]] + self.prA(0, S), writes=[RINT2[oc]])
        kb.sfree("l0fwof")
        for c in range(4):
            kb.sfree(f"l0fwob{c}")
        for c in range(4):
            kb.sfree(f"l0fA{c}")

        LN2 = [kb.salloc(f"LN2_{c}", [128, S], BF16) for c in range(NCH)]
        RLN2 = [kb.R(f"LN2_{c}") for c in range(NCH)]
        lng, Rlng = self.bias["l0f_lng"]
        lnb, Rlnb = self.bias["l0f_lnb"]

        if layer == 0:
            self.dbg("l0fint0", INT2[0], RINT2[0], cols=2052)

        def srcap(c):
            off = 0 if c >= 4 else 1
            return INT2[c][:, off:off + S]
        self.emit_ln(srcap, RINT2, LN2, RLN2,
                     gb=(lambda c: lng[:, 8 * layer + c:8 * layer + c + 1],
                         lambda c: lnb[:, 8 * layer + c:8 * layer + c + 1],
                         Rlng), tagp=f"ln2_{layer}")
        for c in range(NCH):
            kb.dma(lambda q, c=c, Ic=INT2[c]: q.dma_start(
                park_out[c], Ic[:, 1:1 + S]),
                "dmpi", reads=[RINT2[c]], writes=[Rpout], tag=f"parkO{c}")
        for c in range(NCH):
            kb.sfree(f"INT2_{c}")

        GCOLS = [1024, 1024, 682]
        GOFF = [0, 1024, 2048]
        w1f = [kb.salloc(f"w1f{i}", [128, 1024], F32) for i in range(2)]
        Rw1f = [kb.R(f"w1f{i}") for i in range(2)]
        W1A = [[kb.salloc(f"W1A{g}_{c}", [128, GCOLS[g]], BF16) for c in range(NCH)]
               for g in range(3)]
        RW1A = [[kb.R(f"W1A{g}_{c}") for c in range(NCH)] for g in range(3)]
        W1G = [[kb.salloc(f"W1G{g}_{c}", [128, GCOLS[g]], BF16) for c in range(NCH)]
               for g in range(3)]
        RW1G = [[kb.R(f"W1G{g}_{c}") for c in range(NCH)] for g in range(3)]
        ldi = 0
        for g in range(3):
            for nmh, DSTg, RDSTg in (("l0f_w1a", W1A, RW1A),
                                     ("l0f_w1g", W1G, RW1G)):
                for c in range(NCH):
                    i = ldi % 2
                    ldi += 1
                    kb.dma(lambda q, c=c, i=i, g=g, nmh=nmh: q.dma_start(
                        w1f[i][:, 0:GCOLS[g]],
                        dr[nmh][layer][128 * c:128 * (c + 1),
                                       GOFF[g]:GOFF[g] + GCOLS[g]]),
                        f"dmw1a{i}", reads=[Rw1f[i]], writes=[Rw1f[i]])
                    if i == 0:
                        kb.op("s", lambda s_, i=i, g=g, D=DSTg[g][c]: s_.copy(
                            D[:], w1f[i][:, 0:GCOLS[g]]),
                            reads=[Rw1f[i]], writes=[RDSTg[g][c]])
                    else:
                        kb.op("v", lambda v, i=i, g=g, D=DSTg[g][c]: v.tensor_copy(
                            D[:], w1f[i][:, 0:GCOLS[g]]),
                            reads=[Rw1f[i]], writes=[RDSTg[g][c]])
        for i in range(2):
            kb.sfree(f"w1f{i}")

        GT = [kb.salloc(f"GT{j}", [128, 512], BF16) for j in range(22)]
        RGT = [kb.R(f"GT{j}") for j in range(22)]
        kb.op("g", lambda g_: g_.memset(GT[21][:], 0.0), writes=[RGT[21]])
        w2f = [kb.salloc(f"w2f{i}", [128, 512], F32) for i in range(2)]
        Rw2f = [kb.R(f"w2f{i}") for i in range(2)]
        w2b = [kb.salloc(f"w2b{i}", [128, E], BF16) for i in range(2)]
        Rw2b = [kb.R(f"w2b{i}") for i in range(2)]
        PINB = self.SC1[:, 0:512]
        RPINB = self.RSC1
        OUTB = self.SC2[:, 0:512]
        ROUTB = self.RSC2
        b1a, Rb1a = self.bias["l0f_b1a"]
        b1g, Rb1g = self.bias["l0f_b1g"]
        b2c, Rb2 = self.bias["l0f_b2"]

        for Tt in range(4):
            t0 = 512 * Tt
            for j in range(22):
                m = 128 if j < 21 else 42
                g = j // 8
                jo = 128 * (j - 8 * g)
                pcol = 1024 * (j % 2)
                bkA, bkG = self.PBANK[2 * (j % 2)], self.PBANK[2 * (j % 2) + 1]
                for c in range(NCH):
                    kb.op("t", lambda t, j=j, c=c, m=m, t0=t0, pcol=pcol, g=g, jo=jo:
                          t.matmul(
                        pA[0:m, pcol:pcol + 512], W1A[g][c][:, jo:jo + m],
                        LN2[c][:, t0:t0 + 512], start=(c == 0), stop=(c == NCH - 1)),
                        reads=[RW1A[g][c], RLN2[c]], writes=[bkA])
                for c in range(NCH):
                    kb.op("t", lambda t, j=j, c=c, m=m, t0=t0, pcol=pcol, g=g, jo=jo:
                          t.matmul(
                        pA[0:m, pcol + 512:pcol + 1024], W1G[g][c][:, jo:jo + m],
                        LN2[c][:, t0:t0 + 512], start=(c == 0), stop=(c == NCH - 1)),
                        reads=[RW1G[g][c], RLN2[c]], writes=[bkG])
                kb.op("s", lambda s, j=j, m=m, pcol=pcol: s.activation(
                    self.SCa[0:m, 0:512], pA[0:m, pcol + 512:pcol + 1024], AF.Gelu,
                    bias=b1g[0:m, 22 * layer + j:22 * layer + j + 1]),
                    reads=[bkG, Rb1g], writes=[self.RSCa])
                kb.op("v", lambda v, j=j, m=m, pcol=pcol: v.scalar_tensor_tensor(
                    GT[j][0:m, :], pA[0:m, pcol:pcol + 512],
                    b1a[0:m, 22 * layer + j:22 * layer + j + 1],
                    self.SCa[0:m, 0:512], ALU.add, ALU.mult),
                    reads=[bkA, Rb1a, self.RSCa], writes=[RGT[j]])

            for c22 in range(22):
                m = 128 if c22 < 21 else 42
                i = c22 % 2
                for hh in range(2):
                    kb.dma(lambda q, c22=c22, m=m, hh=hh: q.dma_start(
                        w2f[hh][0:m, :],
                        dr["l0f_w2"][layer][128 * c22:128 * c22 + m,
                                            512 * hh:512 * (hh + 1)]),
                        f"dmw2{hh}", reads=[Rw2f[hh]], writes=[Rw2f[hh]])
                    kb.op("s", lambda s_, m=m, i=i, hh=hh: s_.copy(
                        w2b[i][0:m, 512 * hh:512 * (hh + 1)], w2f[hh][0:m, :]),
                        reads=[Rw2f[hh]], writes=[Rw2b[i]])
                for oc in range(8):
                    pt = pA if oc < 4 else pB
                    prf = self.prA if oc < 4 else self.prB
                    col0 = 512 * (oc % 4)
                    kb.op("t", lambda t, c22=c22, oc=oc, m=m, pt=pt, col0=col0, i=i:
                          t.matmul(pt[:, col0:col0 + 512],
                                   w2b[i][0:m, 128 * oc:128 * (oc + 1)],
                                   GT[c22][0:m, :], start=(c22 == 0), stop=(c22 == 21)),
                          reads=[Rw2b[i], RGT[c22]], writes=prf(col0, col0 + 512))
            for oc in range(8):
                kb.dma(lambda q, oc=oc, t0=t0: q.dma_start(
                    PINB[:], park_out[oc][:, t0:t0 + 512]), "dmfi",
                    reads=[Rpout, RPINB], writes=[RPINB], tag=f"ffnin{oc}_{Tt}")
                pt = pA if oc < 4 else pB
                prf = self.prA if oc < 4 else self.prB
                col0 = 512 * (oc % 4)
                kb.op("v", lambda v, oc=oc, pt=pt, col0=col0: v.scalar_tensor_tensor(
                    OUTB[:], pt[:, col0:col0 + 512],
                    b2c[:, 8 * layer + oc:8 * layer + oc + 1],
                    PINB[:], ALU.add, ALU.add),
                    reads=prf(col0, col0 + 512) + [Rb2, RPINB], writes=[ROUTB])
                kb.dma(lambda q, oc=oc, t0=t0: q.dma_start(
                    park_out[oc][:, t0:t0 + 512], OUTB[:]), "dmpi",
                    reads=[ROUTB], writes=[Rpout], tag=f"ffnout{oc}_{Tt}")
        for j in range(22):
            kb.sfree(f"GT{j}")
        for g in range(3):
            for c in range(NCH):
                kb.sfree(f"W1A{g}_{c}")
                kb.sfree(f"W1G{g}_{c}")
        for i in range(2):
            kb.sfree(f"w2f{i}")
            kb.sfree(f"w2b{i}")

        for c in range(NCH):
            kb.sfree(f"LN2_{c}")

    def blk_l1_attn(self):
        kb, nc, dr, pA, pB = self.kb, self.nc, self.dr, self.pA, self.pB
        LNO = [kb.salloc(f"cLN{c}", [128, S], BF16) for c in range(NCH)]
        RLNO = [kb.R(f"cLN{c}") for c in range(NCH)]
        Xs_now = list(self.X)
        self.emit_ln(lambda c: Xs_now[c][:], self.RX, LNO, RLNO, tagp="ln2m")
        for c in range(NCH):
            kb.dma(lambda q, c=c, Xc=self.X[c]: q.dma_start(self.park_main[c], Xc[:]),
                   "dmpk", reads=[self.RX[c]], writes=[kb.R("d_pm")], tag=f"parkX2{c}")
        for c in range(NCH):
            kb.sfree(f"X{c}")

        wf = [kb.salloc(f"l1wf{i}", [128, E], F32) for i in range(2)]
        Rwf = [kb.R(f"l1wf{i}") for i in range(2)]
        wb = [kb.salloc(f"l1wb{c}", [128, E], BF16) for c in range(NCH)]
        Rwb = [kb.R(f"l1wb{c}") for c in range(NCH)]

        def loadW(nm):
            for c in range(NCH):
                i = c % 2
                kb.dma(lambda q, c=c, nm=nm, wfx=wf[i], i=i: q.dma_start(
                    wfx[:], dr[nm][128 * c:128 * (c + 1), :]), f"dmw3{i}",
                    reads=[Rwf[i]], writes=[Rwf[i]])
                kb.op("g", lambda g_, wfx=wf[i], wbx=wb[c], i=i: g_.tensor_copy(
                    wbx[:], wfx[:]),
                    reads=[Rwf[i]], writes=[Rwb[c]])

        Q = [kb.salloc(f"l1Q{c}", [128, S], BF16) for c in range(NCH)]
        RQ = [kb.R(f"l1Q{c}") for c in range(NCH)]
        K = [kb.salloc(f"l1K{c}", [128, S], BF16) for c in range(NCH)]
        RK = [kb.R(f"l1K{c}") for c in range(NCH)]
        VT = [kb.salloc(f"l1V{tt}", [128, 1040], BF16) for tt in range(16)]
        RVT = [kb.R(f"l1V{tt}") for tt in range(16)]
        loadW("l1a_Wq")
        for oc in range(8):
            for c in range(NCH):
                for nb in range(4):
                    kb.op("t", lambda t, oc=oc, c=c, nb=nb, wbx=wb[c]: t.matmul(
                        pA[:, 512 * nb:512 * (nb + 1)], wbx[:, 128 * oc:128 * (oc + 1)],
                        LNO[c][:, 512 * nb:512 * (nb + 1)],
                        start=(c == 0), stop=(c == NCH - 1)),
                        reads=[Rwb[c], RLNO[c]], writes=self.prA(512 * nb, 512 * nb + 512))
            if oc % 2 == 0:
                kb.op("v", lambda v, oc=oc: v.tensor_copy(Q[oc][:], pA[:, 0:S]),
                      reads=self.prA(0, S), writes=[RQ[oc]])
            else:
                kb.op("s", lambda s, oc=oc: s.copy(Q[oc][:], pA[:, 0:S]),
                      reads=self.prA(0, S), writes=[RQ[oc]])
        loadW("l1a_Wk")
        for oc in range(8):
            for c in range(NCH):
                for nb in range(4):
                    kb.op("t", lambda t, oc=oc, c=c, nb=nb, wbx=wb[c]: t.matmul(
                        pB[:, 512 * nb:512 * (nb + 1)], wbx[:, 128 * oc:128 * (oc + 1)],
                        LNO[c][:, 512 * nb:512 * (nb + 1)],
                        start=(c == 0), stop=(c == NCH - 1)),
                        reads=[Rwb[c], RLNO[c]], writes=self.prB(512 * nb, 512 * nb + 512))
            if oc % 2 == 0:
                kb.op("s", lambda s, oc=oc: s.copy(K[oc][:], pB[:, 0:S]),
                      reads=self.prB(0, S), writes=[RK[oc]])
            else:
                kb.op("v", lambda v, oc=oc: v.tensor_copy(K[oc][:], pB[:, 0:S]),
                      reads=self.prB(0, S), writes=[RK[oc]])
        loadW("l1a_Wv")
        for tt in range(16):
            kb.op("g", lambda g_, tt=tt: g_.memset(
                VT[tt][:].rearrange("p (g c) -> p g c", c=65)[:, :, 64:65], 1.0),
                writes=[RVT[tt]])
            for half in range(2):
                col0 = 512 * ((2 * tt + half) % 4)
                for c in range(NCH):
                    kb.op("t", lambda t, tt=tt, c=c, half=half, col0=col0, wbx=wb[c]:
                          t.matmul(
                        pA[:, col0:col0 + 512],
                        LNO[c][:, 128 * tt:128 * (tt + 1)],
                        wbx[:, 512 * half:512 * (half + 1)],
                        start=(c == 0), stop=(c == NCH - 1)),
                        reads=[RLNO[c], Rwb[c]], writes=self.prA(col0, col0 + 512))
                kb.op("v", lambda v, tt=tt, half=half, col0=col0: v.tensor_copy(
                    VT[tt][:].rearrange("p (g c) -> p g c", c=65)
                    [:, 8 * half:8 * (half + 1), 0:64],
                    pA[:, col0:col0 + 512].rearrange("p (g c) -> p g c", c=64)),
                    reads=self.prA(col0, col0 + 512), writes=[RVT[tt]])
        for c in range(NCH):
            kb.sfree(f"cLN{c}")

        prj = kb.salloc("prj", [128, 256], BF16)
        Rprj = kb.R("prj")
        kb.dma(lambda q, wfx=wf[0]: q.dma_start(wfx[0:64, 0:256], dr["l1a_proj"]),
               "dmw30", reads=[Rwf[0]], writes=[Rwf[0]])
        kb.op("v", lambda v, wfx=wf[0]: v.tensor_copy(prj[0:64, :], wfx[0:64, 0:256]),
              reads=[Rwf[0]], writes=[Rprj])
        kb.op("v", lambda v: v.tensor_copy(prj[64:128, :], prj[0:64, :]),
              reads=[Rprj], writes=[Rprj])
        for i in range(2):
            kb.sfree(f"l1wf{i}")
        for c in range(NCH):
            kb.sfree(f"l1wb{c}")

        ATT = [kb.salloc(f"l1A{c}", [128, S], BF16) for c in range(NCH)]
        RATT = [kb.R(f"l1A{c}") for c in range(NCH)]
        QP = [kb.salloc(f"l1QP{f}", [128, S], BF16) for f in range(2)]
        RQP = [kb.R(f"l1QP{f}") for f in range(2)]
        KPT = [kb.salloc(f"l1KPT{tt}", [128, 256], BF16) for tt in range(16)]
        RKPT = [kb.R(f"l1KPT{tt}") for tt in range(16)]
        KVB = kb.salloc("l1KVB", [128, 130], BF16)
        RKVB = kb.R("l1KVB")
        BCS = self.SC1[0:64, :]
        RBCS = self.RSC1
        TMP64 = self.SCa[0:64, :]
        RTMP64 = self.RSCa

        for h in range(16):
            ch = h // 2
            base = 64 * (h % 2)
            for f2 in range(2):
                for nb in range(4):
                    kb.op("t", lambda t, ch=ch, base=base, f2=f2, nb=nb: t.matmul(
                        pA[:, 512 * nb:512 * (nb + 1)],
                        prj[base:base + 64, 128 * f2:128 * (f2 + 1)],
                        Q[ch][base:base + 64, 512 * nb:512 * (nb + 1)],
                        start=True, stop=True),
                        reads=[Rprj, RQ[ch]], writes=self.prA(512 * nb, 512 * nb + 512))
                kb.op("s", lambda s, f2=f2: s.activation(QP[f2][:], pA[:, 0:S], AF.Relu),
                      reads=self.prA(0, S), writes=[RQP[f2]])
            for tt in range(16):
                col0 = 512 * (tt % 2) + 256 * 0
                kb.op("t", lambda t, ch=ch, base=base, tt=tt, col0=col0: t.matmul(
                    pB[:, col0:col0 + 256], K[ch][base:base + 64, 128 * tt:128 * (tt + 1)],
                    prj[base:base + 64, :], start=True, stop=True),
                    reads=[RK[ch], Rprj], writes=self.prB(col0, col0 + 256))
                if tt % 2 == 0:
                    kb.op("s", lambda s, tt=tt, col0=col0: s.activation(
                        KPT[tt][:], pB[:, col0:col0 + 256], AF.Relu),
                        reads=self.prB(col0, col0 + 256), writes=[RKPT[tt]])
                else:
                    kb.op("v", lambda v, tt=tt, col0=col0: v.tensor_scalar(
                        KPT[tt][:], pB[:, col0:col0 + 256], 0.0, None, ALU.max),
                        reads=self.prB(col0, col0 + 256), writes=[RKPT[tt]])
            for f2 in range(2):
                for tt in range(16):
                    kb.op("t", lambda t, f2=f2, tt=tt, h=h: t.matmul(
                        pB[:, 1024:1024 + 65], KPT[tt][:, 128 * f2:128 * (f2 + 1)],
                        VT[tt][:, 65 * h:65 * (h + 1)],
                        start=(tt == 0), stop=(tt == 15)),
                        reads=[RKPT[tt], RVT[tt]], writes=[self.PBANK[6]])
                kb.op("v", lambda v, f2=f2: v.tensor_copy(
                    KVB[:, 65 * f2:65 * (f2 + 1)], pB[:, 1024:1024 + 65]),
                    reads=[self.PBANK[6]], writes=[RKVB])
            for f2 in range(2):
                for nb in range(4):
                    kb.op("t", lambda t, f2=f2, nb=nb: t.matmul(
                        pA[0:65, 512 * nb:512 * (nb + 1)], KVB[:, 65 * f2:65 * (f2 + 1)],
                        QP[f2][:, 512 * nb:512 * (nb + 1)],
                        start=(f2 == 0), stop=(f2 == 1)),
                        reads=[RKVB, RQP[f2]], writes=self.prA(512 * nb, 512 * nb + 512))
            nr = self.ROW[0:1, 0:2048]
            kb.op("v", lambda v: v.tensor_copy(nr, pA[64:65, 0:S]),
                  reads=self.prA(0, S), writes=[self.RROW])
            kb.op("s", lambda s: s.activation(nr, nr, AF.Ln, bias=self.c_bias[0:1, 2:3]),
                  reads=[self.RROW, self.Rcb], writes=[self.RROW])
            kb.op("s", lambda s: s.activation(nr, nr, AF.Exp, scale=-1.0),
                  reads=[self.RROW], writes=[self.RROW])
            for nb in range(4):
                kb.op("t", lambda t, nb=nb: t.matmul(
                    pB[0:64, 512 * nb:512 * (nb + 1)], self.c_e0[:, 0:64],
                    self.ROWT[:, 512 * nb:512 * (nb + 1)], start=True, stop=True),
                    reads=[self.RROW, self.Re0], writes=self.prB(512 * nb, 512 * nb + 512))
            kb.op("s", lambda s: s.copy(BCS, pB[0:64, 0:S]),
                  reads=self.prB(0, S), writes=[RBCS])
            kb.op("v", lambda v: v.tensor_tensor(TMP64, pA[0:64, 0:S], BCS, ALU.mult),
                  reads=self.prA(0, S) + [RBCS], writes=[RTMP64])
            kb.op("v", lambda v, ch=ch, base=base: v.tensor_copy(
                ATT[ch][base:base + 64, :], TMP64),
                reads=[RTMP64], writes=[RATT[ch]])

        for c in range(NCH):
            kb.sfree(f"l1Q{c}")
            kb.sfree(f"l1K{c}")
        for tt in range(16):
            kb.sfree(f"l1V{tt}")
            kb.sfree(f"l1KPT{tt}")
        for f in range(2):
            kb.sfree(f"l1QP{f}")
        kb.sfree("prj"); kb.sfree("l1KVB")

        wf = [kb.salloc(f"l1wf{i}", [128, E], F32) for i in range(2)]
        Rwf = [kb.R(f"l1wf{i}") for i in range(2)]
        wb = [kb.salloc(f"l1wb{c}", [128, E], BF16) for c in range(NCH)]
        Rwb = [kb.R(f"l1wb{c}") for c in range(NCH)]

        def loadW2(nm):
            for c in range(NCH):
                i = c % 2
                kb.dma(lambda q, c=c, nm=nm, wfx=wf[i], i=i: q.dma_start(
                    wfx[:], dr[nm][128 * c:128 * (c + 1), :]), f"dmw3{i}",
                    reads=[Rwf[i]], writes=[Rwf[i]])
                kb.op("g", lambda g_, wfx=wf[i], wbx=wb[c], i=i: g_.tensor_copy(
                    wbx[:], wfx[:]),
                    reads=[Rwf[i]], writes=[Rwb[c]])
        loadW2("l1a_Wo")
        bo, Rbo = self.bias["l1a_bo"]
        self.X = [kb.salloc(f"X{c}", [128, S], F32) for c in range(NCH)]
        self.RX = [kb.R(f"X{c}") for c in range(NCH)]
        for oc in range(8):
            pt = pA if oc % 2 == 0 else pB
            prf = self.prA if oc % 2 == 0 else self.prB
            kb.dma(lambda q, oc=oc, Xc=self.X[oc]: q.dma_start(Xc[:], self.park_main[oc]),
                   "dmpk", reads=[kb.R("d_pm")], writes=[self.RX[oc]], tag=f"unparkX2{oc}")
            for c in range(NCH):
                for nb in range(4):
                    kb.op("t", lambda t, oc=oc, c=c, nb=nb, wbx=wb[c], pt=pt, prf=prf:
                          t.matmul(
                        pt[:, 512 * nb:512 * (nb + 1)], wbx[:, 128 * oc:128 * (oc + 1)],
                        ATT[c][:, 512 * nb:512 * (nb + 1)],
                        start=(c == 0), stop=(c == NCH - 1)),
                        reads=[Rwb[c], RATT[c]], writes=prf(512 * nb, 512 * nb + 512))
            kb.op("v", lambda v, oc=oc, Xc=self.X[oc], pt=pt, prf=prf:
                  v.scalar_tensor_tensor(
                      Xc[:], pt[:, 0:S], bo[:, oc:oc + 1], Xc[:],
                      ALU.add, ALU.add),
                  reads=prf(0, S) + [Rbo, self.RX[oc]], writes=[self.RX[oc]])
        for i in range(2):
            kb.sfree(f"l1wf{i}")
        for c in range(NCH):
            kb.sfree(f"l1wb{c}")
        for c in range(NCH):
            kb.sfree(f"l1A{c}")

    def blk_l1_ffn(self):
        kb, nc, dr, pA, pB = self.kb, self.nc, self.dr, self.pA, self.pB
        LNF = [kb.salloc(f"LNF{c}", [128, S], F32) for c in range(NCH)]
        RLNF = [kb.R(f"LNF{c}") for c in range(NCH)]
        Xs_now = list(self.X)
        self.emit_ln(lambda c: Xs_now[c][:], self.RX, [None] * NCH, [None] * NCH,
                     outf_ap=lambda c: LNF[c][:], routf=RLNF, tagp="ln3")
        for c in range(NCH):
            kb.dma(lambda q, c=c, Xc=self.X[c]: q.dma_start(self.park_main[c], Xc[:]),
                   "dmpk", reads=[self.RX[c]], writes=[kb.R("d_pm")], tag=f"parkX3{c}")
        for c in range(NCH):
            kb.sfree(f"X{c}")

        CAND = kb.salloc("CAND", [128, S], F32)
        RCAND = kb.R("CAND")
        for c in range(NCH):
            kb.op("s", lambda s, c=c: s.activation(self.SC1[:], LNF[c][:], AF.Abs),
                  reads=[RLNF[c]], writes=[self.RSC1])
            if c == 0:
                kb.op("v", lambda v: v.tensor_copy(CAND[:], self.SC1[:]),
                      reads=[self.RSC1], writes=[RCAND])
            else:
                kb.op("v", lambda v: v.tensor_tensor(CAND[:], CAND[:], self.SC1[:], ALU.max),
                      reads=[RCAND, self.RSC1], writes=[RCAND])
        SCLBC, RSCLBC, INVBC, RINVBC = self._token_scale_rows(CAND, RCAND, 127.0, 1e-5)
        self.dbg("sclbc", SCLBC, RSCLBC)
        self.dbg("invbc", INVBC, RINVBC)
        XQ = [kb.salloc(f"XQ{c}", [128, S], BF16) for c in range(NCH)]
        RXQ = [kb.R(f"XQ{c}") for c in range(NCH)]
        X8 = [kb.salloc(f"X8_{i}", [128, S], I8) for i in range(2)]
        RX8 = [kb.R(f"X8_{i}") for i in range(2)]
        for c in range(NCH):
            j = c % 2
            scr = self.SC1 if j == 0 else self.SC4
            rscr = self.RSC1 if j == 0 else self.RSC4
            qeng = "v" if j == 0 else "g"
            kb.op(qeng, lambda e, c=c, scr=scr: e.tensor_tensor(
                scr[:], LNF[c][:], SCLBC[:], ALU.mult),
                reads=[RLNF[c], RSCLBC], writes=[rscr])
            kb.op("v", lambda v, scr=scr: v.tensor_scalar(
                scr[:], scr[:], MAGIC, -MAGIC, ALU.add, ALU.add),
                reads=[rscr], writes=[rscr])
            kb.op("v", lambda v, j=j, scr=scr: v.tensor_copy(X8[j][:], scr[:]),
                  reads=[rscr], writes=[RX8[j]])
            kb.op("v", lambda v, c=c, j=j: v.tensor_copy(XQ[c][:], X8[j][:]),
                  reads=[RX8[j]], writes=[RXQ[c]])
            self.dbg(f"xq{c}", XQ[c], RXQ[c])
        for c in range(NCH):
            kb.sfree(f"LNF{c}")
        kb.sfree("CAND")
        for i in range(2):
            kb.sfree(f"X8_{i}")
        kb.sfree("SCLBC_ts1")

        sw1col, Rsw1col, iw1col, Riw1col = self._weight_absmean("l1f_w1", 32, E)
        self.dbg("sw1", sw1col, Rsw1col, rows=128, cols=1)
        W1QT = [kb.salloc(f"W1QT{d}", [128, 4096], BF16) for d in range(NCH)]
        RW1QT = [kb.R(f"W1QT{d}") for d in range(NCH)]
        self._quant_transpose(dr["l1f_w1"], 32, E, sw1col, Rsw1col, W1QT, RW1QT, "qw1")
        self.dbg("w1qt0", W1QT[0], RW1QT[0], cols=4096)

        convw = kb.salloc("convw", [128, 3 * 32], F32)
        Rconvw = kb.R("convw")
        kb.dma(lambda q: q.dma_start(convw[:], dr["l1f_convw"]), "dmw4",
               writes=[Rconvw])
        convb, Rconvb = self.bias["l1f_convb"]
        H = kb.salloc("H", [128, 2050], BF16)
        RH = kb.R("H")
        kb.op("g", lambda g_: g_.memset(H[:, 0:1], 0.0), writes=[RH])
        kb.op("g", lambda g_: g_.memset(H[:, 2049:2050], 0.0), writes=[RH])
        SILF = kb.salloc("SILF", [128, S], F32)
        RSILF = kb.R("SILF")
        CAND2 = kb.salloc("CAND2", [128, S], F32)
        RCAND2 = kb.R("CAND2")
        for hc in range(32):
            pt = pA if hc % 2 == 0 else pB
            prf = self.prA if hc % 2 == 0 else self.prB
            for d in range(NCH):
                for nb in range(4):
                    kb.op("t", lambda t, hc=hc, d=d, nb=nb, pt=pt, prf=prf: t.matmul(
                        pt[:, 512 * nb:512 * (nb + 1)],
                        W1QT[d][:, 128 * hc:128 * (hc + 1)],
                        XQ[d][:, 512 * nb:512 * (nb + 1)],
                        start=(d == 0), stop=(d == NCH - 1)),
                        reads=[RW1QT[d], RXQ[d]], writes=prf(512 * nb, 512 * nb + 512))
            kb.op("v", lambda v, pt=pt, prf=prf: v.scalar_tensor_tensor(
                H[:, 1:1 + S], pt[:, 0:S], iw1col[:, 0:1], INVBC[:], ALU.mult, ALU.mult),
                reads=prf(0, S) + [Riw1col, RINVBC], writes=[RH])
            kb.op("g", lambda g_, hc=hc: g_.tensor_scalar(
                self.SCb[:], H[:, 0:S], convw[:, hc:hc + 1],
                convb[:, hc:hc + 1], ALU.mult, ALU.add),
                reads=[RH, Rconvw, Rconvb], writes=[self.RSCb])
            kb.op("v", lambda v, hc=hc: v.scalar_tensor_tensor(
                self.SCa[:], H[:, 1:1 + S], convw[:, 32 + hc:32 + hc + 1],
                self.SCb[:], ALU.mult, ALU.add),
                reads=[RH, Rconvw, self.RSCb], writes=[self.RSCa])
            kb.op("v", lambda v, hc=hc: v.scalar_tensor_tensor(
                self.SCa[:], H[:, 2:2 + S], convw[:, 64 + hc:64 + hc + 1],
                self.SCa[:], ALU.mult, ALU.add),
                reads=[RH, Rconvw, self.RSCa], writes=[self.RSCa])
            kb.op("s", lambda s: s.activation(SILF[:], self.SCa[:], AF.Silu),
                  reads=[self.RSCa], writes=[RSILF])
            if hc == 0:
                self.dbg("h0", H, RH, cols=2050)
                self.dbg("silf0", SILF, RSILF)
            kb.dma(lambda q, hc=hc: q.dma_start(self.park_silu[hc], SILF[:]),
                   "dmps", reads=[RSILF], writes=[kb.R("d_ps")], tag=f"parksil{hc}")
            kb.op("s", lambda s: s.activation(self.SC1[:], SILF[:], AF.Abs),
                  reads=[RSILF], writes=[self.RSC1])
            if hc == 0:
                kb.op("v", lambda v: v.tensor_copy(CAND2[:], self.SC1[:]),
                      reads=[self.RSC1], writes=[RCAND2])
            else:
                kb.op("v", lambda v: v.tensor_tensor(
                    CAND2[:], CAND2[:], self.SC1[:], ALU.max),
                    reads=[RCAND2, self.RSC1], writes=[RCAND2])
        for d in range(NCH):
            kb.sfree(f"W1QT{d}")
        for c in range(NCH):
            kb.sfree(f"XQ{c}")
        kb.sfree("H"); kb.sfree("SILF")
        kb.sfree("convw")

        kb.sfree("INVBC_ts1")
        SCL2BC, RSCL2BC, INV2BC, RINV2BC = self._token_scale_rows(CAND2, RCAND2, 127.0, 1e-5)
        self.dbg("scl2bc", SCL2BC, RSCL2BC)
        kb.sfree("CAND2")

        sw2col, Rsw2col, iw2col, Riw2col = self._weight_absmean("l1f_w2", 8, 4096)
        W2QT = [kb.salloc(f"W2QT{hc}", [128, E], BF16) for hc in range(32)]
        RW2QT = [kb.R(f"W2QT{hc}") for hc in range(32)]
        self._quant_transpose(dr["l1f_w2"], 8, 2048, sw2col, Rsw2col,
                              W2QT[0:16], RW2QT[0:16], "qw2a", kcol0=0)
        self._quant_transpose(dr["l1f_w2"], 8, 2048, sw2col, Rsw2col,
                              W2QT[16:32], RW2QT[16:32], "qw2b", kcol0=2048)

        SQB = [kb.salloc(f"SQB{i}", [128, 512], BF16) for i in range(2)]
        RSQB = [kb.R(f"SQB{i}") for i in range(2)]
        S8 = [kb.salloc(f"S8_{i}", [128, 512], I8) for i in range(2)]
        RS8 = [kb.R(f"S8_{i}") for i in range(2)]
        SIN = [kb.salloc(f"SIN{i}", [128, 512], F32) for i in range(2)]
        RSIN = [kb.R(f"SIN{i}") for i in range(2)]
        XOUT = [kb.salloc(f"XO{c}", [128, S], F32) for c in range(NCH)]
        RXOUT = [kb.R(f"XO{c}") for c in range(NCH)]
        for oc in range(8):
            kb.dma(lambda q, oc=oc: q.dma_start(XOUT[oc][:], self.park_main[oc]),
                   "dmpk", reads=[kb.R("d_pm")], writes=[RXOUT[oc]], tag=f"unparkX3{oc}")
        for qu in range(4):
            t0 = 512 * qu
            for hc in range(32):
                i = hc % 2
                kb.dma(lambda q, hc=hc, t0=t0, i=i: q.dma_start(
                    SIN[i][:], self.park_silu[hc][:, t0:t0 + 512]), f"dmsl{i}",
                    reads=[kb.R("d_ps")], writes=[RSIN[i]], tag=f"sil{qu}_{hc}")
                kb.op("v", lambda v, i=i, t0=t0: v.tensor_tensor(
                    SIN[i][:], SIN[i][:], SCL2BC[:, t0:t0 + 512], ALU.mult),
                    reads=[RSIN[i], RSCL2BC], writes=[RSIN[i]])
                kb.op("v", lambda v, i=i: v.tensor_scalar(
                    SIN[i][:], SIN[i][:], MAGIC, -MAGIC, ALU.add, ALU.add),
                    reads=[RSIN[i]], writes=[RSIN[i]])
                kb.op("v", lambda v, i=i: v.tensor_copy(S8[i][:], SIN[i][:]),
                      reads=[RSIN[i]], writes=[RS8[i]])
                kb.op("v", lambda v, i=i: v.tensor_copy(SQB[i][:], S8[i][:]),
                      reads=[RS8[i]], writes=[RSQB[i]])
                for oc in range(8):
                    pt = pA if oc < 4 else pB
                    prf = self.prA if oc < 4 else self.prB
                    col0 = 512 * (oc % 4)
                    kb.op("t", lambda t, hc=hc, oc=oc, pt=pt, col0=col0, i=i: t.matmul(
                        pt[:, col0:col0 + 512], W2QT[hc][:, 128 * oc:128 * (oc + 1)],
                        SQB[i][:], start=(hc == 0), stop=(hc == 31)),
                        reads=[RW2QT[hc], RSQB[i]], writes=prf(col0, col0 + 512))
            for oc in range(8):
                pt = pA if oc < 4 else pB
                prf = self.prA if oc < 4 else self.prB
                col0 = 512 * (oc % 4)
                kb.op("v", lambda v, oc=oc, t0=t0, pt=pt, col0=col0:
                      v.scalar_tensor_tensor(
                          self.SC1[:, 0:512], pt[:, col0:col0 + 512], iw2col[:, 0:1],
                          INV2BC[:, t0:t0 + 512], ALU.mult, ALU.mult),
                      reads=prf(col0, col0 + 512) + [Riw2col, RINV2BC],
                      writes=[self.RSC1])
                kb.op("v", lambda v, oc=oc, t0=t0: v.tensor_tensor(
                    XOUT[oc][:, t0:t0 + 512], XOUT[oc][:, t0:t0 + 512],
                    self.SC1[:, 0:512], ALU.add),
                    reads=[RXOUT[oc], self.RSC1], writes=[RXOUT[oc]])
        for hc in range(32):
            kb.sfree(f"W2QT{hc}")
        for i in range(2):
            kb.sfree(f"SQB{i}")
            kb.sfree(f"S8_{i}")
        for i in range(2):
            kb.sfree(f"SIN{i}")
        self.X = XOUT
        self.RX = RXOUT

    def _token_scale_rows(self, CAND, RCAND, num, clipmin):
        kb, pA, pB = self.kb, self.pA, self.pB
        TRE = self.kb.salloc("TRE_ts", [64, S], F32)
        RTRE = kb.R("TRE_ts")
        kb.op("v", lambda v: v.tensor_copy(TRE[0:64, :], CAND[64:128, :]),
              reads=[RCAND], writes=[RTRE])
        kb.op("v", lambda v: v.tensor_tensor(CAND[0:64, :], CAND[0:64, :],
                                             TRE[0:64, :], ALU.max),
              reads=[RCAND, RTRE], writes=[RCAND])
        kb.op("v", lambda v: v.tensor_copy(TRE[0:32, :], CAND[32:64, :]),
              reads=[RCAND], writes=[RTRE])
        kb.op("v", lambda v: v.tensor_tensor(CAND[0:32, :], CAND[0:32, :],
                                             TRE[0:32, :], ALU.max),
              reads=[RCAND, RTRE], writes=[RCAND])
        kb.sfree("TRE_ts")
        for half in range(2):
            for i in range(8):
                cidx = half * 8 + i
                kb.op("t", lambda t, cidx=cidx, i=i: t.transpose(
                    pB[:, 1536 + 32 * i:1536 + 32 * (i + 1)],
                    CAND[0:32, 128 * cidx:128 * (cidx + 1)], self.c_ident[0:32, 0:32]),
                    reads=[RCAND, self.Rid], writes=[self.PBANK[7]])
            for i in range(8):
                cidx = half * 8 + i
                kb.op("v", lambda v, cidx=cidx, i=i: v.tensor_reduce(
                    self.MCOL[:, cidx:cidx + 1], pB[:, 1536 + 32 * i:1536 + 32 * (i + 1)],
                    axis=AX.X, op=ALU.max),
                    reads=[self.PBANK[7]], writes=[self.RMCOL])
        SCLT = self.kb.salloc("SCLT_ts", [128, 32], F32)
        RSCLT = kb.R("SCLT_ts")
        kb.op("v", lambda v: v.tensor_scalar(
            SCLT[:, 0:16], self.MCOL[:, 0:16], clipmin, None, ALU.max),
            reads=[self.RMCOL], writes=[RSCLT])
        kb.op("v", lambda v: v.reciprocal(SCLT[:, 16:32], SCLT[:, 0:16]),
              reads=[RSCLT], writes=[RSCLT])
        kb.op("v", lambda v: v.tensor_scalar(
            SCLT[:, 16:32], SCLT[:, 16:32], num, None, ALU.mult),
            reads=[RSCLT], writes=[RSCLT])
        kb.op("v", lambda v: v.tensor_scalar(
            SCLT[:, 0:16], SCLT[:, 0:16], 1.0 / num, None, ALU.mult),
            reads=[RSCLT], writes=[RSCLT])
        for i in range(16):
            kb.op("t", lambda t, i=i: t.transpose(
                pA[0:1, 128 * i:128 * (i + 1)],
                SCLT[:, 16 + i:17 + i], self.c_ident[:]),
                reads=[RSCLT, self.Rid], writes=self.prA(128 * i, 128 * (i + 1)))
            kb.op("t", lambda t, i=i: t.transpose(
                pB[0:1, 128 * i:128 * (i + 1)],
                SCLT[:, i:i + 1], self.c_ident[:]),
                reads=[RSCLT, self.Rid], writes=self.prB(128 * i, 128 * (i + 1)))
        kb.op("v", lambda v: v.tensor_copy(self.ROW[0:1, 0:S], pA[0:1, 0:S]),
              reads=self.prA(0, S), writes=[self.RROW])
        self._ts_ctr = getattr(self, "_ts_ctr", 0) + 1
        SCLBC = self.kb.salloc(f"SCLBC_ts{self._ts_ctr}", [128, S], F32)
        RSCLBC = kb.R(f"SCLBC_ts{self._ts_ctr}")
        INVBC = self.kb.salloc(f"INVBC_ts{self._ts_ctr}", [128, S], F32)
        RINVBC = kb.R(f"INVBC_ts{self._ts_ctr}")
        for nb in range(4):
            kb.op("t", lambda t, nb=nb: t.matmul(
                self.pA[:, 512 * nb:512 * (nb + 1)], self.c_e0[:],
                self.ROWT[:, 512 * nb:512 * (nb + 1)], start=True, stop=True),
                reads=[self.RROW, self.Re0], writes=self.prA(512 * nb, 512 * nb + 512))
        kb.op("v", lambda v: v.tensor_copy(SCLBC[:], self.pA[:, 0:S]),
              reads=self.prA(0, S), writes=[RSCLBC])
        kb.op("s", lambda s: s.copy(self.ROW[0:1, 0:S], pB[0:1, 0:S]),
              reads=self.prB(0, S), writes=[self.RROW])
        kb.sfree("SCLT_ts")
        for nb in range(4):
            kb.op("t", lambda t, nb=nb: t.matmul(
                self.pB[:, 512 * nb:512 * (nb + 1)], self.c_e0[:],
                self.ROWT[:, 512 * nb:512 * (nb + 1)],
                start=True, stop=True),
                reads=[self.RROW, self.Re0], writes=self.prB(512 * nb, 512 * nb + 512))
        kb.op("v", lambda v: v.tensor_copy(INVBC[:], self.pB[:, 0:S]),
              reads=self.prB(0, S), writes=[RINVBC])
        return SCLBC, RSCLBC, INVBC, RINVBC

    def _weight_absmean(self, nm, nch, K):
        kb, dr = self.kb, self.dr
        stg = [kb.salloc(f"wam_stg{i}", [128, K], F32) for i in range(2)]
        Rstg = [kb.R(f"wam_stg{i}") for i in range(2)]
        acc = kb.salloc("wam_acc", [128, 3], F32)
        Racc = kb.R("wam_acc")
        for c in range(nch):
            i = c % 2
            kb.dma(lambda q, c=c, i=i: q.dma_start(
                stg[i][:], dr[nm][128 * c:128 * (c + 1), :]),
                f"dmwm{i}", reads=[Rstg[i]], writes=[Rstg[i]])
            kb.op("s", lambda s, c=c, i=i: s.activation(
                stg[i][:], stg[i][:], AF.Abs,
                accum_out=acc[:, 1 + i:2 + i]),
                reads=[Rstg[i]], writes=[Rstg[i], Racc])
            if c == 0:
                kb.op("v", lambda v, i=i: v.tensor_copy(acc[:, 0:1], acc[:, 1 + i:2 + i]),
                      reads=[Racc], writes=[Racc])
            else:
                kb.op("v", lambda v, i=i: v.tensor_tensor(
                    acc[:, 0:1], acc[:, 0:1], acc[:, 1 + i:2 + i], ALU.add),
                    reads=[Racc], writes=[Racc])
        for i in range(2):
            kb.sfree(f"wam_stg{i}")
        kb.op("t", lambda t: t.matmul(self.pB[0:1, 1536:1537], acc[:, 0:1],
                                      self.c_ones[:], start=True, stop=True),
              reads=[Racc, self.Ron], writes=[self.PBANK[7]])
        r = self.ROW[0:1, 2050:2051]
        ri = self.ROW[0:1, 2051:2052]
        kb.op("v", lambda v, nch=nch, K=K: v.tensor_scalar(
            r, self.pB[0:1, 1536:1537], 1.0 / (nch * 128 * K), None, ALU.mult),
            reads=[self.PBANK[7]], writes=[self.RROW])
        kb.op("v", lambda v: v.tensor_scalar(r, r, 1e-5, None, ALU.max),
              reads=[self.RROW], writes=[self.RROW])
        kb.op("v", lambda v: v.reciprocal(ri, r), reads=[self.RROW], writes=[self.RROW])
        scol = kb.salloc("wam_scol" + nm, [128, 1], F32)
        Rscol = kb.R("wam_scol" + nm)
        icol = kb.salloc("wam_icol" + nm, [128, 1], F32)
        Ricol = kb.R("wam_icol" + nm)
        kb.op("t", lambda t: t.matmul(self.pB[:, 1536:1537], self.c_e0[:],
                                      self.ROWT[:, 2051:2052],
                                      start=True, stop=True),
              reads=[self.RROW, self.Re0], writes=[self.PBANK[7]])
        kb.op("v", lambda v: v.tensor_copy(scol[:], self.pB[:, 1536:1537]),
              reads=[self.PBANK[7]], writes=[Rscol])
        kb.op("t", lambda t: t.matmul(self.pB[:, 1537:1538], self.c_e0[:],
                                      self.ROWT[:, 2050:2051],
                                      start=True, stop=True),
              reads=[self.RROW, self.Re0], writes=[self.PBANK[7]])
        kb.op("v", lambda v: v.tensor_copy(icol[:], self.pB[:, 1537:1538]),
              reads=[self.PBANK[7]], writes=[Ricol])
        kb.sfree("wam_acc")
        return scol, Rscol, icol, Ricol

    def _quant_transpose(self, dram, nch, K, scol, Rscol, OUT, ROUT, tag, kcol0=0):
        kb = self.kb
        stg = [kb.salloc(f"{tag}_stg{i}", [128, K], F32) for i in range(2)]
        Rstg = [kb.R(f"{tag}_stg{i}") for i in range(2)]
        q8 = [kb.salloc(f"{tag}_q8{i}", [128, K], I8) for i in range(2)]
        Rq8 = [kb.R(f"{tag}_q8{i}") for i in range(2)]
        qf = [kb.salloc(f"{tag}_qf{i}", [128, K], F32) for i in range(2)]
        Rqf = [kb.R(f"{tag}_qf{i}") for i in range(2)]
        nd = K // 128
        for c in range(nch):
            i = c % 2
            kb.dma(lambda q, c=c, i=i, kcol0=kcol0, K=K: q.dma_start(
                stg[i][:], dram[128 * c:128 * (c + 1), kcol0:kcol0 + K]),
                f"dmwq{i}", reads=[Rstg[i]], writes=[Rstg[i]])
            kb.op("g", lambda g_, i=i: g_.tensor_scalar(
                qf[i][:], stg[i][:], scol[:, 0:1], 0.0, ALU.mult, ALU.add),
                reads=[Rstg[i], Rscol], writes=[Rqf[i]])
            kb.op("v", lambda v, i=i: v.tensor_scalar(
                qf[i][:], qf[i][:], MAGIC, -MAGIC, ALU.add, ALU.add),
                reads=[Rqf[i]], writes=[Rqf[i]])
            kb.op("v", lambda v, i=i: v.tensor_copy(q8[i][:], qf[i][:]),
                  reads=[Rqf[i]], writes=[Rq8[i]])
            kb.op("v", lambda v, i=i: v.tensor_scalar(
                qf[i][:], q8[i][:], -1.0, 1.0, ALU.max, ALU.min),
                reads=[Rq8[i]], writes=[Rqf[i]])
            for d in range(nd):
                col0 = 512 * (d % 3)
                kb.op("t", lambda t, d=d, col0=col0, i=i: t.transpose(
                    self.pB[:, col0:col0 + 128], qf[i][:, 128 * d:128 * (d + 1)],
                    self.c_ident[:]),
                    reads=[Rqf[i], self.Rid], writes=self.prB(col0, col0 + 128))
                if d % 2 == 0:
                    kb.op("v", lambda v, d=d, c=c, col0=col0: v.tensor_copy(
                        OUT[d][:, 128 * c:128 * (c + 1)], self.pB[:, col0:col0 + 128]),
                        reads=self.prB(col0, col0 + 128), writes=[ROUT[d]])
                else:
                    kb.op("s", lambda s, d=d, c=c, col0=col0: s.copy(
                        OUT[d][:, 128 * c:128 * (c + 1)], self.pB[:, col0:col0 + 128]),
                        reads=self.prB(col0, col0 + 128), writes=[ROUT[d]])
        for i in range(2):
            kb.sfree(f"{tag}_stg{i}")
            kb.sfree(f"{tag}_q8{i}")
            kb.sfree(f"{tag}_qf{i}")

    def p_out(self):
        kb, pA = self.kb, self.pA
        OT = [kb.salloc(f"OT{i}", [128, E], F32) for i in range(2)]
        ROT = [kb.R(f"OT{i}") for i in range(2)]
        for tt in range(16):
            i = tt % 2
            for c in range(NCH):
                col0 = 512 * (c % 4)
                kb.op("t", lambda t, tt=tt, col0=col0, Xc=self.X[c]: t.transpose(
                    pA[:, col0:col0 + 128], Xc[:, 128 * tt:128 * (tt + 1)],
                    self.c_ident[:]),
                    reads=[self.RX[c], self.Rid], writes=self.prA(col0, col0 + 128))
                kb.op("v", lambda v, i=i, c=c, col0=col0: v.tensor_copy(
                    OT[i][:, 128 * c:128 * (c + 1)], pA[:, col0:col0 + 128]),
                    reads=self.prA(col0, col0 + 128), writes=[ROT[i]])
            kb.dma(lambda q, tt=tt, i=i: q.dma_start(
                self.y_out[128 * tt:128 * (tt + 1), :], OT[i][:]), "dmout",
                reads=[ROT[i]], tag=f"out{tt}")


class _SliceTensor:
    def __init__(self, t, a, b):
        self.t = t
        self.a = a
        self.b = b

    def __getitem__(self, idx):
        return self.t[:, self.a:self.b]


_PROG_CACHE = {}


def get_prog(n_blocks=4):
    if n_blocks not in _PROG_CACHE:
        _PROG_CACHE[n_blocks] = Prog(n_blocks)
    return _PROG_CACHE[n_blocks]


def kernel(**inputs):
    n_blocks = int(inputs.pop("_n_blocks", 4))
    prog = get_prog(n_blocks)
    tabs = make_tables()
    w = prep_weights(inputs)
    x = np.ascontiguousarray(np.asarray(inputs["L0_attn_x"], dtype=np.float32))
    in_maps = []
    for core in range(8):
        m = {}
        for name, shape, dt in DRAM_SPECS:
            if name == "x_in":
                m[name] = x[core]
            elif name in tabs:
                m[name] = tabs[name]
            else:
                m[name] = w[name]
        in_maps.append(m)
    res = run_bass_kernel_spmd(prog.nc, in_maps, core_ids=list(range(8)))
    out = np.stack([res.results[i]["y_out"] for i in range(8)], axis=0)
    return out
